# version 18
# baseline (speedup 1.0000x reference)
import numpy as np
import ml_dtypes
import concourse.bass as bass
import concourse.mybir as mybir
from concourse.bass_utils import run_bass_kernel_spmd

F32 = mybir.dt.float32
BF16 = mybir.dt.bfloat16
AF = mybir.ActivationFunctionType
ALU = mybir.AluOpType
AX = mybir.AxisListType
AP = bass.AP

B = 8
T = 4096
D = 1024
NCH = 8
NT = 8
NKT = 32
WB = 8064
C0 = 4218
EPS = 1e-5
RD = 1.0 / D

ENGS = ("sync", "tensor", "vector", "scalar", "gpsimd")
UPTO = "full"
CHAIN_SAME_ENGINE = {"DVE": True, "ACT": False, "POOL": False}


def band_class(kt, qc):
    kmin = kt * 128 - (qc * 512 + 511)
    kmax = kt * 128 + 127 - qc * 512
    if kmax <= -250:
        return 1
    if kmin >= 249:
        return 2
    return 0


class Sem:
    __slots__ = ("h", "n", "name")

    def __init__(self, h, name):
        self.h = h
        self.n = 0
        self.name = name


class Builder:
    def __init__(self, upto=UPTO):
        self.upto = upto
        nc = self.nc = bass.Bass("TRN2", target_bir_lowering=False)

        dt_ = nc.dram_tensor
        self.d_xT = dt_("xT", [B, NCH, 128, T], F32, kind="ExternalInput").ap()
        self.d_xTk = dt_("xTk", [NCH, 128, T], F32, kind="ExternalInput").ap()
        self.d_wq = dt_("wq", [NCH, 128, 128], F32, kind="ExternalInput").ap()
        self.d_wk = dt_("wk", [NCH, 128, 128], F32, kind="ExternalInput").ap()
        self.d_wv = dt_("wv", [NCH, 128, 128], F32, kind="ExternalInput").ap()
        self.d_bqkv = dt_("bqkv", [1, 384], F32, kind="ExternalInput").ap()
        self.d_woT = dt_("woT", [NCH, 128, D], F32, kind="ExternalInput").ap()
        self.d_bo = dt_("bo", [NCH, 128], F32, kind="ExternalInput").ap()
        self.d_tb2 = dt_("tb2", [128, WB], F32, kind="ExternalInput").ap()
        self.d_etis = dt_("etis", [1, 2], F32, kind="ExternalInput").ap()
        self.d_eye = dt_("eye", [128, 128], F32, kind="ExternalInput").ap()
        self.d_fwin = dt_("fwin", [NCH, 128, 64], F32, kind="ExternalInput").ap()
        self.d_fbin = dt_("fbin", [1, 64], F32, kind="ExternalInput").ap()
        self.d_fw1 = dt_("fw1", [64, 16], F32, kind="ExternalInput").ap()
        self.d_fb1 = dt_("fb1", [64, 16], F32, kind="ExternalInput").ap()
        self.d_fw2 = dt_("fw2", [64, 16], F32, kind="ExternalInput").ap()
        self.d_fb2 = dt_("fb2", [1, 64], F32, kind="ExternalInput").ap()
        self.d_fwout = dt_("fwout", [64, D], F32, kind="ExternalInput").ap()
        self.d_fbout = dt_("fbout", [NCH, 128], F32, kind="ExternalInput").ap()
        self.d_crmsw = dt_("crmsw", [NCH, 128], F32, kind="ExternalInput").ap()
        self.d_cwA = dt_("cwA", [NCH, 128, D], F32, kind="ExternalInput").ap()
        self.d_cbA = dt_("cbA", [NCH, 128], F32, kind="ExternalInput").ap()
        self.d_cwx = dt_("cwx", [NCH, 128, D], F32, kind="ExternalInput").ap()
        self.d_cwsg = dt_("cwsg", [NCH, 128, D], F32, kind="ExternalInput").ap()
        self.d_cwog = dt_("cwog", [NCH, 128, D], F32, kind="ExternalInput").ap()
        self.d_cwo = dt_("cwo", [NCH, 128, D], F32, kind="ExternalInput").ap()
        self.d_dwrT = dt_("dwrT", [NCH, 128, D], F32, kind="ExternalInput").ap()
        self.d_dwkT = dt_("dwkT", [NCH, 128, D], F32, kind="ExternalInput").ap()
        self.d_dwvT = dt_("dwvT", [NCH, 128, D], F32, kind="ExternalInput").ap()
        self.d_mur = dt_("mur", [NCH, 128], F32, kind="ExternalInput").ap()
        self.d_muk = dt_("muk", [NCH, 128], F32, kind="ExternalInput").ap()
        self.d_out = dt_("out", [T, D], F32, kind="ExternalOutput").ap()

        sb = nc.alloc_sbuf_tensor

        def gsb(name, shape, dt):
            guard = nc.sbuf_tensor(name, shape, dt, side="right")
            h = guard.__enter__()
            self._guards[name] = guard
            return h

        self._guards = {}
        self.xsq = sb("xsq", [128, NCH * 512], BF16)
        self.ident_f = sb("ident_f", [128, 128], F32)
        self.ident_bf = sb("ident_bf", [128, 128], BF16)
        self.ones128 = sb("ones128", [128, 128], BF16)
        self.ones_row = sb("ones_row", [1, 512], BF16)
        self.onescol_bf = sb("onescol_bf", [128, 1], BF16)
        self.ones1_f = sb("ones1_f", [1, 128], F32)
        self.onescol_f = sb("onescol_f", [128, 1], F32)
        self.bias_row = sb("bias_row", [1, 384], BF16)
        self.bo_col = sb("bo_col", [128, NCH], F32)
        self.wsum = sb("wsum", [128, 4], F32)
        self.tiny = sb("tiny", [128, 16], F32)
        self.ubc = sb("ubc", [128, 8], F32)
        self.rmB1 = sb("rmB1", [128, 1024], BF16)
        self.rmB2 = sb("rmB2", [128, 1024], BF16)
        self.tscr = sb("tscr", [128, 1024], F32)
        self.tscr2 = sb("tscr2", [128, 1024], F32)
        self.reciprow = sb("reciprow", [1, 512], BF16)
        self.lnrow = sb("lnrow", [1, 512], F32)
        self.recb = sb("recb", [128, 512], BF16)
        self.bscr = sb("bscr", [128, 1024], BF16)
        self.colmax = sb("colmax", [128, 8], F32)
        self.svec = sb("svec", [128, 16], F32)
        self.eps_col = sb("eps_col", [128, 1], F32)
        self.scr = gsb("scr", [128, NCH * T], BF16)
        self.qk = gsb("qk", [128, 4 * T], BF16)
        self.x_nt = gsb("x_nt", [128, 2 * NCH * 512], BF16)
        self.vt = gsb("vt", [128, 2 * 32 * 132], BF16)
        self.band = gsb("band", [128, WB], BF16)
        self.pbuf = gsb("pbuf", [128, 2 * 1024], BF16)
        self.ctxAB = gsb("ctxAB", [128, T], BF16)
        self.wsb = gsb("wsb", [128, 3 * NCH * 128], BF16)
        for n in ("wsb", "ctxAB", "pbuf", "band", "vt", "x_nt"):
            self._guards.pop(n).__exit__(None, None, None)
        self.stream = [sb(f"stream{c}", [128, T], BF16) for c in range(NCH)]
        self.wslot = [sb("wslot0", [128, NCH * D], BF16)]
        for n in ("qk", "scr"):
            self._guards.pop(n).__exit__(None, None, None)
        self.wslot.append(sb("wslot1", [128, NCH * D], BF16))
        self.ck = {n: sb("ck_" + n, [128, NCH * 512], BF16)
                   for n in ("xm", "xn", "aa", "bb", "gg")}
        self.ck_xms = sb("ck_xms", [128, NCH * 513], BF16)
        self.ainv_t = sb("ainv_t", [128, 1024], BF16)
        self.outstage = [sb("outs0", [128, 1024], F32), sb("outs1", [128, 1024], F32)]
        self.fmid = sb("fmid", [64, 512], F32)
        self.fmid_bf = sb("fmid_bf", [64, 512 * 4], BF16)
        self.fw12 = sb("fw12", [64, 48], F32)
        self.carry = sb("carry", [128, 24], F32)
        self.cv = sb("cv", [128, 16], F32)

        ps = nc.alloc_psum_tensor
        self.ps_s0 = ps("ps_s0", [128, 1024], F32)
        self.ps_s1 = ps("ps_s1", [128, 1024], F32)
        self.ps_ctxA = ps("ps_ctxA", [65, 512], F32)
        self.ps_ctxB = ps("ps_ctxB", [65, 512], F32)
        _g0 = nc.psum_tensor("ps_w0", [128, 512], F32)
        self.ps_w0 = _g0.__enter__()
        _g1 = nc.psum_tensor("ps_w1", [128, 512], F32)
        self.ps_w1 = _g1.__enter__()
        _g1.__exit__(None, None, None)
        _g0.__exit__(None, None, None)
        self.ps_wpair = ps("ps_wpair", [128, 1024], F32)

        self.sems = {}
        for name in ("dS", "dG", "dW1", "dT", "dX0", "dX1", "PE", "ACT", "DVE", "POOL", "ST"):
            self.sems[name] = Sem(nc.alloc_semaphore(name), name)
        self._pass_counts = []

    def on(self, eng):
        return self.engn == eng

    def pe(self, fn, waits=()):
        s = self.sems["PE"]
        if self.on("tensor"):
            for sem, val in waits:
                if val > 0:
                    self.e.wait_ge(sem.h, val)
            fn(self.e).then_inc(s.h, 1)
        s.n += 1
        return s.n

    def _chained(self, engkey, engname, fn, waits):
        s = self.sems[engkey]
        if self.on(engname):
            for sem, val in waits:
                if val > 0:
                    self.e.wait_ge(sem.h, val)
            if CHAIN_SAME_ENGINE.get(engkey, True) and s.n > 0:
                self.e.wait_ge(s.h, s.n)
            fn(self.e).then_inc(s.h, 1)
        s.n += 1
        return s.n

    def dve(self, fn, waits=()):
        return self._chained("DVE", "vector", fn, waits)

    def act(self, fn, waits=()):
        return self._chained("ACT", "scalar", fn, waits)

    def pool(self, fn, waits=()):
        return self._chained("POOL", "gpsimd", fn, waits)

    def dma(self, eng, semkey, fn, waits=()):
        s = self.sems[semkey]
        if self.on(eng):
            for sem, val in waits:
                if val > 0:
                    self.e.wait_ge(sem.h, val)
            fn(self.e).then_inc(s.h, 16)
        s.n += 16
        return s.n

    def wload3(self, eng, semkey, dst, dst_off, src, wd, nch=NCH, waits=()):
        cols = dst.shape[1]
        return self.dma(eng, semkey, lambda e: e.dma_start(
            out=AP(dst, dst_off, [[cols, 128], [wd, nch], [1, wd]]),
            in_=src.rearrange("c p m -> p c m")), waits=waits)

    def phase_barrier(self):
        for s in self.sems.values():
            if s.n > 0:
                self.e.wait_ge(s.h, s.n)

    def emit(self, engn, eng):
        self.engn = engn
        self.e = eng
        for x in self.sems.values():
            x.n = 0
        S = self.sems
        self.dS, self.dG = S["dS"], S["dG"]
        self.dX = [S["dX0"], S["dX1"]]
        self.sPE, self.sACT, self.sDVE, self.sPOOL = S["PE"], S["ACT"], S["DVE"], S["POOL"]
        self.sST = S["ST"]
        z = (self.sPE, 0)
        self.rel_s0 = z
        self.rel_s1 = z
        self.rel_w0 = z
        self.rel_w1 = z
        self.rel_ctx = z
        self.xbuf_rel = [z, z]
        self.xsq_rel = z
        self.B_rel = [z, z]
        self.ctxAB_rel = z
        self.qkv_ready = z
        self.stream_rel = [z] * NCH
        self.wslot_rel = [z, z]

        self.ubc_act_rel = 0
        self.qkv_done = {}

        self.preamble()
        self.phaseA()
        self.phase_barrier()
        self.phase_wo_close()
        self.phase_barrier()
        if self.upto in ("l0f", "l1a", "full"):
            self.phaseF()
            self.phase_barrier()
        if self.upto in ("l1a", "full"):
            self.phaseC()
            self.phase_barrier()
        if self.upto == "full":
            self.phaseD()
            self.phase_barrier()
        self.phase_out()
        self._pass_counts.append({k: v.n for k, v in S.items()})

    def preamble(self):
        dS, dG, sPE, sACT, sDVE = self.dS, self.dG, self.sPE, self.sACT, self.sDVE

        self.dma("sync", "dS", lambda e: e.dma_start(out=self.ident_f[:], in_=self.d_eye))
        self.dma("sync", "dS", lambda e: e.dma_start(out=self.tiny[0:1, 0:2], in_=self.d_etis))
        d_sync0 = self.dma("sync", "dS", lambda e: e.dma_start(
            out=self.bo_col[:], in_=self.d_bo.rearrange("c p -> p c"),
            allow_slow_non_contiguous=True))
        self.dma("gpsimd", "dG", lambda e: e.dma_start(out=self.band[:], in_=self.d_tb2))
        for j, w in enumerate((self.d_wq, self.d_wk, self.d_wv)):
            self.wload3("gpsimd", "dG", self.wsb, j * 1024, w, 128)
        d_g0 = self.dma("gpsimd", "dG", lambda e: e.dma_start(out=self.bias_row[:], in_=self.d_bqkv))

        self.dve(lambda e: e.memset(self.ones128[:], 1.0))
        self.dve(lambda e: e.memset(self.ones_row[:], 1.0))
        self.dve(lambda e: e.memset(self.eps_col[:], EPS))
        self.dve(lambda e: e.memset(self.onescol_bf[:], 1.0))
        self.dve(lambda e: e.memset(self.ones1_f[:], 1.0))
        self.dve(lambda e: e.memset(self.onescol_f[:], 1.0))
        self.dve(lambda e: e.memset(AP(self.vt, 64, [[2 * 32 * 132, 128], [132, 64]]), 1.0))
        self.dve(lambda e: e.memset(AP(self.vt, 130, [[2 * 32 * 132, 128], [132, 64]]), 1.0))
        d_idbf = self.dve(lambda e: e.tensor_copy(self.ident_bf[:], self.ident_f[:]),
                          waits=[(dS, d_sync0)])
        a_sig = self.act(lambda e: e.activation(self.tiny[0:1, 3:4], self.tiny[0:1, 0:1], AF.Sigmoid),
                         waits=[(dS, d_sync0)])
        p = 0
        for o in range(3):
            for c in range(NCH):
                p = self.pe(lambda e, o=o, c=c: e.matmul(
                    self.ps_w0[:, o:o + 1],
                    self.wsb[:, o * 1024 + c * 128: o * 1024 + (c + 1) * 128],
                    self.onescol_bf[:, 0:1], start=(c == 0), stop=(c == NCH - 1)),
                    waits=([(dG, d_g0), (sDVE, d_idbf)] if (o == 0 and c == 0) else ()))
        a_ws = self.act(lambda e: e.copy(self.wsum[:, 0:3], self.ps_w0[:, 0:3]), waits=[(sPE, p)])
        self.dve(lambda e: e.tensor_scalar_mul(self.tiny[0:1, 4:5], self.tiny[0:1, 3:4], 8.0),
                 waits=[(sACT, a_sig)])
        d2 = self.dve(lambda e: e.tensor_scalar_mul(self.tiny[0:1, 5:6], self.tiny[0:1, 1:2], 8.0))
        p = self.pe(lambda e: e.matmul(self.ps_w1[:, 0:1], self.ones1_f[0:1, :], self.tiny[0:1, 4:5],
                                       start=True, stop=True), waits=[(sDVE, d2)])
        p = self.pe(lambda e: e.matmul(self.ps_w1[:, 1:2], self.ones1_f[0:1, :], self.tiny[0:1, 5:6],
                                       start=True, stop=True))
        a_sc = self.act(lambda e: e.copy(self.tiny[:, 6:8], self.ps_w1[:, 0:2]), waits=[(sPE, p)])
        d_band = self.dve(lambda e: e.tensor_scalar(self.band[:], self.band[:], self.tiny[:, 6:7],
                                                    self.tiny[:, 7:8], ALU.mult, ALU.add),
                          waits=[(sACT, a_sc), (dG, d_g0)])
        self.dve(lambda e: e.reduce_max(self.colmax[:, 2:3], self.band[:], axis=AX.X))
        d3 = self.dve(lambda e: e.tensor_copy(self.tscr[:, 0:1], self.colmax[:, 2:3]))
        p2 = self.pe(lambda e: e.transpose(self.ps_w1[0:1, 0:128], self.tscr[:, 0:1], self.ident_f[:]),
                     waits=[(sDVE, d3), (sACT, a_sc)])
        self.dve(lambda e: e.reduce_max(self.tiny[0:1, 8:9], self.ps_w1[0:1, 0:128], axis=AX.X),
                 waits=[(sPE, p2)])
        self.dve(lambda e: e.tensor_scalar_mul(self.tiny[0:1, 9:10], self.tiny[0:1, 8:9], 0.125))
        self.dve(lambda e: e.tensor_scalar_mul(self.tiny[0:1, 10:11], self.band[0:1, WB - 1:WB], 0.125))
        d_pre = self.dve(lambda e: e.tensor_scalar_mul(self.tiny[0:1, 11:12], self.band[0:1, 0:1], 0.125))
        self.band_ready = (sDVE, d_band)
        self.pre_done = (sDVE, d_pre)
        self.rel_w0 = (sACT, a_ws)
        self.rel_w1 = (sDVE, d_pre)

    def stats512(self, src, ready, rms=False, bsel=0, wp=False, wide_src=None):
        sPE, sACT, sDVE, sPOOL = self.sPE, self.sACT, self.sDVE, self.sPOOL
        h0 = bsel * 512
        B1 = self.rmB1[:, h0:h0 + 512]
        B2 = self.rmB2[:, h0:h0 + 512]
        if wp:
            pA, pB = self.ps_w0[:, 0:512], self.ps_w1[:, 0:512]
            relw = [self.rel_w0, self.rel_w1]
        else:
            pA, pB = self.ps_s0[:, 0:512], self.ps_s0[:, 512:1024]
            relw = [self.rel_s0]
        pl = 0
        if wide_src is not None:
            pl = self.pool(lambda e: e.tensor_mul(self.xsq[:, 0:4096], wide_src(), wide_src()),
                           waits=list(ready) + [self.xsq_rel])
        else:
            for c in range(NCH):
                pl = self.pool(lambda e, c=c: e.tensor_mul(
                    self.xsq[:, c * 512:(c + 1) * 512], src(c), src(c)),
                    waits=(list(ready) + [self.xsq_rel]) if c == 0 else ())
        pe_c = 0
        for c in range(NCH):
            pe_c = self.pe(lambda e, c=c, pA=pA: e.matmul(pA, self.ones128[:], src(c),
                                                          start=(c == 0), stop=(c == NCH - 1)),
                           waits=(list(ready) + relw) if c == 0 else ())
        for c in range(NCH):
            pe_c = self.pe(lambda e, c=c, pB=pB: e.matmul(pB, self.ones128[:],
                                                          self.xsq[:, c * 512:(c + 1) * 512],
                                                          start=(c == 0), stop=(c == NCH - 1)),
                           waits=[(sPOOL, pl)] if c == 0 else ())
        self.xsq_rel = (sPE, pe_c)
        pool_snap = self.sPOOL.n
        mS = self.tscr[:, h0:h0 + 512]
        d = self.dve(lambda e, pA=pA: e.tensor_scalar_mul(mS, pA, RD),
                     waits=[(sPE, pe_c), self.B_rel[bsel], (self.sPOOL, pool_snap)])
        d = self.dve(lambda e: e.tensor_mul(B2, mS, mS))
        vdst = self.tscr2[:, 512:1024] if rms else B2
        d_v = self.dve(lambda e, vdst=vdst, pB=pB: e.scalar_tensor_tensor(
            vdst, pB, RD, B2, ALU.mult, ALU.subtract))
        if wp:
            self.rel_w0 = (sDVE, d_v)
            self.rel_w1 = (sDVE, d_v)
        else:
            self.rel_s0 = (sDVE, d_v)
        a = self.act(lambda e, vdst=vdst: e.activation(B1, vdst, AF.Ln,
                                                       bias=self.eps_col[:, 0:1]),
                     waits=[(sDVE, d_v)])
        a = self.act(lambda e: e.activation(B2, B1, AF.Exp, scale=-0.5))
        dB = self.dve(lambda e: e.scalar_tensor_tensor(B1, mS, -1.0, B2, ALU.mult, ALU.mult),
                      waits=[(sACT, a)])
        if rms:
            d = self.dve(lambda e: e.tensor_mul(self.tscr[:, 512:1024], B2, B2))
            d = self.dve(lambda e: e.tensor_mul(self.tscr[:, 512:1024], self.tscr[:, 512:1024],
                                                vdst))
            a = self.act(lambda e: e.activation(self.tscr2[:, 512:1024], self.tscr[:, 512:1024],
                                                AF.Ln, bias=self.eps_col[:, 0:1]), waits=[(sDVE, d)])
            a = self.act(lambda e: e.activation(self.ainv_t[:, h0:h0 + 512],
                                                self.tscr2[:, 512:1024], AF.Exp, scale=-0.5))
            dB = self.dve(lambda e: e.engine_nop(), waits=[(sACT, a)])
        return dB

    def qkp(self, b):
        return (b % 2) * 2 * T

    def vtp(self, b):
        return (b % 2) * 32 * 132

    def _qkv_xv(self, c, nt):
        xoff = (nt % 2) * 4096
        return self.x_nt[:, xoff + c * 512: xoff + (c + 1) * 512]

    def _qkv_dma(self, b, nt):
        buf = nt % 2
        xoff = buf * 4096
        self.dgA[nt] = self.dma("gpsimd", "dX0" if buf == 0 else "dX1",
                                lambda e, b=b, nt=nt, xoff=xoff: e.dma_start(
            out=AP(self.x_nt, xoff, [[2 * NCH * 512, 128], [512, NCH], [1, 512]]),
            in_=self.d_xT[b, :, :, nt * 512:nt * 512 + 512].rearrange("c p t -> p c t")),
            waits=[self.xbuf_rel[buf]])

    def _qkv_stats_a(self, b, nt):
        sPE, sPOOL = self.sPE, self.sPOOL
        buf = nt % 2
        xoff = buf * 4096
        ready = [(self.dX[buf], self.dgA[nt])]
        pl = self.pool(lambda e, xoff=xoff: e.tensor_mul(
            self.xsq[:, 0:4096], self.x_nt[:, xoff:xoff + 4096],
            self.x_nt[:, xoff:xoff + 4096]), waits=ready + [self.xsq_rel])
        pe_c = 0
        for c in range(NCH):
            pe_c = self.pe(lambda e, c=c: e.matmul(
                self.ps_w0[:, 0:512], self.ones128[:], self._qkv_xv(c, nt),
                start=(c == 0), stop=(c == NCH - 1)),
                waits=(ready + [self.rel_w0, self.rel_w1]) if c == 0 else ())
        self._st_pl = pl
        self._st_pe = pe_c

    def _qkv_stats_b(self, b, nt):
        sPE, sACT, sDVE, sPOOL = self.sPE, self.sACT, self.sDVE, self.sPOOL
        buf = nt % 2
        h0 = buf * 512
        B1 = self.rmB1[:, h0:h0 + 512]
        B2 = self.rmB2[:, h0:h0 + 512]
        pe_c = 0
        for c in range(NCH):
            pe_c = self.pe(lambda e, c=c: e.matmul(
                self.ps_w1[:, 0:512], self.ones128[:], self.xsq[:, c * 512:(c + 1) * 512],
                start=(c == 0), stop=(c == NCH - 1)),
                waits=[(sPOOL, self._st_pl)] if c == 0 else ())
        self.xsq_rel = (sPE, pe_c)
        mS = self.tscr[:, h0:h0 + 512]
        d = self.dve(lambda e: e.tensor_scalar_mul(mS, self.ps_w0[:, 0:512], RD),
                     waits=[(sPE, pe_c), self.B_rel[buf]])
        d = self.dve(lambda e: e.tensor_mul(B2, mS, mS))
        d_v = self.dve(lambda e: e.scalar_tensor_tensor(
            B2, self.ps_w1[:, 0:512], RD, B2, ALU.mult, ALU.subtract))
        self.rel_w0 = (sDVE, d_v)
        self.rel_w1 = (sDVE, d_v)
        a = self.act(lambda e: e.activation(B1, B2, AF.Ln, bias=self.eps_col[:, 0:1]),
                     waits=[(sDVE, d_v)])
        a = self.act(lambda e: e.activation(B2, B1, AF.Exp, scale=-0.5))
        self.dBA[nt] = self.dve(lambda e: e.scalar_tensor_tensor(
            B1, mS, -1.0, B2, ALU.mult, ALU.mult), waits=[(sACT, a)])

    def _qkv_proj_a(self, b, nt, o):
        sPE, sDVE = self.sPE, self.sDVE
        psum, rel = ((self.ps_w0, self.rel_w0) if o != 1 else (self.ps_w1, self.rel_w1))
        pq = 0
        for c in range(4):
            pq = self.pe(lambda e, o=o, c=c, nt=nt, psum=psum: e.matmul(
                psum[:, 0:512],
                self.wsb[:, o * 1024 + c * 128: o * 1024 + (c + 1) * 128],
                self._qkv_xv(c, nt), start=(c == 0), stop=False),
                waits=[rel] if c == 0 else ())

    def _qkv_proj_b(self, b, nt, o):
        sPE, sDVE = self.sPE, self.sDVE
        buf = nt % 2
        h0 = buf * 512
        q0 = nt * 512
        qo = self.qkp(b)
        dests = (self.qk[:, qo + q0:qo + q0 + 512],
                 self.qk[:, qo + T + q0:qo + T + q0 + 512], None)
        psum = self.ps_w0 if o != 1 else self.ps_w1
        pq = 0
        for c in range(4, NCH):
            pq = self.pe(lambda e, o=o, c=c, nt=nt, psum=psum: e.matmul(
                psum[:, 0:512],
                self.wsb[:, o * 1024 + c * 128: o * 1024 + (c + 1) * 128],
                self._qkv_xv(c, nt), start=False, stop=False))
        pq = self.pe(lambda e, o=o, psum=psum: e.matmul(
            psum[:, 0:512], self.bias_row[0:1, o * 128:(o + 1) * 128],
            self.ones_row[0:1, 0:512], start=False, stop=True))
        d_u = self.dve(lambda e, psum=psum, h0=h0: e.tensor_mul(
            self.tscr2[:, 512:1024], psum[:, 0:512], self.rmB2[:, h0:h0 + 512]),
            waits=[(sPE, pq), (sDVE, self.dBA[nt])])
        dst = dests[o] if o != 2 else self.tscr2[:, 0:512]
        d_o = self.dve(lambda e, o=o, dst=dst, h0=h0: e.scalar_tensor_tensor(
            dst, self.rmB1[:, h0:h0 + 512], self.wsum[:, o:o + 1],
            self.tscr2[:, 512:1024], ALU.mult, ALU.add))
        self.fixA.setdefault(nt, {})[o] = d_o
        if o != 1:
            self.rel_w0 = (sDVE, d_u)
        else:
            self.rel_w1 = (sDVE, d_u)
        if o == 2:
            self.B_rel[buf] = (sDVE, d_o)
            self.xbuf_rel[buf] = (sPE, pq)

    def _qkv_vt(self, b, nt):
        sPE, sDVE = self.sPE, self.sDVE
        vo = self.vtp(b)
        pt = 0
        for i in range(4):
            pt = self.pe(lambda e, i=i: e.transpose(
                self.ps_w1[:, i * 128:(i + 1) * 128], self.tscr2[:, i * 128:(i + 1) * 128],
                self.ident_f[:]),
                waits=[(sDVE, self.fixA[nt][2]), self.rel_w1] if i == 0 else ())
        d_vt = self.dve(lambda e, nt=nt, vo=vo: e.tensor_copy(
            AP(self.vt, vo + nt * 4 * 132, [[2 * 32 * 132, 128], [132, 4], [66, 2], [1, 64]]),
            AP(self.ps_w1, 0, [[512, 128], [128, 4], [64, 2], [1, 64]])),
            waits=[(sPE, pt)])
        self.rel_w1 = (sDVE, d_vt)
        self.vt_rel = (sDVE, d_vt)

    def make_qkv_thunks(self, b, stage):
        th = []
        if stage == 0:
            p = b % 2
            th.append(lambda: self._qkv_dma(b, 0))
            th.append(lambda: self.dve(
                lambda e, p=p: e.memset(self.colmax[:, 4 * p:4 * p + 2], 0.0)))
            th.append(lambda: self._qkv_stats_a(b, 0))
            th.append(lambda: self._qkv_stats_b(b, 0))
            th.append(lambda: self._qkv_dma(b, 1))
            return th
        nt = stage - 1
        for o in range(3):
            th.append(lambda o=o: self._qkv_proj_a(b, nt, o))
            th.append(lambda o=o: self._qkv_proj_b(b, nt, o))
        th.append(lambda: self._qkv_vt(b, nt))
        if stage < NT:
            th.append(lambda: self._qkv_stats_a(b, stage))
            th.append(lambda: self._qkv_stats_b(b, stage))
            if stage + 1 < NT:
                th.append(lambda: self._qkv_dma(b, stage + 1))
        return th

    def _ub_norms(self, b, j):
        sPE, sACT, sDVE, sPOOL = self.sPE, self.sACT, self.sDVE, self.sPOOL
        p = b % 2
        qo = self.qkp(b)
        cm = 4 * p
        off = j * T
        pl = self.pool(lambda e, qo=qo, off=off: e.tensor_mul(
            self.xsq[:, 0:4096], self.qk[:, qo + off:qo + off + 4096],
            self.qk[:, qo + off:qo + off + 4096]),
            waits=[(sDVE, self.fixA[7][j]), self.xsq_rel])
        pn = 0
        for r in range(4):
            pn = self.pe(lambda e, r=r: e.matmul(
                self.ps_w0[0:1, 0:512], self.onescol_bf[:, 0:1],
                self.xsq[:, (2 * r) * 512:(2 * r + 1) * 512], start=True, stop=True),
                waits=[(sPOOL, pl), self.rel_w0] if r == 0 else [self.rel_w0])
            pn = self.pe(lambda e, r=r: e.matmul(
                self.ps_w1[0:1, 0:512], self.onescol_bf[:, 0:1],
                self.xsq[:, (2 * r + 1) * 512:(2 * r + 2) * 512], start=True, stop=True),
                waits=[self.rel_w1])
            dq = self.dve(lambda e, cm=cm: e.reduce_max(
                self.colmax[0:1, cm + 2:cm + 3],
                AP(self.ps_wpair, 0, [[1024, 1], [1, 1024]]), axis=AX.X),
                waits=[(sPE, pn)])
            self.dve(lambda e, j=j, cm=cm: e.tensor_max(
                self.colmax[0:1, cm + j:cm + j + 1], self.colmax[0:1, cm + j:cm + j + 1],
                self.colmax[0:1, cm + 2:cm + 3]))
            self.rel_w0 = (sDVE, dq)
            self.rel_w1 = (sDVE, dq)
        self.xsq_rel = (sPE, pn)

    def emit_ub(self, b):
        sPE, sACT, sDVE, sPOOL = self.sPE, self.sACT, self.sDVE, self.sPOOL
        p = b % 2
        cm = 4 * p
        d_k_last = self.sDVE.n
        d = self.dve(lambda e, cm=cm: e.tensor_mul(self.tiny[0:1, 14:15],
                                                   self.colmax[0:1, cm:cm + 1],
                                                   self.colmax[0:1, cm + 1:cm + 2]),
                     waits=[(sDVE, d_k_last), (sACT, self.ubc_act_rel)])
        a = self.act(lambda e: e.activation(self.tiny[0:1, 15:16], self.tiny[0:1, 14:15], AF.Ln),
                     waits=[(sDVE, d)])
        a = self.act(lambda e: e.activation(self.tiny[0:1, 14:15], self.tiny[0:1, 15:16], AF.Exp,
                                            scale=0.5))
        d = self.dve(lambda e: e.scalar_tensor_tensor(self.tiny[0:1, 15:16], self.tiny[0:1, 14:15],
                                                      0.125, self.tiny[0:1, 9:10], ALU.mult, ALU.add),
                     waits=[(sACT, a)])
        d = self.dve(lambda e: e.tensor_scalar_mul(self.tiny[0:1, 12:13], self.tiny[0:1, 15:16], -1.0))
        d = self.dve(lambda e: e.tensor_sub(self.tiny[0:1, 13:14], self.tiny[0:1, 10:11],
                                            self.tiny[0:1, 15:16]))
        d = self.dve(lambda e: e.tensor_sub(self.tiny[0:1, 14:15], self.tiny[0:1, 11:12],
                                            self.tiny[0:1, 15:16]))
        pb = self.pe(lambda e: e.matmul(self.ps_w0[:, 2:3], self.ones1_f[0:1, :],
                                        self.tiny[0:1, 12:13], start=True, stop=True),
                     waits=[(sDVE, d), self.rel_w0])
        pb = self.pe(lambda e: e.matmul(self.ps_w0[:, 3:4], self.ones1_f[0:1, :],
                                        self.tiny[0:1, 13:14], start=True, stop=True))
        pb = self.pe(lambda e: e.matmul(self.ps_w0[:, 4:5], self.ones1_f[0:1, :],
                                        self.tiny[0:1, 14:15], start=True, stop=True))
        a_ub = self.act(lambda e, p=p: e.copy(self.ubc[:, 4 * p:4 * p + 3], self.ps_w0[:, 2:5]),
                        waits=[(sPE, pb)])
        self.rel_w0 = (sACT, a_ub)
        self.ubc_act_rel = a_ub
        self.qkv_done[b] = [(sACT, a_ub), self.band_ready, self.vt_rel]

    def ub_thunks(self, b):
        return [lambda: self._ub_norms(b, 0), lambda: self._ub_norms(b, 1),
                lambda: self.emit_ub(b)]

    def emit_attn_qc(self, b, qc, thunks=()):
        thunks = list(thunks)
        if thunks:
            step = max(1, (NKT - 3) // len(thunks))
            points = set(2 + i * step for i in range(len(thunks)))
        else:
            points = set()
        sPE, sACT, sDVE, sPOOL = self.sPE, self.sACT, self.sDVE, self.sPOOL
        p = b % 2
        qo = self.qkp(b)
        vo = self.vtp(b)
        cm = 4 * p
        slots = (self.ps_s0, self.ps_s1)
        q0 = qc * 512
        pv_done = 0
        exp_of = {}

        def emit_pv(kt):
            nonlocal pv_done
            poff = (kt % 2) * 1024
            w = [(sACT, exp_of[kt])]
            if kt == 0:
                w.append(self.rel_ctx)
            self.pe(lambda e, kt=kt, poff=poff, vo=vo: e.matmul(
                self.ps_ctxA[:, :], self.vt[:, vo + kt * 132:vo + kt * 132 + 65],
                self.pbuf[:, poff:poff + 512], start=(kt == 0), stop=(kt == NKT - 1)), waits=w)
            pv_done = self.pe(lambda e, kt=kt, poff=poff, vo=vo: e.matmul(
                self.ps_ctxB[:, :], self.vt[:, vo + kt * 132 + 66:vo + kt * 132 + 131],
                self.pbuf[:, poff + 512:poff + 1024], start=(kt == 0), stop=(kt == NKT - 1)))

        for kt in range(NKT):
            sl = kt % 2
            pss = slots[sl]
            poff = sl * 1024
            cls = band_class(kt, qc)
            w = list(self.slot_rel[sl])
            if qc == 0 and kt < 2:
                w += self.qkv_done[b]
            self.pe(lambda e, pss=pss, kt=kt, q0=q0, cls=cls, qo=qo: e.matmul(
                pss[:, 0:512], self.qk[0:64, qo + T + kt * 128:qo + T + (kt + 1) * 128],
                self.qk[0:64, qo + q0:qo + q0 + 512], start=True, stop=(cls != 0),
                tile_position=(0, 0)), waits=w)
            ps_last = self.pe(lambda e, pss=pss, kt=kt, q0=q0, cls=cls, qo=qo: e.matmul(
                pss[:, 512:1024], self.qk[64:128, qo + T + kt * 128:qo + T + (kt + 1) * 128],
                self.qk[64:128, qo + q0:qo + q0 + 512], start=True, stop=(cls != 0),
                tile_position=(64, 0)))
            if cls == 0:
                s0_ = 3968 - kt * 128 + q0
                self.pe(lambda e, pss=pss, s0_=s0_: e.matmul(
                    pss[:, 0:512], self.ident_bf[:], self.band[:, s0_:s0_ + 512],
                    start=False, stop=True))
                ps_last = self.pe(lambda e, pss=pss, s0_=s0_: e.matmul(
                    pss[:, 512:1024], self.ident_bf[:], self.band[:, s0_:s0_ + 512],
                    start=False, stop=True))
            a_exp = self.act(lambda e, pss=pss, poff=poff, cls=cls, cm=cm: e.activation(
                self.pbuf[:, poff:poff + 1024], pss[:, 0:1024], AF.Exp, scale=0.125,
                bias=self.ubc[:, cm + cls:cm + cls + 1]), waits=[(sPE, ps_last)])
            exp_of[kt] = a_exp
            self.slot_rel[sl] = [(sACT, a_exp)]
            if kt >= 1:
                emit_pv(kt - 1)
            if thunks and kt in points:
                thunks.pop(0)()
        while thunks:
            thunks.pop(0)()
        emit_pv(NKT - 1)
        a_r = self.act(lambda e: e.activation(self.lnrow[0:1, 0:512], self.ps_ctxA[64:65, 0:512],
                                              AF.Ln), waits=[(sPE, pv_done)])
        a_r = self.act(lambda e: e.activation(self.reciprow[0:1, 0:512], self.lnrow[0:1, 0:512],
                                              AF.Exp, scale=-1.0))
        a_rA = self.act(lambda e: e.activation(self.lnrow[0:1, 0:512], self.ps_ctxB[64:65, 0:512],
                                               AF.Ln), waits=[self.ctxAB_rel])
        pbA = self.pe(lambda e: e.matmul(self.ps_w0[0:64, 0:512], self.ones_row[0:1, 0:64],
                                         self.reciprow[0:1, 0:512], start=True, stop=True),
                      waits=[(sACT, a_rA), self.rel_w0])
        a_r = self.act(lambda e: e.activation(self.reciprow[0:1, 0:512], self.lnrow[0:1, 0:512],
                                              AF.Exp, scale=-1.0), waits=[(sPE, pbA)])
        pb = self.pe(lambda e: e.matmul(self.ps_w1[0:64, 0:512], self.ones_row[0:1, 0:64],
                                        self.reciprow[0:1, 0:512], start=True, stop=True),
                     waits=[(sACT, a_r), self.rel_w1])
        a_e = self.act(lambda e: e.copy(self.recb[0:64, 0:512], self.ps_w0[0:64, 0:512]),
                       waits=[(sPE, pb)])
        a_e = self.act(lambda e: e.copy(self.recb[64:128, 0:512], self.ps_w1[0:64, 0:512]))
        d_norm = self.dve(lambda e, q0=q0: e.tensor_mul(
            self.ctxAB[0:64, q0:q0 + 512], self.ps_ctxA[0:64, 0:512],
            self.recb[0:64, 0:512]), waits=[(sACT, a_e)])
        d_norm = self.dve(lambda e, q0=q0: e.tensor_mul(
            self.ctxAB[64:128, q0:q0 + 512], self.ps_ctxB[0:64, 0:512],
            self.recb[64:128, 0:512]))
        self.rel_w0 = (sACT, a_e)
        self.rel_w1 = (sACT, a_e)
        self.rel_ctx = (sDVE, d_norm)
        self.d_norm_last = d_norm

    def _scr_part(self, b, part, dn):
        sPOOL, sDVE = self.sPOOL, self.sDVE
        pl = 0
        for idx in range(part * 4, part * 4 + 4):
            h, s16 = divmod(idx, 16)
            g = 8 * h + b
            pch = s16 // 2
            pb = 64 * (s16 % 2)
            pl = self.pool(lambda e, g=g, h=h, s16=s16, pch=pch, pb=pb: e.tensor_copy(
                self.scr[pb:pb + 64, pch * T + g * 256: pch * T + (g + 1) * 256],
                self.ctxAB[h * 64:(h + 1) * 64, s16:T:16]),
                waits=[(sDVE, dn)] if idx == part * 4 else ())
        if part == 7:
            self.ctxAB_rel = (sPOOL, pl)

    def scr_thunks(self, b):
        dn = self.d_norm_last
        return [lambda part=part: self._scr_part(b, part, dn) for part in range(8)]

    def emit_scr(self, b):
        dn = self.d_norm_last
        for part in range(8):
            self._scr_part(b, part, dn)

    def phaseA(self):
        self.dgA = {}
        self.dBA = {}
        self.fixA = {}
        self.vt_rel = (self.sPE, 0)
        self.slot_rel = [[self.rel_s0], [self.rel_s1]]
        for stage in range(NT + 1):
            for t in self.make_qkv_thunks(0, stage):
                t()
        for t in self.ub_thunks(0):
            t()
        for b in range(B):
            for qc in range(NT):
                th = []
                if b + 1 < B:
                    th += self.make_qkv_thunks(b + 1, qc)
                    if qc == NT - 1:
                        th += self.make_qkv_thunks(b + 1, NT) + self.ub_thunks(b + 1)
                if b >= 1 and qc == 0:
                    th = th[:1] + self.scr_thunks(b - 1) + th[1:]
                self.emit_attn_qc(b, qc, th)
            if b == B - 1:
                self.emit_scr(b)
        self.qkv_ready = (self.sPE, self.sPE.n)
        self.rel_s0 = self.slot_rel[0][0]
        self.rel_s1 = self.slot_rel[1][0]

    def close512(self, src, q0, extra_ready=(), dst=None, bsel=0):
        h0 = bsel * 512
        """stream[c][:, q0:q0+512] (or dst chunks) += src(c)*rmB2 + rmB1."""
        sPOOL, sDVE, sACT = self.sPOOL, self.sDVE, self.sACT
        d_cl = 0
        d_read = self.sDVE.n
        for c in range(NCH):
            pl = self.pool(lambda e, c=c, h0=h0: e.tensor_mul(self.bscr[:, 512:1024], src(c),
                                                              self.rmB2[:, h0:h0 + 512]),
                           waits=(list(extra_ready) if c == 0 else []) + [(sDVE, d_read)])
            d_read = self.dve(lambda e, h0=h0: e.tensor_add(
                self.bscr[:, 0:512], self.bscr[:, 512:1024], self.rmB1[:, h0:h0 + 512]),
                              waits=[(sPOOL, pl)])
            tgt = self.stream[c][:, q0:q0 + 512] if dst is None else dst(c)
            d_cl = self.dve(lambda e, tgt=tgt: e.tensor_add(tgt, tgt, self.bscr[:, 0:512]))
        del d_read
        self.B_rel[bsel] = (sDVE, d_cl)
        return d_cl

    def phase_wo_close(self):
        sPE, sACT, sDVE, sPOOL, dG = self.sPE, self.sACT, self.sDVE, self.sPOOL, self.dG
        dg_w = self.wload3("gpsimd", "dG", self.wslot[0], 0, self.d_woT, D,
                           waits=[self.wslot_rel[0]])
        dg_x = {}
        for c in range(NCH):
            dg_x[c] = self.dma("gpsimd", "dT", lambda e, c=c: e.dma_start(
                out=self.stream[c][:], in_=self.d_xTk[c]),
                waits=[self.stream_rel[c]])
        d_close = 0
        for quar in range(4):
            qq0 = quar * 1024
            pool_snap = self.sPOOL.n
            a_y = 0
            for j in range(NCH):
                for ntl in range(2):
                    q0 = qq0 + ntl * 512
                    even = (j * 2 + ntl) % 2 == 0
                    psum = self.ps_w0 if even else self.ps_w1
                    rel = self.rel_w0 if even else self.rel_w1
                    p = 0
                    for c in range(NCH):
                        p = self.pe(lambda e, j=j, c=c, q0=q0, psum=psum: e.matmul(
                            psum[:, 0:512],
                            self.wslot[0][:, c * D + j * 128: c * D + (j + 1) * 128],
                            self.scr[:, c * T + q0: c * T + q0 + 512],
                            start=(c == 0), stop=(c == NCH - 1)),
                            waits=([rel, (dG, dg_w)] +
                                   ([self.qkv_ready, self.ctxAB_rel]
                                    if (quar == 0 and j == 0 and ntl == 0) else []))
                            if c == 0 else ())
                    a_y = self.act(lambda e, j=j, ntl=ntl, psum=psum: e.activation(
                        self.qk[:, j * 1024 + ntl * 512: j * 1024 + ntl * 512 + 512],
                        psum[:, 0:512], AF.Identity, bias=self.bo_col[:, j:j + 1]),
                        waits=[(sPE, p), (sPOOL, pool_snap)])
                    if even:
                        self.rel_w0 = (sACT, a_y)
                    else:
                        self.rel_w1 = (sACT, a_y)
            for ntl in range(2):
                q0 = qq0 + ntl * 512

                def yv(c, ntl=ntl):
                    return self.qk[:, c * 1024 + ntl * 512: c * 1024 + ntl * 512 + 512]

                dB = self.stats512(yv, [(sACT, a_y)])
                xw = [(self.sems["dT"], dg_x[NCH - 1]), (sDVE, dB)]
                d_close = self.close512(yv, q0, extra_ready=xw)
        self.wslot_rel[0] = (sPE, self.sems["PE"].n)
        self.stream_written = (sDVE, d_close)
        self.qk_rel = (sDVE, d_close)

    def apply512(self, src, dst, q0, extra=None, bsel=0):
        h0 = bsel * 512
        """dst(c) (bf16) = src(c)*rmB2 + rmB1 (extra: per-c second multiply col/tile)."""
        sPOOL, sDVE = self.sPOOL, self.sDVE
        d_read = self.sDVE.n
        d = 0
        for c in range(NCH):
            pl = self.pool(lambda e, c=c, h0=h0: e.tensor_mul(self.bscr[:, 512:1024], src(c),
                                                              self.rmB2[:, h0:h0 + 512]),
                           waits=[(sDVE, d_read)])
            d_read = self.dve(lambda e, c=c, h0=h0: e.tensor_add(dst(c), self.bscr[:, 512:1024],
                                                                 self.rmB1[:, h0:h0 + 512]),
                              waits=[(sPOOL, pl)])
            d = d_read
        self.B_rel[bsel] = (sDVE, d)
        return d

    def phaseF(self):
        sPE, sACT, sDVE, sPOOL, dG, dS = self.sPE, self.sACT, self.sDVE, self.sPOOL, self.dG, self.dS
        self.wload3("gpsimd", "dG", self.wslot[0], 0, self.d_fwin, 64)
        dg_w = self.dma("gpsimd", "dG", lambda e: e.dma_start(
            out=self.wslot[1][0:64, 0:D], in_=self.d_fwout))
        ds = self.dma("sync", "dS", lambda e: e.dma_start(out=self.fw12[:, 0:16], in_=self.d_fw1))
        ds = self.dma("sync", "dS", lambda e: e.dma_start(out=self.fw12[:, 16:32], in_=self.d_fb1))
        ds = self.dma("sync", "dS", lambda e: e.dma_start(out=self.fw12[:, 32:48], in_=self.d_fw2))
        ds = self.dma("sync", "dS", lambda e: e.dma_start(
            out=self.svec[0:64, 0:1], in_=self.d_fbin.rearrange("a m -> m a"),
            allow_slow_non_contiguous=True))
        ds = self.dma("sync", "dS", lambda e: e.dma_start(
            out=self.svec[0:64, 1:2], in_=self.d_fb2.rearrange("a m -> m a"),
            allow_slow_non_contiguous=True))
        ds = self.dma("sync", "dS", lambda e: e.dma_start(
            out=self.svec[:, 8:16], in_=self.d_fbout.rearrange("c p -> p c"),
            allow_slow_non_contiguous=True))
        xn, outf = self.ck["xn"], self.ck["xm"]

        def sv(c, q0):
            return self.stream[c][:, q0:q0 + 512]

        dBF = {}
        dBF[0] = self.stats512(lambda c: sv(c, 0), [(sDVE, 1)], bsel=0)
        for cq in range(NCH):
            q0 = cq * 512
            bs = cq % 2
            if cq + 1 < NCH:
                dBF[cq + 1] = self.stats512(lambda c, q1=q0 + 512: sv(c, q1),
                                            [(sDVE, 1)], bsel=(cq + 1) % 2)
            d_xn = self.apply512(lambda c, q0=q0: sv(c, q0),
                                 lambda c: xn[:, c * 512:(c + 1) * 512], q0, bsel=bs)
            p = 0
            for c in range(NCH):
                p = self.pe(lambda e, c=c: e.matmul(
                    self.ps_w0[0:64, 0:512], self.wslot[0][:, c * 64:(c + 1) * 64],
                    xn[:, c * 512:(c + 1) * 512], start=(c == 0), stop=(c == NCH - 1)),
                    waits=[(sDVE, d_xn), (dG, dg_w), self.rel_w0] if c == 0 else ())
            a_s = self.act(lambda e: e.activation(self.fmid_bf[0:64, 0:512], self.ps_w0[0:64, 0:512],
                                                  AF.Identity, bias=self.svec[0:64, 0:1]),
                           waits=[(sPE, p), (dS, ds)])
            self.rel_w0 = (sACT, a_s)
            a_g = self.act(lambda e: e.activation(self.fmid[0:64, 0:512], self.fmid_bf[0:64, 0:512],
                                                  AF.Identity, scale=0.0, bias=self.svec[0:64, 1:2]))
            d_g = 0
            a_r = {}
            for i in range(16):
                ro = 512 * (1 + i % 3)
                w = []
                if i >= 3:
                    w.append((sDVE, d_g - (2 if False else 2)))
                a_r[i] = self.act(lambda e, i=i, ro=ro: e.activation(
                    self.fmid_bf[0:64, ro:ro + 512], self.fmid_bf[0:64, 0:512], AF.Relu,
                    scale=self.fw12[0:64, i:i + 1], bias=self.fw12[0:64, 16 + i:17 + i]),
                    waits=[(sDVE, max(0, d_g - 2))] if i >= 3 else ())
                d_g = self.dve(lambda e, i=i, ro=ro: e.scalar_tensor_tensor(
                    self.fmid[0:64, 0:512], self.fmid_bf[0:64, ro:ro + 512],
                    self.fw12[0:64, 32 + i:33 + i], self.fmid[0:64, 0:512],
                    ALU.mult, ALU.add), waits=[(sACT, a_r[i]), (sACT, a_g)] if i == 0 else [(sACT, a_r[i])])
            d_g = self.dve(lambda e: e.tensor_copy(self.fmid_bf[0:64, 0:512],
                                                   self.fmid[0:64, 0:512]))
            a_o = 0
            for j in range(NCH):
                psum = self.ps_w0 if j % 2 == 0 else self.ps_w1
                rel = self.rel_w0 if j % 2 == 0 else self.rel_w1
                p = self.pe(lambda e, j=j, psum=psum: e.matmul(
                    psum[:, 0:512], self.wslot[1][0:64, j * 128:(j + 1) * 128],
                    self.fmid_bf[0:64, 0:512], start=True, stop=True),
                    waits=[(sDVE, d_g), rel])
                a_o = self.act(lambda e, j=j, psum=psum: e.activation(
                    outf[:, j * 512:(j + 1) * 512], psum[:, 0:512], AF.Identity,
                    bias=self.svec[:, 8 + j:9 + j]), waits=[(sPE, p)])
                if j % 2 == 0:
                    self.rel_w0 = (sACT, a_o)
                else:
                    self.rel_w1 = (sACT, a_o)
            dBo = self.stats512(lambda c: outf[:, c * 512:(c + 1) * 512], [(sACT, a_o)], bsel=bs)
            self.close512(lambda c: outf[:, c * 512:(c + 1) * 512], q0,
                          extra_ready=[(sDVE, dBo)], bsel=bs)
        self.wslot_rel = [(sPE, self.sPE.n), (sPE, self.sPE.n)]

    def phaseC(self):
        sPE, sACT, sDVE, sPOOL, dG, dS = self.sPE, self.sACT, self.sDVE, self.sPOOL, self.dG, self.dS
        dW1 = self.sems["dW1"]
        ds = self.dma("sync", "dS", lambda e: e.dma_start(
            out=self.cv[:, 0:8], in_=self.d_crmsw.rearrange("c p -> p c"),
            allow_slow_non_contiguous=True))
        ds = self.dma("sync", "dS", lambda e: e.dma_start(
            out=self.cv[:, 8:16], in_=self.d_cbA.rearrange("c p -> p c"),
            allow_slow_non_contiguous=True))
        d_c0 = self.dve(lambda e: e.memset(self.carry[:, 0:8], 0.0))
        xm, xn, aa, bb, gg = (self.ck[n] for n in ("xm", "xn", "aa", "bb", "gg"))
        slot_rel = list(self.wslot_rel)
        slot_sem = (self.dG, dW1)

        def sv(c, q0):
            return self.stream[c][:, q0:q0 + 512]

        dBC = {}
        dBC[0] = self.stats512(lambda c: sv(c, 0), [(sDVE, 1)], rms=True, bsel=0)
        d_h_prev = 0
        for cq in range(NCH):
            q0 = cq * 512
            bs = cq % 2
            worder = ((self.d_cwog, 0), (self.d_cwx, 1), (self.d_cwA, 0),
                      (self.d_cwsg, 1), (self.d_cwo, 0))
            wdma = []
            for wsrc, sl in worder[:2]:
                wdma.append(self.wload3("gpsimd", "dG" if sl == 0 else "dW1",
                                        self.wslot[sl], 0, wsrc, D, waits=[slot_rel[sl]]))
            d_xm = self.apply512(lambda c, q0=q0: sv(c, q0),
                                 lambda c: xm[:, c * 512:(c + 1) * 512], q0, bsel=bs)
            d_xn = 0
            for c in range(NCH):
                d_xn = self.dve(lambda e, c=c, bs=bs: e.scalar_tensor_tensor(
                    xn[:, c * 512:(c + 1) * 512], xm[:, c * 512:(c + 1) * 512],
                    self.cv[:, c:c + 1], self.ainv_t[:, bs * 512:bs * 512 + 512],
                    ALU.mult, ALU.mult),
                    waits=[(dS, ds)] if c == 0 else ())

            def proj(widx, wsl, dgv, act, dst, bias=None):
                act_snap = self.sACT.n
                lastd = 0
                for j in range(NCH):
                    psum = self.ps_w0 if j % 2 == 0 else self.ps_w1
                    rel = self.rel_w0 if j % 2 == 0 else self.rel_w1
                    p = 0
                    for c in range(NCH):
                        p = self.pe(lambda e, j=j, c=c, psum=psum, wsl=wsl: e.matmul(
                            psum[:, 0:512],
                            self.wslot[wsl][:, c * D + j * 128: c * D + (j + 1) * 128],
                            xn[:, c * 512:(c + 1) * 512],
                            start=(c == 0), stop=(c == NCH - 1)),
                            waits=[(sDVE, d_xn), (slot_sem[wsl], dgv), rel] if c == 0 else ())
                    if act == "sig":
                        lastd = self.act(lambda e, j=j, psum=psum: e.activation(
                            dst(j), psum[:, 0:512], AF.Sigmoid,
                            bias=(self.cv[:, 8 + j:9 + j] if bias else 0.0)),
                            waits=[(sPE, p), (sDVE, d_h_prev)])
                        relv = (sACT, lastd)
                    else:
                        lastd = self.dve(lambda e, j=j, psum=psum: e.tensor_mul(
                            dst(j), psum[:, 0:512], bb[:, j * 512:(j + 1) * 512]),
                            waits=[(sPE, p), (sACT, act_snap)])
                        relv = (sDVE, lastd)
                    if j % 2 == 0:
                        self.rel_w0 = relv
                    else:
                        self.rel_w1 = relv
                return lastd

            a_sg = proj(0, 0, wdma[0], "sig", lambda j: bb[:, j * 512:(j + 1) * 512])
            if cq + 1 < NCH:
                dBC[cq + 1] = self.stats512(lambda c, q1=q0 + 512: sv(c, q1),
                                            [(sDVE, 1)], rms=True, bsel=(cq + 1) % 2)
            wdma.append(self.wload3("gpsimd", "dG", self.wslot[0], 0, worder[2][0], D,
                                    waits=[self.rel_w0, self.rel_w1]))
            d_b = proj(1, 1, wdma[1], "mulsg", lambda j: bb[:, j * 512:(j + 1) * 512])
            wdma.append(self.wload3("gpsimd", "dW1", self.wslot[1], 0, worder[3][0], D,
                                    waits=[self.rel_w0, self.rel_w1]))
            a_A = proj(2, 0, wdma[2], "sig", lambda j: aa[:, j * 512:(j + 1) * 512], bias=True)
            wdma.append(self.wload3("gpsimd", "dG", self.wslot[0], 0, worder[4][0], D,
                                    waits=[self.rel_w0, self.rel_w1]))
            a_gate = proj(3, 1, wdma[3], "sig", lambda j: gg[:, j * 512:(j + 1) * 512])
            pe_xn_done = self.sPE.n
            d_h = 0
            for c in range(NCH):
                d_h = self.dve(lambda e, c=c: e.tensor_tensor_scan(
                    xn[:, c * 512:(c + 1) * 512], aa[:, c * 512:(c + 1) * 512],
                    bb[:, c * 512:(c + 1) * 512], self.carry[:, c:c + 1],
                    ALU.mult, ALU.add),
                    waits=([(sACT, a_A), (sDVE, d_b), (sPE, pe_xn_done)] if c == 0 else ()))
                d_h = self.dve(lambda e, c=c: e.tensor_copy(
                    self.carry[:, c:c + 1], xn[:, c * 512 + 511:c * 512 + 512]))
            d_h_prev = d_h
            d_y = 0
            for c in range(NCH):
                d_y = self.dve(lambda e, c=c: e.tensor_mul(
                    gg[:, c * 512:(c + 1) * 512], gg[:, c * 512:(c + 1) * 512],
                    xn[:, c * 512:(c + 1) * 512]), waits=[(sACT, a_gate)] if c == 0 else ())
            d_oc = 0
            for j in range(NCH):
                psum = self.ps_w0 if j % 2 == 0 else self.ps_w1
                rel = self.rel_w0 if j % 2 == 0 else self.rel_w1
                p = 0
                for c in range(NCH):
                    p = self.pe(lambda e, j=j, c=c, psum=psum: e.matmul(
                        psum[:, 0:512],
                        self.wslot[0][:, c * D + j * 128: c * D + (j + 1) * 128],
                        gg[:, c * 512:(c + 1) * 512],
                        start=(c == 0), stop=(c == NCH - 1)),
                        waits=[(sDVE, d_y), (dG, wdma[4]), rel] if c == 0 else ())
                d_oc = self.dve(lambda e, j=j, psum=psum: e.tensor_add(
                    xm[:, j * 512:(j + 1) * 512], psum[:, 0:512],
                    xm[:, j * 512:(j + 1) * 512]), waits=[(sPE, p)])
                if j % 2 == 0:
                    self.rel_w0 = (sDVE, d_oc)
                else:
                    self.rel_w1 = (sDVE, d_oc)
            slot_rel = [self.rel_w0, self.rel_w1]
            dBo = self.stats512(lambda c: xm[:, c * 512:(c + 1) * 512], [(sDVE, d_oc)], bsel=bs)
            self.close512(lambda c: xm[:, c * 512:(c + 1) * 512], q0,
                          extra_ready=[(sDVE, dBo)], bsel=bs)
        self.wslot_rel = [(sPE, self.sPE.n), (sPE, self.sPE.n)]

    def phaseD(self):
        sPE, sACT, sDVE, sPOOL, dG, dS = self.sPE, self.sACT, self.sDVE, self.sPOOL, self.dG, self.dS
        dW1 = self.sems["dW1"]
        ds = self.dma("sync", "dS", lambda e: e.dma_start(
            out=self.cv[:, 0:8], in_=self.d_mur.rearrange("c p -> p c"),
            allow_slow_non_contiguous=True))
        ds = self.dma("sync", "dS", lambda e: e.dma_start(
            out=self.cv[:, 8:16], in_=self.d_muk.rearrange("c p -> p c"),
            allow_slow_non_contiguous=True))
        d_mu = 0
        for h, (src_c, dst_c) in enumerate(((0, 0), (8, 8))):
            d_mu = self.dve(lambda e, src_c=src_c, dst_c=dst_c: e.tensor_scalar(
                self.svec[:, dst_c:dst_c + 8], self.cv[:, src_c:src_c + 8], -1.0, 1.0,
                ALU.mult, ALU.add), waits=[(dS, ds)] if h == 0 else ())
        d_cd = self.dve(lambda e: e.memset(self.carry[:, 8:16], 0.0))
        xms, xr, xk, rr, ka = self.ck_xms, self.ck["aa"], self.ck["bb"], self.ck["gg"], self.ck["xn"]
        od = self.ck["xm"]
        slot_rel = list(self.wslot_rel)

        def sv(c, q0):
            return self.stream[c][:, q0:q0 + 512]

        dBD = {}
        dBD[0] = self.stats512(lambda c: sv(c, 0), [(sDVE, d_mu)], bsel=0)
        for cq in range(NCH):
            q0 = cq * 512
            bs = cq % 2
            wdma_r = self.wload3("gpsimd", "dG", self.wslot[0], 0, self.d_dwrT, D,
                                 waits=[slot_rel[0]])
            wdma_k = self.wload3("gpsimd", "dW1", self.wslot[1], 0, self.d_dwkT, D,
                                 waits=[slot_rel[1]])
            d_sh = 0
            for c in range(NCH):
                d_sh = self.dve(lambda e, c=c: e.tensor_copy(
                    xms[:, c * 513:c * 513 + 1], self.carry[:, 8 + c:9 + c]),
                    waits=[(sDVE, d_cd)] if (cq == 0 and c == 0) else ())
            d_xm = self.apply512(lambda c, q0=q0: sv(c, q0),
                                 lambda c: xms[:, c * 513 + 1:c * 513 + 513], q0, bsel=bs)
            for c in range(NCH):
                d_xm = self.dve(lambda e, c=c: e.tensor_copy(
                    self.carry[:, 8 + c:9 + c], xms[:, c * 513 + 512:c * 513 + 513]))
            if cq + 1 < NCH:
                dBD[cq + 1] = self.stats512(lambda c, q1=q0 + 512: sv(c, q1),
                                            [(sDVE, 1)], bsel=(cq + 1) % 2)
            d_mix = 0
            for c in range(NCH):
                cur = xms[:, c * 513 + 1:c * 513 + 513]
                prv = xms[:, c * 513:c * 513 + 512]
                d1 = self.dve(lambda e, c=c, prv=prv: e.tensor_scalar_mul(
                    self.tscr2[:, 0:512], prv, self.svec[:, c:c + 1]))
                d_mix = self.dve(lambda e, c=c, cur=cur: e.scalar_tensor_tensor(
                    xr[:, c * 512:(c + 1) * 512], cur, self.cv[:, c:c + 1],
                    self.tscr2[:, 0:512], ALU.mult, ALU.add))
                d1 = self.dve(lambda e, c=c, prv=prv: e.tensor_scalar_mul(
                    self.tscr2[:, 0:512], prv, self.svec[:, 8 + c:9 + c]))
                d_mix = self.dve(lambda e, c=c, cur=cur: e.scalar_tensor_tensor(
                    xk[:, c * 512:(c + 1) * 512], cur, self.cv[:, 8 + c:9 + c],
                    self.tscr2[:, 0:512], ALU.mult, ALU.add))

            def proj2(wsl, dgv, src2, fin, waits0=()):
                lastd = 0
                for j in range(NCH):
                    psum = self.ps_w0 if j % 2 == 0 else self.ps_w1
                    rel = self.rel_w0 if j % 2 == 0 else self.rel_w1
                    p = 0
                    for c in range(NCH):
                        p = self.pe(lambda e, j=j, c=c, psum=psum, wsl=wsl: e.matmul(
                            psum[:, 0:512],
                            self.wslot[wsl][:, c * D + j * 128: c * D + (j + 1) * 128],
                            src2[:, c * 512:(c + 1) * 512],
                            start=(c == 0), stop=(c == NCH - 1)),
                            waits=(list(waits0) + [(self.dG if wsl == 0 else dW1, dgv), rel])
                            if c == 0 else ())
                    lastd, relv = fin(j, psum, p)
                    if j % 2 == 0:
                        self.rel_w0 = relv
                    else:
                        self.rel_w1 = relv
                return lastd

            def fin_r(j, psum, p):
                a = self.act(lambda e, j=j, psum=psum: e.activation(
                    rr[:, j * 512:(j + 1) * 512], psum[:, 0:512], AF.Sigmoid),
                    waits=[(sPE, p)])
                return a, (sACT, a)

            k_prev_act = [0]

            def fin_k(j, psum, p):
                d1 = self.dve(lambda e, psum=psum: e.tensor_scalar_max(
                    self.tscr2[:, 512:1024], psum[:, 0:512], 0.0),
                    waits=[(sPE, p), (sACT, k_prev_act[0])])
                a = self.act(lambda e, j=j: e.activation(
                    ka[:, j * 512:(j + 1) * 512], self.tscr2[:, 512:1024], AF.Square),
                    waits=[(sDVE, d1)])
                k_prev_act[0] = a
                return a, (sDVE, d1)

            def fin_v(j, psum, p):
                d1 = self.dve(lambda e, j=j, psum=psum: e.tensor_mul(
                    od[:, j * 512:(j + 1) * 512], psum[:, 0:512],
                    rr[:, j * 512:(j + 1) * 512]), waits=[(sPE, p)])
                return d1, (sDVE, d1)

            a_r = proj2(0, wdma_r, xr, fin_r, waits0=[(sDVE, d_mix)])
            a_k = proj2(1, wdma_k, xk, fin_k)
            wdma_v = self.wload3("gpsimd", "dG", self.wslot[0], 0, self.d_dwvT, D,
                                 waits=[self.rel_w0, self.rel_w1])
            d_v = proj2(0, wdma_v, ka, fin_v, waits0=[(sACT, self.sACT.n)])
            slot_rel = [self.rel_w0, self.rel_w1]
            dBo = self.stats512(lambda c: od[:, c * 512:(c + 1) * 512], [(sDVE, d_v)], bsel=bs)
            self.close512(lambda c: od[:, c * 512:(c + 1) * 512], q0,
                          extra_ready=[(sDVE, dBo)], bsel=bs)
        self.wslot_rel = [(sPE, self.sPE.n), (sPE, self.sPE.n)]

    def phase_out(self):
        sPE, sACT, sDVE, sST = self.sPE, self.sACT, self.sDVE, self.sST
        st_after = {}
        st_count = 0
        pe_prev = 0
        for tt in range(32):
            t0 = tt * 128
            ob = tt % 2
            d_c = 0
            for c in range(NCH):
                half = self.tscr if c < 4 else self.tscr2
                d_c = self.dve(lambda e, c=c, t0=t0, half=half: e.tensor_copy(
                    half[:, (c % 4) * 128:(c % 4 + 1) * 128], self.stream[c][:, t0:t0 + 128]),
                    waits=([self.stream_written, (sPE, pe_prev), self.B_rel[0], self.B_rel[1]]
                           if c == 0 else ()))
            p = 0
            for c in range(NCH):
                half = self.tscr if c < 4 else self.tscr2
                psum = self.ps_w0 if c < 4 else self.ps_w1
                p = self.pe(lambda e, c=c, half=half, psum=psum: e.transpose(
                    psum[:, (c % 4) * 128:(c % 4 + 1) * 128],
                    half[:, (c % 4) * 128:(c % 4 + 1) * 128], self.ident_f[:]),
                    waits=([(sDVE, d_c), self.rel_w0, self.rel_w1] if c == 0 else ()))
            pe_prev = p
            a = self.act(lambda e, ob=ob: e.copy(self.outstage[ob][:, 0:512], self.ps_w0[:, 0:512]),
                         waits=[(sPE, p)] + ([(sST, st_after.get(tt - 1, 0))] if tt >= 2 else []))
            a = self.act(lambda e, ob=ob: e.copy(self.outstage[ob][:, 512:1024], self.ps_w1[:, 0:512]))
            self.rel_w0 = (sACT, a)
            self.rel_w1 = (sACT, a)
            st_count = self.dma("sync", "ST", lambda e, t0=t0, ob=ob: e.dma_start(
                out=self.d_out[t0:t0 + 128, :], in_=self.outstage[ob][:]),
                waits=[(sACT, a)])
            st_after[tt] = st_count
        if self.on("sync"):
            self.e.wait_ge(self.sST.h, st_count)

    def build(self):
        blk_cm = self.nc.Block()
        blk = blk_cm.__enter__()
        with self.nc.allow_low_precision(reason="bf16 operands are intentional"):
            for engn in ENGS:
                getattr(blk, engn)(lambda e, _n=engn: self.emit(_n, e))
        blk_cm.__exit__(None, None, None)
        first = self._pass_counts[0]
        for cts in self._pass_counts[1:]:
            assert cts == first, (cts, first)
        return self.nc


NCORES = 8


def _host_prepare(inputs, core, xT, tb2):
    f = np.float32
    r0 = 128 * core
    d = dict(
        xT=xT,
        xTk=xT[core],
        wq=np.ascontiguousarray(inputs["L0_attn_Wq"][r0:r0 + 128, :].T.reshape(NCH, 128, 128)),
        wk=np.ascontiguousarray(inputs["L0_attn_Wk"][r0:r0 + 128, :].T.reshape(NCH, 128, 128)),
        wv=np.ascontiguousarray(inputs["L0_attn_Wv"][r0:r0 + 128, :].T.reshape(NCH, 128, 128)),
        bqkv=np.concatenate([inputs["L0_attn_bq"][r0:r0 + 128],
                             inputs["L0_attn_bk"][r0:r0 + 128],
                             inputs["L0_attn_bv"][r0:r0 + 128]]).reshape(1, 384).astype(f),
        woT=np.ascontiguousarray(inputs["L0_attn_Wo"].T.reshape(NCH, 128, D)),
        bo=inputs["L0_attn_bo"].reshape(NCH, 128).astype(f),
        tb2=tb2.astype(f),
        etis=np.array([[inputs["L0_attn_et_gate"].reshape(()),
                        inputs["L0_attn_is_gate"].reshape(())]], dtype=f),
        eye=np.eye(128, dtype=f),
        fwin=np.ascontiguousarray(inputs["L0_ffn_Win"].reshape(NCH, 128, 64)),
        fbin=inputs["L0_ffn_bin_"].reshape(1, 64).astype(f),
        fw1=inputs["L0_ffn_W1"].astype(f), fb1=inputs["L0_ffn_b1"].astype(f),
        fw2=inputs["L0_ffn_W2"].astype(f),
        fb2=inputs["L0_ffn_b2"].reshape(1, 64).astype(f),
        fwout=inputs["L0_ffn_Wout"].astype(f),
        fbout=inputs["L0_ffn_bout"].reshape(NCH, 128).astype(f),
        crmsw=inputs["L1_attn_rms_weight"].reshape(NCH, 128).astype(f),
        cwA=np.ascontiguousarray(inputs["L1_attn_w_A"].reshape(NCH, 128, D)),
        cbA=inputs["L1_attn_b_A"].reshape(NCH, 128).astype(f),
        cwx=np.ascontiguousarray(inputs["L1_attn_w_x"].reshape(NCH, 128, D)),
        cwsg=np.ascontiguousarray(inputs["L1_attn_w_sidegate"].reshape(NCH, 128, D)),
        cwog=np.ascontiguousarray(inputs["L1_attn_w_output"].reshape(NCH, 128, D)),
        cwo=np.ascontiguousarray(inputs["L1_attn_w_o"].reshape(NCH, 128, D)),
        dwrT=np.ascontiguousarray(inputs["L1_ffn_W_r"].T.reshape(NCH, 128, D)),
        dwkT=np.ascontiguousarray(inputs["L1_ffn_W_k"].T.reshape(NCH, 128, D)),
        dwvT=np.ascontiguousarray(inputs["L1_ffn_W_v"].T.reshape(NCH, 128, D)),
        mur=inputs["L1_ffn_mu_r"].reshape(NCH, 128).astype(f),
        muk=inputs["L1_ffn_mu_k"].reshape(NCH, 128).astype(f),
    )
    return d


def kernel(**inputs):
    x = np.asarray(inputs["L0_attn_x"], dtype=np.float32)
    xT = np.ascontiguousarray(x.transpose(0, 2, 1).reshape(B, NCH, 128, T))
    tb = np.asarray(inputs["L0_attn_time_bias"], dtype=np.float32).reshape(-1)
    idx = np.clip(C0 + np.arange(128)[:, None] - np.arange(WB)[None, :], 0, 499)
    tb2 = np.ascontiguousarray(tb[idx])
    bld = Builder()
    nc = bld.build()
    in_maps = [_host_prepare(inputs, k, xT, tb2) for k in range(NCORES)]
    res = run_bass_kernel_spmd(nc, in_maps, core_ids=list(range(NCORES)))
    out = np.stack([np.asarray(res.results[k]["out"]) for k in range(NCORES)], axis=0)
    return out


# version 19
# speedup vs baseline: 1.0253x; 1.0253x over previous
import numpy as np
import ml_dtypes
import concourse.bass as bass
import concourse.mybir as mybir
from concourse.bass_utils import run_bass_kernel_spmd

F32 = mybir.dt.float32
BF16 = mybir.dt.bfloat16
AF = mybir.ActivationFunctionType
ALU = mybir.AluOpType
AX = mybir.AxisListType
AP = bass.AP

B = 8
T = 4096
D = 1024
NCH = 8
NT = 8
NKT = 32
WB = 8064
C0 = 4218
EPS = 1e-5
RD = 1.0 / D

ENGS = ("sync", "tensor", "vector", "scalar", "gpsimd")
UPTO = "full"
CHAIN_SAME_ENGINE = {"DVE": True, "ACT": False, "POOL": False}


def band_class(kt, qc):
    kmin = kt * 128 - (qc * 512 + 511)
    kmax = kt * 128 + 127 - qc * 512
    if kmax <= -250:
        return 1
    if kmin >= 249:
        return 2
    return 0


class Sem:
    __slots__ = ("h", "n", "name")

    def __init__(self, h, name):
        self.h = h
        self.n = 0
        self.name = name


class Builder:
    def __init__(self, upto=UPTO):
        self.upto = upto
        nc = self.nc = bass.Bass("TRN2", target_bir_lowering=False)

        dt_ = nc.dram_tensor
        self.d_xT = dt_("xT", [B, NCH, 128, T], F32, kind="ExternalInput").ap()
        self.d_xTk = dt_("xTk", [NCH, 128, T], F32, kind="ExternalInput").ap()
        self.d_wq = dt_("wq", [NCH, 128, 128], F32, kind="ExternalInput").ap()
        self.d_wk = dt_("wk", [NCH, 128, 128], F32, kind="ExternalInput").ap()
        self.d_wv = dt_("wv", [NCH, 128, 128], F32, kind="ExternalInput").ap()
        self.d_bqkv = dt_("bqkv", [1, 384], F32, kind="ExternalInput").ap()
        self.d_woT = dt_("woT", [NCH, 128, D], F32, kind="ExternalInput").ap()
        self.d_bo = dt_("bo", [NCH, 128], F32, kind="ExternalInput").ap()
        self.d_tb2 = dt_("tb2", [128, WB], F32, kind="ExternalInput").ap()
        self.d_etis = dt_("etis", [1, 2], F32, kind="ExternalInput").ap()
        self.d_eye = dt_("eye", [128, 128], F32, kind="ExternalInput").ap()
        self.d_fwin = dt_("fwin", [NCH, 128, 64], F32, kind="ExternalInput").ap()
        self.d_fbin = dt_("fbin", [1, 64], F32, kind="ExternalInput").ap()
        self.d_fw1 = dt_("fw1", [64, 16], F32, kind="ExternalInput").ap()
        self.d_fb1 = dt_("fb1", [64, 16], F32, kind="ExternalInput").ap()
        self.d_fw2 = dt_("fw2", [64, 16], F32, kind="ExternalInput").ap()
        self.d_fb2 = dt_("fb2", [1, 64], F32, kind="ExternalInput").ap()
        self.d_fwout = dt_("fwout", [64, D], F32, kind="ExternalInput").ap()
        self.d_fbout = dt_("fbout", [NCH, 128], F32, kind="ExternalInput").ap()
        self.d_crmsw = dt_("crmsw", [NCH, 128], F32, kind="ExternalInput").ap()
        self.d_cwA = dt_("cwA", [NCH, 128, D], F32, kind="ExternalInput").ap()
        self.d_cbA = dt_("cbA", [NCH, 128], F32, kind="ExternalInput").ap()
        self.d_cwx = dt_("cwx", [NCH, 128, D], F32, kind="ExternalInput").ap()
        self.d_cwsg = dt_("cwsg", [NCH, 128, D], F32, kind="ExternalInput").ap()
        self.d_cwog = dt_("cwog", [NCH, 128, D], F32, kind="ExternalInput").ap()
        self.d_cwo = dt_("cwo", [NCH, 128, D], F32, kind="ExternalInput").ap()
        self.d_dwrT = dt_("dwrT", [NCH, 128, D], F32, kind="ExternalInput").ap()
        self.d_dwkT = dt_("dwkT", [NCH, 128, D], F32, kind="ExternalInput").ap()
        self.d_dwvT = dt_("dwvT", [NCH, 128, D], F32, kind="ExternalInput").ap()
        self.d_mur = dt_("mur", [NCH, 128], F32, kind="ExternalInput").ap()
        self.d_muk = dt_("muk", [NCH, 128], F32, kind="ExternalInput").ap()
        self.d_out = dt_("out", [T, D], F32, kind="ExternalOutput").ap()

        sb = nc.alloc_sbuf_tensor

        def gsb(name, shape, dt):
            guard = nc.sbuf_tensor(name, shape, dt, side="right")
            h = guard.__enter__()
            self._guards[name] = guard
            return h

        self._guards = {}
        self.xsq = sb("xsq", [128, NCH * 512], BF16)
        self.ident_f = sb("ident_f", [128, 128], F32)
        self.ident_bf = sb("ident_bf", [128, 128], BF16)
        self.ones128 = sb("ones128", [128, 128], BF16)
        self.ones_row = sb("ones_row", [1, 512], BF16)
        self.onescol_bf = sb("onescol_bf", [128, 1], BF16)
        self.ones1_f = sb("ones1_f", [1, 128], F32)
        self.onescol_f = sb("onescol_f", [128, 1], F32)
        self.bias_row = sb("bias_row", [1, 384], BF16)
        self.bo_col = sb("bo_col", [128, NCH], F32)
        self.wsum = sb("wsum", [128, 4], F32)
        self.tiny = sb("tiny", [128, 16], F32)
        self.ubc = sb("ubc", [128, 8], F32)
        self.rmB1 = sb("rmB1", [128, 1024], BF16)
        self.rmB2 = sb("rmB2", [128, 1024], BF16)
        self.tscr = sb("tscr", [128, 1024], F32)
        self.tscr2 = sb("tscr2", [128, 1024], F32)
        self.reciprow = sb("reciprow", [1, 512], BF16)
        self.lnrow = sb("lnrow", [1, 512], F32)
        self.recb = sb("recb", [128, 512], BF16)
        self.bscr = sb("bscr", [128, 1024], BF16)
        self.colmax = sb("colmax", [128, 8], F32)
        self.svec = sb("svec", [128, 16], F32)
        self.eps_col = sb("eps_col", [128, 1], F32)
        self.scr = gsb("scr", [128, NCH * T], BF16)
        self.qk = gsb("qk", [128, 4 * T], BF16)
        self.x_nt = gsb("x_nt", [128, 2 * NCH * 512], BF16)
        self.vt = gsb("vt", [128, 2 * 32 * 132], BF16)
        self.band = gsb("band", [128, WB], BF16)
        self.pbuf = gsb("pbuf", [128, 2 * 1024], BF16)
        self.ctxAB = gsb("ctxAB", [128, T], BF16)
        self.wsb = gsb("wsb", [128, 3 * NCH * 128], BF16)
        for n in ("wsb", "ctxAB", "pbuf", "band", "vt", "x_nt"):
            self._guards.pop(n).__exit__(None, None, None)
        self.stream = [sb(f"stream{c}", [128, T], BF16) for c in range(NCH)]
        self.wslot = [sb("wslot0", [128, NCH * D], BF16)]
        for n in ("qk", "scr"):
            self._guards.pop(n).__exit__(None, None, None)
        self.wslot.append(sb("wslot1", [128, NCH * D], BF16))
        self.ck = {n: sb("ck_" + n, [128, NCH * 512], BF16)
                   for n in ("xm", "xn", "aa", "bb", "gg")}
        self.ck_xms = sb("ck_xms", [128, NCH * 513], BF16)
        self.ainv_t = sb("ainv_t", [128, 1024], BF16)
        self.outstage = [sb("outs0", [128, 1024], F32), sb("outs1", [128, 1024], F32)]
        self.fmid = sb("fmid", [64, 512], F32)
        self.fmid_bf = sb("fmid_bf", [64, 512 * 4], BF16)
        self.fw12 = sb("fw12", [64, 48], F32)
        self.carry = sb("carry", [128, 24], F32)
        self.cv = sb("cv", [128, 16], F32)

        ps = nc.alloc_psum_tensor
        self.ps_s0 = ps("ps_s0", [128, 1024], F32)
        self.ps_s1 = ps("ps_s1", [128, 1024], F32)
        self.ps_ctxA = ps("ps_ctxA", [65, 512], F32)
        self.ps_ctxB = ps("ps_ctxB", [65, 512], F32)
        _g0 = nc.psum_tensor("ps_w0", [128, 512], F32)
        self.ps_w0 = _g0.__enter__()
        _g1 = nc.psum_tensor("ps_w1", [128, 512], F32)
        self.ps_w1 = _g1.__enter__()
        _g1.__exit__(None, None, None)
        _g0.__exit__(None, None, None)
        self.ps_wpair = ps("ps_wpair", [128, 1024], F32)

        self.sems = {}
        for name in ("dS", "dG", "dW1", "dT", "dX0", "dX1", "PE", "ACT", "DVE", "POOL", "ST"):
            self.sems[name] = Sem(nc.alloc_semaphore(name), name)
        self._pass_counts = []

    def on(self, eng):
        return self.engn == eng

    def pe(self, fn, waits=()):
        s = self.sems["PE"]
        if self.on("tensor"):
            for sem, val in waits:
                if val > 0:
                    self.e.wait_ge(sem.h, val)
            fn(self.e).then_inc(s.h, 1)
        s.n += 1
        return s.n

    def _chained(self, engkey, engname, fn, waits):
        s = self.sems[engkey]
        if self.on(engname):
            for sem, val in waits:
                if val > 0:
                    self.e.wait_ge(sem.h, val)
            if CHAIN_SAME_ENGINE.get(engkey, True) and s.n > 0:
                self.e.wait_ge(s.h, s.n)
            fn(self.e).then_inc(s.h, 1)
        s.n += 1
        return s.n

    def dve(self, fn, waits=()):
        return self._chained("DVE", "vector", fn, waits)

    def act(self, fn, waits=()):
        return self._chained("ACT", "scalar", fn, waits)

    def pool(self, fn, waits=()):
        return self._chained("POOL", "gpsimd", fn, waits)

    def dma(self, eng, semkey, fn, waits=()):
        s = self.sems[semkey]
        if self.on(eng):
            for sem, val in waits:
                if val > 0:
                    self.e.wait_ge(sem.h, val)
            fn(self.e).then_inc(s.h, 16)
        s.n += 16
        return s.n

    def wload3(self, eng, semkey, dst, dst_off, src, wd, nch=NCH, waits=()):
        cols = dst.shape[1]
        return self.dma(eng, semkey, lambda e: e.dma_start(
            out=AP(dst, dst_off, [[cols, 128], [wd, nch], [1, wd]]),
            in_=src.rearrange("c p m -> p c m")), waits=waits)

    def phase_barrier(self):
        for s in self.sems.values():
            if s.n > 0:
                self.e.wait_ge(s.h, s.n)

    def emit(self, engn, eng):
        self.engn = engn
        self.e = eng
        for x in self.sems.values():
            x.n = 0
        S = self.sems
        self.dS, self.dG = S["dS"], S["dG"]
        self.dX = [S["dX0"], S["dX1"]]
        self.sPE, self.sACT, self.sDVE, self.sPOOL = S["PE"], S["ACT"], S["DVE"], S["POOL"]
        self.sST = S["ST"]
        z = (self.sPE, 0)
        self.rel_s0 = z
        self.rel_s1 = z
        self.rel_w0 = z
        self.rel_w1 = z
        self.rel_ctx = z
        self.xbuf_rel = [z, z]
        self.xsq_rel = z
        self.B_rel = [z, z]
        self.ctxAB_rel = z
        self.qkv_ready = z
        self.stream_rel = [z] * NCH
        self.wslot_rel = [z, z]

        self.ubc_act_rel = 0
        self.qkv_done = {}

        self.preamble()
        self.phaseA()
        self.phase_barrier()
        self.phase_wo_close()
        self.phase_barrier()
        if self.upto in ("l0f", "l1a", "full"):
            self.phaseF()
            self.phase_barrier()
        if self.upto in ("l1a", "full"):
            self.phaseC()
            self.phase_barrier()
        if self.upto == "full":
            self.phaseD()
            self.phase_barrier()
        self.phase_out()
        self._pass_counts.append({k: v.n for k, v in S.items()})

    def preamble(self):
        dS, dG, sPE, sACT, sDVE = self.dS, self.dG, self.sPE, self.sACT, self.sDVE

        self.dma("sync", "dS", lambda e: e.dma_start(out=self.ident_f[:], in_=self.d_eye))
        self.dma("sync", "dS", lambda e: e.dma_start(out=self.tiny[0:1, 0:2], in_=self.d_etis))
        d_sync0 = self.dma("sync", "dS", lambda e: e.dma_start(
            out=self.bo_col[:], in_=self.d_bo.rearrange("c p -> p c"),
            allow_slow_non_contiguous=True))
        self.dma("gpsimd", "dG", lambda e: e.dma_start(out=self.band[:], in_=self.d_tb2))
        for j, w in enumerate((self.d_wq, self.d_wk, self.d_wv)):
            self.wload3("gpsimd", "dG", self.wsb, j * 1024, w, 128)
        d_g0 = self.dma("gpsimd", "dG", lambda e: e.dma_start(out=self.bias_row[:], in_=self.d_bqkv))

        self.dve(lambda e: e.memset(self.ones128[:], 1.0))
        self.dve(lambda e: e.memset(self.ones_row[:], 1.0))
        self.dve(lambda e: e.memset(self.eps_col[:], EPS))
        self.dve(lambda e: e.memset(self.onescol_bf[:], 1.0))
        self.dve(lambda e: e.memset(self.ones1_f[:], 1.0))
        self.dve(lambda e: e.memset(self.onescol_f[:], 1.0))
        self.dve(lambda e: e.memset(AP(self.vt, 64, [[2 * 32 * 132, 128], [132, 64]]), 1.0))
        self.dve(lambda e: e.memset(AP(self.vt, 130, [[2 * 32 * 132, 128], [132, 64]]), 1.0))
        d_idbf = self.dve(lambda e: e.tensor_copy(self.ident_bf[:], self.ident_f[:]),
                          waits=[(dS, d_sync0)])
        a_sig = self.act(lambda e: e.activation(self.tiny[0:1, 3:4], self.tiny[0:1, 0:1], AF.Sigmoid),
                         waits=[(dS, d_sync0)])
        p = 0
        for o in range(3):
            for c in range(NCH):
                p = self.pe(lambda e, o=o, c=c: e.matmul(
                    self.ps_w0[:, o:o + 1],
                    self.wsb[:, o * 1024 + c * 128: o * 1024 + (c + 1) * 128],
                    self.onescol_bf[:, 0:1], start=(c == 0), stop=(c == NCH - 1)),
                    waits=([(dG, d_g0), (sDVE, d_idbf)] if (o == 0 and c == 0) else ()))
        a_ws = self.act(lambda e: e.copy(self.wsum[:, 0:3], self.ps_w0[:, 0:3]), waits=[(sPE, p)])
        self.dve(lambda e: e.tensor_scalar_mul(self.tiny[0:1, 4:5], self.tiny[0:1, 3:4], 8.0),
                 waits=[(sACT, a_sig)])
        d2 = self.dve(lambda e: e.tensor_scalar_mul(self.tiny[0:1, 5:6], self.tiny[0:1, 1:2], 8.0))
        p = self.pe(lambda e: e.matmul(self.ps_w1[:, 0:1], self.ones1_f[0:1, :], self.tiny[0:1, 4:5],
                                       start=True, stop=True), waits=[(sDVE, d2)])
        p = self.pe(lambda e: e.matmul(self.ps_w1[:, 1:2], self.ones1_f[0:1, :], self.tiny[0:1, 5:6],
                                       start=True, stop=True))
        a_sc = self.act(lambda e: e.copy(self.tiny[:, 6:8], self.ps_w1[:, 0:2]), waits=[(sPE, p)])
        d_band = self.dve(lambda e: e.tensor_scalar(self.band[:], self.band[:], self.tiny[:, 6:7],
                                                    self.tiny[:, 7:8], ALU.mult, ALU.add),
                          waits=[(sACT, a_sc), (dG, d_g0)])
        self.dve(lambda e: e.reduce_max(self.colmax[:, 2:3], self.band[:], axis=AX.X))
        d3 = self.dve(lambda e: e.tensor_copy(self.tscr[:, 0:1], self.colmax[:, 2:3]))
        p2 = self.pe(lambda e: e.transpose(self.ps_w1[0:1, 0:128], self.tscr[:, 0:1], self.ident_f[:]),
                     waits=[(sDVE, d3), (sACT, a_sc)])
        self.dve(lambda e: e.reduce_max(self.tiny[0:1, 8:9], self.ps_w1[0:1, 0:128], axis=AX.X),
                 waits=[(sPE, p2)])
        self.dve(lambda e: e.tensor_scalar_mul(self.tiny[0:1, 9:10], self.tiny[0:1, 8:9], 0.125))
        self.dve(lambda e: e.tensor_scalar_mul(self.tiny[0:1, 10:11], self.band[0:1, WB - 1:WB], 0.125))
        d_pre = self.dve(lambda e: e.tensor_scalar_mul(self.tiny[0:1, 11:12], self.band[0:1, 0:1], 0.125))
        self.band_ready = (sDVE, d_band)
        self.pre_done = (sDVE, d_pre)
        self.rel_w0 = (sACT, a_ws)
        self.rel_w1 = (sDVE, d_pre)

    def stats512(self, src, ready, rms=False, bsel=0, wp=False, wide_src=None):
        sPE, sACT, sDVE, sPOOL = self.sPE, self.sACT, self.sDVE, self.sPOOL
        h0 = bsel * 512
        B1 = self.rmB1[:, h0:h0 + 512]
        B2 = self.rmB2[:, h0:h0 + 512]
        if wp:
            pA, pB = self.ps_w0[:, 0:512], self.ps_w1[:, 0:512]
            relw = [self.rel_w0, self.rel_w1]
        else:
            pA, pB = self.ps_s0[:, 0:512], self.ps_s0[:, 512:1024]
            relw = [self.rel_s0]
        pl = 0
        if wide_src is not None:
            pl = self.pool(lambda e: e.tensor_mul(self.xsq[:, 0:4096], wide_src(), wide_src()),
                           waits=list(ready) + [self.xsq_rel])
        else:
            for c in range(NCH):
                pl = self.pool(lambda e, c=c: e.tensor_mul(
                    self.xsq[:, c * 512:(c + 1) * 512], src(c), src(c)),
                    waits=(list(ready) + [self.xsq_rel]) if c == 0 else ())
        pe_c = 0
        for c in range(NCH):
            pe_c = self.pe(lambda e, c=c, pA=pA: e.matmul(pA, self.ones128[:], src(c),
                                                          start=(c == 0), stop=(c == NCH - 1)),
                           waits=(list(ready) + relw) if c == 0 else ())
        for c in range(NCH):
            pe_c = self.pe(lambda e, c=c, pB=pB: e.matmul(pB, self.ones128[:],
                                                          self.xsq[:, c * 512:(c + 1) * 512],
                                                          start=(c == 0), stop=(c == NCH - 1)),
                           waits=[(sPOOL, pl)] if c == 0 else ())
        self.xsq_rel = (sPE, pe_c)
        pool_snap = self.sPOOL.n
        mS = self.tscr[:, h0:h0 + 512]
        d = self.dve(lambda e, pA=pA: e.tensor_scalar_mul(mS, pA, RD),
                     waits=[(sPE, pe_c), self.B_rel[bsel], (self.sPOOL, pool_snap)])
        d = self.dve(lambda e: e.tensor_mul(B2, mS, mS))
        vdst = self.tscr2[:, 512:1024] if rms else B2
        d_v = self.dve(lambda e, vdst=vdst, pB=pB: e.scalar_tensor_tensor(
            vdst, pB, RD, B2, ALU.mult, ALU.subtract))
        if wp:
            self.rel_w0 = (sDVE, d_v)
            self.rel_w1 = (sDVE, d_v)
        else:
            self.rel_s0 = (sDVE, d_v)
        a = self.act(lambda e, vdst=vdst: e.activation(B1, vdst, AF.Ln,
                                                       bias=self.eps_col[:, 0:1]),
                     waits=[(sDVE, d_v)])
        a = self.act(lambda e: e.activation(B2, B1, AF.Exp, scale=-0.5))
        dB = self.dve(lambda e: e.scalar_tensor_tensor(B1, mS, -1.0, B2, ALU.mult, ALU.mult),
                      waits=[(sACT, a)])
        if rms:
            d = self.dve(lambda e: e.tensor_mul(self.tscr[:, 512:1024], B2, B2))
            d = self.dve(lambda e: e.tensor_mul(self.tscr[:, 512:1024], self.tscr[:, 512:1024],
                                                vdst))
            a = self.act(lambda e: e.activation(self.tscr2[:, 512:1024], self.tscr[:, 512:1024],
                                                AF.Ln, bias=self.eps_col[:, 0:1]), waits=[(sDVE, d)])
            a = self.act(lambda e: e.activation(self.ainv_t[:, h0:h0 + 512],
                                                self.tscr2[:, 512:1024], AF.Exp, scale=-0.5))
            dB = self.dve(lambda e: e.engine_nop(), waits=[(sACT, a)])
        return dB

    def qkp(self, b):
        return (b % 2) * 2 * T

    def vtp(self, b):
        return (b % 2) * 32 * 132

    def _qkv_xv(self, c, nt):
        xoff = (nt % 2) * 4096
        return self.x_nt[:, xoff + c * 512: xoff + (c + 1) * 512]

    def _qkv_dma(self, b, nt):
        buf = nt % 2
        xoff = buf * 4096
        self.dgA[nt] = self.dma("gpsimd", "dX0" if buf == 0 else "dX1",
                                lambda e, b=b, nt=nt, xoff=xoff: e.dma_start(
            out=AP(self.x_nt, xoff, [[2 * NCH * 512, 128], [512, NCH], [1, 512]]),
            in_=self.d_xT[b, :, :, nt * 512:nt * 512 + 512].rearrange("c p t -> p c t")),
            waits=[self.xbuf_rel[buf]])

    def _qkv_stats_a(self, b, nt):
        sPE, sPOOL = self.sPE, self.sPOOL
        buf = nt % 2
        xoff = buf * 4096
        ready = [(self.dX[buf], self.dgA[nt])]
        pl = self.pool(lambda e, xoff=xoff: e.tensor_mul(
            self.xsq[:, 0:4096], self.x_nt[:, xoff:xoff + 4096],
            self.x_nt[:, xoff:xoff + 4096]), waits=ready + [self.xsq_rel])
        pe_c = 0
        for c in range(NCH):
            pe_c = self.pe(lambda e, c=c: e.matmul(
                self.ps_w0[:, 0:512], self.ones128[:], self._qkv_xv(c, nt),
                start=(c == 0), stop=(c == NCH - 1)),
                waits=(ready + [self.rel_w0, self.rel_w1]) if c == 0 else ())
        self._st_pl = pl
        self._st_pe = pe_c

    def _qkv_stats_b(self, b, nt):
        sPE, sACT, sDVE, sPOOL = self.sPE, self.sACT, self.sDVE, self.sPOOL
        buf = nt % 2
        h0 = buf * 512
        B1 = self.rmB1[:, h0:h0 + 512]
        B2 = self.rmB2[:, h0:h0 + 512]
        pe_c = 0
        for c in range(NCH):
            pe_c = self.pe(lambda e, c=c: e.matmul(
                self.ps_w1[:, 0:512], self.ones128[:], self.xsq[:, c * 512:(c + 1) * 512],
                start=(c == 0), stop=(c == NCH - 1)),
                waits=[(sPOOL, self._st_pl)] if c == 0 else ())
        self.xsq_rel = (sPE, pe_c)
        mS = self.tscr[:, h0:h0 + 512]
        d = self.dve(lambda e: e.tensor_scalar_mul(mS, self.ps_w0[:, 0:512], RD),
                     waits=[(sPE, pe_c), self.B_rel[buf]])
        d = self.dve(lambda e: e.tensor_mul(B2, mS, mS))
        d_v = self.dve(lambda e: e.scalar_tensor_tensor(
            B2, self.ps_w1[:, 0:512], RD, B2, ALU.mult, ALU.subtract))
        self.rel_w0 = (sDVE, d_v)
        self.rel_w1 = (sDVE, d_v)
        a = self.act(lambda e: e.activation(B1, B2, AF.Ln, bias=self.eps_col[:, 0:1]),
                     waits=[(sDVE, d_v)])
        a = self.act(lambda e: e.activation(B2, B1, AF.Exp, scale=-0.5))
        self.dBA[nt] = self.dve(lambda e: e.scalar_tensor_tensor(
            B1, mS, -1.0, B2, ALU.mult, ALU.mult), waits=[(sACT, a)])

    def _qkv_proj_a(self, b, nt, o):
        sPE, sDVE = self.sPE, self.sDVE
        psum, rel = ((self.ps_w0, self.rel_w0) if o != 1 else (self.ps_w1, self.rel_w1))
        pq = 0
        for c in range(4):
            pq = self.pe(lambda e, o=o, c=c, nt=nt, psum=psum: e.matmul(
                psum[:, 0:512],
                self.wsb[:, o * 1024 + c * 128: o * 1024 + (c + 1) * 128],
                self._qkv_xv(c, nt), start=(c == 0), stop=False),
                waits=[rel] if c == 0 else ())

    def _qkv_proj_b(self, b, nt, o):
        sPE, sDVE = self.sPE, self.sDVE
        buf = nt % 2
        h0 = buf * 512
        q0 = nt * 512
        qo = self.qkp(b)
        dests = (self.qk[:, qo + q0:qo + q0 + 512],
                 self.qk[:, qo + T + q0:qo + T + q0 + 512], None)
        psum = self.ps_w0 if o != 1 else self.ps_w1
        pq = 0
        for c in range(4, NCH):
            pq = self.pe(lambda e, o=o, c=c, nt=nt, psum=psum: e.matmul(
                psum[:, 0:512],
                self.wsb[:, o * 1024 + c * 128: o * 1024 + (c + 1) * 128],
                self._qkv_xv(c, nt), start=False, stop=False))
        pq = self.pe(lambda e, o=o, psum=psum: e.matmul(
            psum[:, 0:512], self.bias_row[0:1, o * 128:(o + 1) * 128],
            self.ones_row[0:1, 0:512], start=False, stop=True))
        d_u = self.dve(lambda e, psum=psum, h0=h0: e.tensor_mul(
            self.tscr2[:, 512:1024], psum[:, 0:512], self.rmB2[:, h0:h0 + 512]),
            waits=[(sPE, pq), (sDVE, self.dBA[nt])])
        dst = dests[o] if o != 2 else self.tscr2[:, 0:512]
        d_o = self.dve(lambda e, o=o, dst=dst, h0=h0: e.scalar_tensor_tensor(
            dst, self.rmB1[:, h0:h0 + 512], self.wsum[:, o:o + 1],
            self.tscr2[:, 512:1024], ALU.mult, ALU.add))
        self.fixA.setdefault(nt, {})[o] = d_o
        if o != 1:
            self.rel_w0 = (sDVE, d_u)
        else:
            self.rel_w1 = (sDVE, d_u)
        if o == 2:
            self.B_rel[buf] = (sDVE, d_o)
            self.xbuf_rel[buf] = (sPE, pq)

    def _qkv_vt(self, b, nt):
        sPE, sDVE = self.sPE, self.sDVE
        vo = self.vtp(b)
        pt = 0
        for i in range(4):
            pt = self.pe(lambda e, i=i: e.transpose(
                self.ps_w1[:, i * 128:(i + 1) * 128], self.tscr2[:, i * 128:(i + 1) * 128],
                self.ident_f[:]),
                waits=[(sDVE, self.fixA[nt][2]), self.rel_w1] if i == 0 else ())
        d_vt = self.dve(lambda e, nt=nt, vo=vo: e.tensor_copy(
            AP(self.vt, vo + nt * 4 * 132, [[2 * 32 * 132, 128], [132, 4], [66, 2], [1, 64]]),
            AP(self.ps_w1, 0, [[512, 128], [128, 4], [64, 2], [1, 64]])),
            waits=[(sPE, pt)])
        self.rel_w1 = (sDVE, d_vt)
        self.vt_rel = (sDVE, d_vt)

    def make_qkv_thunks(self, b, stage):
        th = []
        if stage == 0:
            p = b % 2
            th.append(lambda: self._qkv_dma(b, 0))
            th.append(lambda: self.dve(
                lambda e, p=p: e.memset(self.colmax[:, 4 * p:4 * p + 2], 0.0)))
            th.append(lambda: self._qkv_stats_a(b, 0))
            th.append(lambda: self._qkv_stats_b(b, 0))
            th.append(lambda: self._qkv_dma(b, 1))
            return th
        nt = stage - 1
        if stage < NT:
            th.append(lambda: self._qkv_stats_a(b, stage))
            th.append(lambda: self._qkv_stats_b(b, stage))
        for o in range(3):
            th.append(lambda o=o: self._qkv_proj_a(b, nt, o))
            th.append(lambda o=o: self._qkv_proj_b(b, nt, o))
        th.append(lambda: self._qkv_vt(b, nt))
        if stage + 1 < NT:
            th.append(lambda: self._qkv_dma(b, stage + 1))
        return th

    def _ub_norms(self, b, j):
        sPE, sACT, sDVE, sPOOL = self.sPE, self.sACT, self.sDVE, self.sPOOL
        p = b % 2
        qo = self.qkp(b)
        cm = 4 * p
        off = j * T
        pl = self.pool(lambda e, qo=qo, off=off: e.tensor_mul(
            self.xsq[:, 0:4096], self.qk[:, qo + off:qo + off + 4096],
            self.qk[:, qo + off:qo + off + 4096]),
            waits=[(sDVE, self.fixA[7][j]), self.xsq_rel])
        pn = 0
        for r in range(4):
            pn = self.pe(lambda e, r=r: e.matmul(
                self.ps_w0[0:1, 0:512], self.onescol_bf[:, 0:1],
                self.xsq[:, (2 * r) * 512:(2 * r + 1) * 512], start=True, stop=True),
                waits=[(sPOOL, pl), self.rel_w0] if r == 0 else [self.rel_w0])
            pn = self.pe(lambda e, r=r: e.matmul(
                self.ps_w1[0:1, 0:512], self.onescol_bf[:, 0:1],
                self.xsq[:, (2 * r + 1) * 512:(2 * r + 2) * 512], start=True, stop=True),
                waits=[self.rel_w1])
            dq = self.dve(lambda e, cm=cm: e.reduce_max(
                self.colmax[0:1, cm + 2:cm + 3],
                AP(self.ps_wpair, 0, [[1024, 1], [1, 1024]]), axis=AX.X),
                waits=[(sPE, pn)])
            self.dve(lambda e, j=j, cm=cm: e.tensor_max(
                self.colmax[0:1, cm + j:cm + j + 1], self.colmax[0:1, cm + j:cm + j + 1],
                self.colmax[0:1, cm + 2:cm + 3]))
            self.rel_w0 = (sDVE, dq)
            self.rel_w1 = (sDVE, dq)
        self.xsq_rel = (sPE, pn)

    def emit_ub(self, b):
        sPE, sACT, sDVE, sPOOL = self.sPE, self.sACT, self.sDVE, self.sPOOL
        p = b % 2
        cm = 4 * p
        d_k_last = self.sDVE.n
        d = self.dve(lambda e, cm=cm: e.tensor_mul(self.tiny[0:1, 14:15],
                                                   self.colmax[0:1, cm:cm + 1],
                                                   self.colmax[0:1, cm + 1:cm + 2]),
                     waits=[(sDVE, d_k_last), (sACT, self.ubc_act_rel)])
        a = self.act(lambda e: e.activation(self.tiny[0:1, 15:16], self.tiny[0:1, 14:15], AF.Ln),
                     waits=[(sDVE, d)])
        a = self.act(lambda e: e.activation(self.tiny[0:1, 14:15], self.tiny[0:1, 15:16], AF.Exp,
                                            scale=0.5))
        d = self.dve(lambda e: e.scalar_tensor_tensor(self.tiny[0:1, 15:16], self.tiny[0:1, 14:15],
                                                      0.125, self.tiny[0:1, 9:10], ALU.mult, ALU.add),
                     waits=[(sACT, a)])
        d = self.dve(lambda e: e.tensor_scalar_mul(self.tiny[0:1, 12:13], self.tiny[0:1, 15:16], -1.0))
        d = self.dve(lambda e: e.tensor_sub(self.tiny[0:1, 13:14], self.tiny[0:1, 10:11],
                                            self.tiny[0:1, 15:16]))
        d = self.dve(lambda e: e.tensor_sub(self.tiny[0:1, 14:15], self.tiny[0:1, 11:12],
                                            self.tiny[0:1, 15:16]))
        pb = self.pe(lambda e: e.matmul(self.ps_w0[:, 2:3], self.ones1_f[0:1, :],
                                        self.tiny[0:1, 12:13], start=True, stop=True),
                     waits=[(sDVE, d), self.rel_w0])
        pb = self.pe(lambda e: e.matmul(self.ps_w0[:, 3:4], self.ones1_f[0:1, :],
                                        self.tiny[0:1, 13:14], start=True, stop=True))
        pb = self.pe(lambda e: e.matmul(self.ps_w0[:, 4:5], self.ones1_f[0:1, :],
                                        self.tiny[0:1, 14:15], start=True, stop=True))
        a_ub = self.act(lambda e, p=p: e.copy(self.ubc[:, 4 * p:4 * p + 3], self.ps_w0[:, 2:5]),
                        waits=[(sPE, pb)])
        self.rel_w0 = (sACT, a_ub)
        self.ubc_act_rel = a_ub
        self.qkv_done[b] = [(sACT, a_ub), self.band_ready, self.vt_rel]

    def ub_thunks(self, b):
        return [lambda: self._ub_norms(b, 0), lambda: self._ub_norms(b, 1),
                lambda: self.emit_ub(b)]

    def emit_attn_qc(self, b, qc, thunks=()):
        thunks = list(thunks)
        if thunks:
            step = max(1, (NKT - 3) // len(thunks))
            points = set(2 + i * step for i in range(len(thunks)))
        else:
            points = set()
        sPE, sACT, sDVE, sPOOL = self.sPE, self.sACT, self.sDVE, self.sPOOL
        p = b % 2
        qo = self.qkp(b)
        vo = self.vtp(b)
        cm = 4 * p
        slots = (self.ps_s0, self.ps_s1)
        q0 = qc * 512
        pv_done = 0
        exp_of = {}

        def emit_pv(kt):
            nonlocal pv_done
            poff = (kt % 2) * 1024
            w = [(sACT, exp_of[kt])]
            if kt == 0:
                w.append(self.rel_ctx)
            self.pe(lambda e, kt=kt, poff=poff, vo=vo: e.matmul(
                self.ps_ctxA[:, :], self.vt[:, vo + kt * 132:vo + kt * 132 + 65],
                self.pbuf[:, poff:poff + 512], start=(kt == 0), stop=(kt == NKT - 1)), waits=w)
            pv_done = self.pe(lambda e, kt=kt, poff=poff, vo=vo: e.matmul(
                self.ps_ctxB[:, :], self.vt[:, vo + kt * 132 + 66:vo + kt * 132 + 131],
                self.pbuf[:, poff + 512:poff + 1024], start=(kt == 0), stop=(kt == NKT - 1)))

        for kt in range(NKT):
            sl = kt % 2
            pss = slots[sl]
            poff = sl * 1024
            cls = band_class(kt, qc)
            w = list(self.slot_rel[sl])
            if qc == 0 and kt < 2:
                w += self.qkv_done[b]
            self.pe(lambda e, pss=pss, kt=kt, q0=q0, cls=cls, qo=qo: e.matmul(
                pss[:, 0:512], self.qk[0:64, qo + T + kt * 128:qo + T + (kt + 1) * 128],
                self.qk[0:64, qo + q0:qo + q0 + 512], start=True, stop=(cls != 0),
                tile_position=(0, 0)), waits=w)
            ps_last = self.pe(lambda e, pss=pss, kt=kt, q0=q0, cls=cls, qo=qo: e.matmul(
                pss[:, 512:1024], self.qk[64:128, qo + T + kt * 128:qo + T + (kt + 1) * 128],
                self.qk[64:128, qo + q0:qo + q0 + 512], start=True, stop=(cls != 0),
                tile_position=(64, 0)))
            if cls == 0:
                s0_ = 3968 - kt * 128 + q0
                self.pe(lambda e, pss=pss, s0_=s0_: e.matmul(
                    pss[:, 0:512], self.ident_bf[:], self.band[:, s0_:s0_ + 512],
                    start=False, stop=True))
                ps_last = self.pe(lambda e, pss=pss, s0_=s0_: e.matmul(
                    pss[:, 512:1024], self.ident_bf[:], self.band[:, s0_:s0_ + 512],
                    start=False, stop=True))
            a_exp = self.act(lambda e, pss=pss, poff=poff, cls=cls, cm=cm: e.activation(
                self.pbuf[:, poff:poff + 1024], pss[:, 0:1024], AF.Exp, scale=0.125,
                bias=self.ubc[:, cm + cls:cm + cls + 1]), waits=[(sPE, ps_last)])
            exp_of[kt] = a_exp
            self.slot_rel[sl] = [(sACT, a_exp)]
            if kt >= 1:
                emit_pv(kt - 1)
            if thunks and kt in points:
                thunks.pop(0)()
        while thunks:
            thunks.pop(0)()
        emit_pv(NKT - 1)
        a_r = self.act(lambda e: e.activation(self.lnrow[0:1, 0:512], self.ps_ctxA[64:65, 0:512],
                                              AF.Ln), waits=[(sPE, pv_done)])
        a_r = self.act(lambda e: e.activation(self.reciprow[0:1, 0:512], self.lnrow[0:1, 0:512],
                                              AF.Exp, scale=-1.0))
        a_rA = self.act(lambda e: e.activation(self.lnrow[0:1, 0:512], self.ps_ctxB[64:65, 0:512],
                                               AF.Ln), waits=[self.ctxAB_rel])
        pbA = self.pe(lambda e: e.matmul(self.ps_w0[0:64, 0:512], self.ones_row[0:1, 0:64],
                                         self.reciprow[0:1, 0:512], start=True, stop=True),
                      waits=[(sACT, a_rA), self.rel_w0])
        a_r = self.act(lambda e: e.activation(self.reciprow[0:1, 0:512], self.lnrow[0:1, 0:512],
                                              AF.Exp, scale=-1.0), waits=[(sPE, pbA)])
        pb = self.pe(lambda e: e.matmul(self.ps_w1[0:64, 0:512], self.ones_row[0:1, 0:64],
                                        self.reciprow[0:1, 0:512], start=True, stop=True),
                     waits=[(sACT, a_r), self.rel_w1])
        a_e = self.act(lambda e: e.copy(self.recb[0:64, 0:512], self.ps_w0[0:64, 0:512]),
                       waits=[(sPE, pb)])
        a_e = self.act(lambda e: e.copy(self.recb[64:128, 0:512], self.ps_w1[0:64, 0:512]))
        d_norm = self.dve(lambda e, q0=q0: e.tensor_mul(
            self.ctxAB[0:64, q0:q0 + 512], self.ps_ctxA[0:64, 0:512],
            self.recb[0:64, 0:512]), waits=[(sACT, a_e)])
        d_norm = self.dve(lambda e, q0=q0: e.tensor_mul(
            self.ctxAB[64:128, q0:q0 + 512], self.ps_ctxB[0:64, 0:512],
            self.recb[64:128, 0:512]))
        self.rel_w0 = (sACT, a_e)
        self.rel_w1 = (sACT, a_e)
        self.rel_ctx = (sDVE, d_norm)
        self.d_norm_last = d_norm

    def _scr_part(self, b, part, dn):
        sPOOL, sDVE = self.sPOOL, self.sDVE
        pl = 0
        for idx in range(part * 4, part * 4 + 4):
            h, s16 = divmod(idx, 16)
            g = 8 * h + b
            pch = s16 // 2
            pb = 64 * (s16 % 2)
            pl = self.pool(lambda e, g=g, h=h, s16=s16, pch=pch, pb=pb: e.tensor_copy(
                self.scr[pb:pb + 64, pch * T + g * 256: pch * T + (g + 1) * 256],
                self.ctxAB[h * 64:(h + 1) * 64, s16:T:16]),
                waits=[(sDVE, dn)] if idx == part * 4 else ())
        if part == 7:
            self.ctxAB_rel = (sPOOL, pl)

    def scr_thunks(self, b):
        dn = self.d_norm_last
        return [lambda part=part: self._scr_part(b, part, dn) for part in range(8)]

    def emit_scr(self, b):
        dn = self.d_norm_last
        for part in range(8):
            self._scr_part(b, part, dn)

    def phaseA(self):
        self.dgA = {}
        self.dBA = {}
        self.fixA = {}
        self.vt_rel = (self.sPE, 0)
        self.slot_rel = [[self.rel_s0], [self.rel_s1]]
        for stage in range(NT + 1):
            for t in self.make_qkv_thunks(0, stage):
                t()
        for t in self.ub_thunks(0):
            t()
        for b in range(B):
            for qc in range(NT):
                th = []
                if b + 1 < B:
                    th += self.make_qkv_thunks(b + 1, qc)
                    if qc == NT - 1:
                        th += self.make_qkv_thunks(b + 1, NT) + self.ub_thunks(b + 1)
                if b >= 1 and qc == 0:
                    th = th[:1] + self.scr_thunks(b - 1) + th[1:]
                self.emit_attn_qc(b, qc, th)
            if b == B - 1:
                self.emit_scr(b)
        self.qkv_ready = (self.sPE, self.sPE.n)
        self.rel_s0 = self.slot_rel[0][0]
        self.rel_s1 = self.slot_rel[1][0]

    def close512(self, src, q0, extra_ready=(), dst=None, bsel=0):
        h0 = bsel * 512
        """stream[c][:, q0:q0+512] (or dst chunks) += src(c)*rmB2 + rmB1."""
        sPOOL, sDVE, sACT = self.sPOOL, self.sDVE, self.sACT
        d_cl = 0
        d_read = self.sDVE.n
        for c in range(NCH):
            pl = self.pool(lambda e, c=c, h0=h0: e.tensor_mul(self.bscr[:, 512:1024], src(c),
                                                              self.rmB2[:, h0:h0 + 512]),
                           waits=(list(extra_ready) if c == 0 else []) + [(sDVE, d_read)])
            d_read = self.dve(lambda e, h0=h0: e.tensor_add(
                self.bscr[:, 0:512], self.bscr[:, 512:1024], self.rmB1[:, h0:h0 + 512]),
                              waits=[(sPOOL, pl)])
            tgt = self.stream[c][:, q0:q0 + 512] if dst is None else dst(c)
            d_cl = self.dve(lambda e, tgt=tgt: e.tensor_add(tgt, tgt, self.bscr[:, 0:512]))
        del d_read
        self.B_rel[bsel] = (sDVE, d_cl)
        return d_cl

    def phase_wo_close(self):
        sPE, sACT, sDVE, sPOOL, dG = self.sPE, self.sACT, self.sDVE, self.sPOOL, self.dG
        dg_w = self.wload3("gpsimd", "dG", self.wslot[0], 0, self.d_woT, D,
                           waits=[self.wslot_rel[0]])
        dg_x = {}
        for c in range(NCH):
            dg_x[c] = self.dma("gpsimd", "dT", lambda e, c=c: e.dma_start(
                out=self.stream[c][:], in_=self.d_xTk[c]),
                waits=[self.stream_rel[c]])
        d_close = 0
        for quar in range(4):
            qq0 = quar * 1024
            pool_snap = self.sPOOL.n
            a_y = 0
            for j in range(NCH):
                for ntl in range(2):
                    q0 = qq0 + ntl * 512
                    even = (j * 2 + ntl) % 2 == 0
                    psum = self.ps_w0 if even else self.ps_w1
                    rel = self.rel_w0 if even else self.rel_w1
                    p = 0
                    for c in range(NCH):
                        p = self.pe(lambda e, j=j, c=c, q0=q0, psum=psum: e.matmul(
                            psum[:, 0:512],
                            self.wslot[0][:, c * D + j * 128: c * D + (j + 1) * 128],
                            self.scr[:, c * T + q0: c * T + q0 + 512],
                            start=(c == 0), stop=(c == NCH - 1)),
                            waits=([rel, (dG, dg_w)] +
                                   ([self.qkv_ready, self.ctxAB_rel]
                                    if (quar == 0 and j == 0 and ntl == 0) else []))
                            if c == 0 else ())
                    a_y = self.act(lambda e, j=j, ntl=ntl, psum=psum: e.activation(
                        self.qk[:, j * 1024 + ntl * 512: j * 1024 + ntl * 512 + 512],
                        psum[:, 0:512], AF.Identity, bias=self.bo_col[:, j:j + 1]),
                        waits=[(sPE, p), (sPOOL, pool_snap)])
                    if even:
                        self.rel_w0 = (sACT, a_y)
                    else:
                        self.rel_w1 = (sACT, a_y)
            for ntl in range(2):
                q0 = qq0 + ntl * 512

                def yv(c, ntl=ntl):
                    return self.qk[:, c * 1024 + ntl * 512: c * 1024 + ntl * 512 + 512]

                dB = self.stats512(yv, [(sACT, a_y)])
                xw = [(self.sems["dT"], dg_x[NCH - 1]), (sDVE, dB)]
                d_close = self.close512(yv, q0, extra_ready=xw)
        self.wslot_rel[0] = (sPE, self.sems["PE"].n)
        self.stream_written = (sDVE, d_close)
        self.qk_rel = (sDVE, d_close)

    def apply512(self, src, dst, q0, extra=None, bsel=0):
        h0 = bsel * 512
        """dst(c) (bf16) = src(c)*rmB2 + rmB1 (extra: per-c second multiply col/tile)."""
        sPOOL, sDVE = self.sPOOL, self.sDVE
        d_read = self.sDVE.n
        d = 0
        for c in range(NCH):
            pl = self.pool(lambda e, c=c, h0=h0: e.tensor_mul(self.bscr[:, 512:1024], src(c),
                                                              self.rmB2[:, h0:h0 + 512]),
                           waits=[(sDVE, d_read)])
            d_read = self.dve(lambda e, c=c, h0=h0: e.tensor_add(dst(c), self.bscr[:, 512:1024],
                                                                 self.rmB1[:, h0:h0 + 512]),
                              waits=[(sPOOL, pl)])
            d = d_read
        self.B_rel[bsel] = (sDVE, d)
        return d

    def phaseF(self):
        sPE, sACT, sDVE, sPOOL, dG, dS = self.sPE, self.sACT, self.sDVE, self.sPOOL, self.dG, self.dS
        self.wload3("gpsimd", "dG", self.wslot[0], 0, self.d_fwin, 64)
        dg_w = self.dma("gpsimd", "dG", lambda e: e.dma_start(
            out=self.wslot[1][0:64, 0:D], in_=self.d_fwout))
        ds = self.dma("sync", "dS", lambda e: e.dma_start(out=self.fw12[:, 0:16], in_=self.d_fw1))
        ds = self.dma("sync", "dS", lambda e: e.dma_start(out=self.fw12[:, 16:32], in_=self.d_fb1))
        ds = self.dma("sync", "dS", lambda e: e.dma_start(out=self.fw12[:, 32:48], in_=self.d_fw2))
        ds = self.dma("sync", "dS", lambda e: e.dma_start(
            out=self.svec[0:64, 0:1], in_=self.d_fbin.rearrange("a m -> m a"),
            allow_slow_non_contiguous=True))
        ds = self.dma("sync", "dS", lambda e: e.dma_start(
            out=self.svec[0:64, 1:2], in_=self.d_fb2.rearrange("a m -> m a"),
            allow_slow_non_contiguous=True))
        ds = self.dma("sync", "dS", lambda e: e.dma_start(
            out=self.svec[:, 8:16], in_=self.d_fbout.rearrange("c p -> p c"),
            allow_slow_non_contiguous=True))
        xn, outf = self.ck["xn"], self.ck["xm"]

        def sv(c, q0):
            return self.stream[c][:, q0:q0 + 512]

        dBF = {}
        dBF[0] = self.stats512(lambda c: sv(c, 0), [(sDVE, 1)], bsel=0)
        for cq in range(NCH):
            q0 = cq * 512
            bs = cq % 2
            if cq + 1 < NCH:
                dBF[cq + 1] = self.stats512(lambda c, q1=q0 + 512: sv(c, q1),
                                            [(sDVE, 1)], bsel=(cq + 1) % 2)
            d_xn = self.apply512(lambda c, q0=q0: sv(c, q0),
                                 lambda c: xn[:, c * 512:(c + 1) * 512], q0, bsel=bs)
            p = 0
            for c in range(NCH):
                p = self.pe(lambda e, c=c: e.matmul(
                    self.ps_w0[0:64, 0:512], self.wslot[0][:, c * 64:(c + 1) * 64],
                    xn[:, c * 512:(c + 1) * 512], start=(c == 0), stop=(c == NCH - 1)),
                    waits=[(sDVE, d_xn), (dG, dg_w), self.rel_w0] if c == 0 else ())
            a_s = self.act(lambda e: e.activation(self.fmid_bf[0:64, 0:512], self.ps_w0[0:64, 0:512],
                                                  AF.Identity, bias=self.svec[0:64, 0:1]),
                           waits=[(sPE, p), (dS, ds)])
            self.rel_w0 = (sACT, a_s)
            a_g = self.act(lambda e: e.activation(self.fmid[0:64, 0:512], self.fmid_bf[0:64, 0:512],
                                                  AF.Identity, scale=0.0, bias=self.svec[0:64, 1:2]))
            d_g = 0
            a_r = {}
            for i in range(16):
                ro = 512 * (1 + i % 3)
                w = []
                if i >= 3:
                    w.append((sDVE, d_g - (2 if False else 2)))
                a_r[i] = self.act(lambda e, i=i, ro=ro: e.activation(
                    self.fmid_bf[0:64, ro:ro + 512], self.fmid_bf[0:64, 0:512], AF.Relu,
                    scale=self.fw12[0:64, i:i + 1], bias=self.fw12[0:64, 16 + i:17 + i]),
                    waits=[(sDVE, max(0, d_g - 2))] if i >= 3 else ())
                d_g = self.dve(lambda e, i=i, ro=ro: e.scalar_tensor_tensor(
                    self.fmid[0:64, 0:512], self.fmid_bf[0:64, ro:ro + 512],
                    self.fw12[0:64, 32 + i:33 + i], self.fmid[0:64, 0:512],
                    ALU.mult, ALU.add), waits=[(sACT, a_r[i]), (sACT, a_g)] if i == 0 else [(sACT, a_r[i])])
            d_g = self.dve(lambda e: e.tensor_copy(self.fmid_bf[0:64, 0:512],
                                                   self.fmid[0:64, 0:512]))
            a_o = 0
            for j in range(NCH):
                psum = self.ps_w0 if j % 2 == 0 else self.ps_w1
                rel = self.rel_w0 if j % 2 == 0 else self.rel_w1
                p = self.pe(lambda e, j=j, psum=psum: e.matmul(
                    psum[:, 0:512], self.wslot[1][0:64, j * 128:(j + 1) * 128],
                    self.fmid_bf[0:64, 0:512], start=True, stop=True),
                    waits=[(sDVE, d_g), rel])
                a_o = self.act(lambda e, j=j, psum=psum: e.activation(
                    outf[:, j * 512:(j + 1) * 512], psum[:, 0:512], AF.Identity,
                    bias=self.svec[:, 8 + j:9 + j]), waits=[(sPE, p)])
                if j % 2 == 0:
                    self.rel_w0 = (sACT, a_o)
                else:
                    self.rel_w1 = (sACT, a_o)
            dBo = self.stats512(lambda c: outf[:, c * 512:(c + 1) * 512], [(sACT, a_o)], bsel=bs)
            self.close512(lambda c: outf[:, c * 512:(c + 1) * 512], q0,
                          extra_ready=[(sDVE, dBo)], bsel=bs)
        self.wslot_rel = [(sPE, self.sPE.n), (sPE, self.sPE.n)]

    def phaseC(self):
        sPE, sACT, sDVE, sPOOL, dG, dS = self.sPE, self.sACT, self.sDVE, self.sPOOL, self.dG, self.dS
        dW1 = self.sems["dW1"]
        ds = self.dma("sync", "dS", lambda e: e.dma_start(
            out=self.cv[:, 0:8], in_=self.d_crmsw.rearrange("c p -> p c"),
            allow_slow_non_contiguous=True))
        ds = self.dma("sync", "dS", lambda e: e.dma_start(
            out=self.cv[:, 8:16], in_=self.d_cbA.rearrange("c p -> p c"),
            allow_slow_non_contiguous=True))
        d_c0 = self.dve(lambda e: e.memset(self.carry[:, 0:8], 0.0))
        xm, xn, aa, bb, gg = (self.ck[n] for n in ("xm", "xn", "aa", "bb", "gg"))
        slot_rel = list(self.wslot_rel)
        slot_sem = (self.dG, dW1)

        def sv(c, q0):
            return self.stream[c][:, q0:q0 + 512]

        dBC = {}
        dBC[0] = self.stats512(lambda c: sv(c, 0), [(sDVE, 1)], rms=True, bsel=0)
        d_h_prev = 0
        for cq in range(NCH):
            q0 = cq * 512
            bs = cq % 2
            worder = ((self.d_cwog, 0), (self.d_cwx, 1), (self.d_cwA, 0),
                      (self.d_cwsg, 1), (self.d_cwo, 0))
            wdma = []
            for wsrc, sl in worder[:2]:
                wdma.append(self.wload3("gpsimd", "dG" if sl == 0 else "dW1",
                                        self.wslot[sl], 0, wsrc, D, waits=[slot_rel[sl]]))
            d_xm = self.apply512(lambda c, q0=q0: sv(c, q0),
                                 lambda c: xm[:, c * 512:(c + 1) * 512], q0, bsel=bs)
            d_xn = 0
            for c in range(NCH):
                d_xn = self.dve(lambda e, c=c, bs=bs: e.scalar_tensor_tensor(
                    xn[:, c * 512:(c + 1) * 512], xm[:, c * 512:(c + 1) * 512],
                    self.cv[:, c:c + 1], self.ainv_t[:, bs * 512:bs * 512 + 512],
                    ALU.mult, ALU.mult),
                    waits=[(dS, ds)] if c == 0 else ())

            def proj(widx, wsl, dgv, act, dst, bias=None):
                act_snap = self.sACT.n
                lastd = 0
                for j in range(NCH):
                    psum = self.ps_w0 if j % 2 == 0 else self.ps_w1
                    rel = self.rel_w0 if j % 2 == 0 else self.rel_w1
                    p = 0
                    for c in range(NCH):
                        p = self.pe(lambda e, j=j, c=c, psum=psum, wsl=wsl: e.matmul(
                            psum[:, 0:512],
                            self.wslot[wsl][:, c * D + j * 128: c * D + (j + 1) * 128],
                            xn[:, c * 512:(c + 1) * 512],
                            start=(c == 0), stop=(c == NCH - 1)),
                            waits=[(sDVE, d_xn), (slot_sem[wsl], dgv), rel] if c == 0 else ())
                    if act == "sig":
                        lastd = self.act(lambda e, j=j, psum=psum: e.activation(
                            dst(j), psum[:, 0:512], AF.Sigmoid,
                            bias=(self.cv[:, 8 + j:9 + j] if bias else 0.0)),
                            waits=[(sPE, p), (sDVE, d_h_prev)])
                        relv = (sACT, lastd)
                    else:
                        lastd = self.dve(lambda e, j=j, psum=psum: e.tensor_mul(
                            dst(j), psum[:, 0:512], bb[:, j * 512:(j + 1) * 512]),
                            waits=[(sPE, p), (sACT, act_snap)])
                        relv = (sDVE, lastd)
                    if j % 2 == 0:
                        self.rel_w0 = relv
                    else:
                        self.rel_w1 = relv
                return lastd

            a_sg = proj(0, 0, wdma[0], "sig", lambda j: bb[:, j * 512:(j + 1) * 512])
            if cq + 1 < NCH:
                dBC[cq + 1] = self.stats512(lambda c, q1=q0 + 512: sv(c, q1),
                                            [(sDVE, 1)], rms=True, bsel=(cq + 1) % 2)
            wdma.append(self.wload3("gpsimd", "dG", self.wslot[0], 0, worder[2][0], D,
                                    waits=[self.rel_w0, self.rel_w1]))
            d_b = proj(1, 1, wdma[1], "mulsg", lambda j: bb[:, j * 512:(j + 1) * 512])
            wdma.append(self.wload3("gpsimd", "dW1", self.wslot[1], 0, worder[3][0], D,
                                    waits=[self.rel_w0, self.rel_w1]))
            a_A = proj(2, 0, wdma[2], "sig", lambda j: aa[:, j * 512:(j + 1) * 512], bias=True)
            wdma.append(self.wload3("gpsimd", "dG", self.wslot[0], 0, worder[4][0], D,
                                    waits=[self.rel_w0, self.rel_w1]))
            a_gate = proj(3, 1, wdma[3], "sig", lambda j: gg[:, j * 512:(j + 1) * 512])
            pe_xn_done = self.sPE.n
            d_h = 0
            for c in range(NCH):
                d_h = self.dve(lambda e, c=c: e.tensor_tensor_scan(
                    xn[:, c * 512:(c + 1) * 512], aa[:, c * 512:(c + 1) * 512],
                    bb[:, c * 512:(c + 1) * 512], self.carry[:, c:c + 1],
                    ALU.mult, ALU.add),
                    waits=([(sACT, a_A), (sDVE, d_b), (sPE, pe_xn_done)] if c == 0 else ()))
                d_h = self.dve(lambda e, c=c: e.tensor_copy(
                    self.carry[:, c:c + 1], xn[:, c * 512 + 511:c * 512 + 512]))
            d_h_prev = d_h
            d_y = 0
            for c in range(NCH):
                d_y = self.dve(lambda e, c=c: e.tensor_mul(
                    gg[:, c * 512:(c + 1) * 512], gg[:, c * 512:(c + 1) * 512],
                    xn[:, c * 512:(c + 1) * 512]), waits=[(sACT, a_gate)] if c == 0 else ())
            d_oc = 0
            for j in range(NCH):
                psum = self.ps_w0 if j % 2 == 0 else self.ps_w1
                rel = self.rel_w0 if j % 2 == 0 else self.rel_w1
                p = 0
                for c in range(NCH):
                    p = self.pe(lambda e, j=j, c=c, psum=psum: e.matmul(
                        psum[:, 0:512],
                        self.wslot[0][:, c * D + j * 128: c * D + (j + 1) * 128],
                        gg[:, c * 512:(c + 1) * 512],
                        start=(c == 0), stop=(c == NCH - 1)),
                        waits=[(sDVE, d_y), (dG, wdma[4]), rel] if c == 0 else ())
                d_oc = self.dve(lambda e, j=j, psum=psum: e.tensor_add(
                    xm[:, j * 512:(j + 1) * 512], psum[:, 0:512],
                    xm[:, j * 512:(j + 1) * 512]), waits=[(sPE, p)])
                if j % 2 == 0:
                    self.rel_w0 = (sDVE, d_oc)
                else:
                    self.rel_w1 = (sDVE, d_oc)
            slot_rel = [self.rel_w0, self.rel_w1]
            dBo = self.stats512(lambda c: xm[:, c * 512:(c + 1) * 512], [(sDVE, d_oc)], bsel=bs)
            self.close512(lambda c: xm[:, c * 512:(c + 1) * 512], q0,
                          extra_ready=[(sDVE, dBo)], bsel=bs)
        self.wslot_rel = [(sPE, self.sPE.n), (sPE, self.sPE.n)]

    def phaseD(self):
        sPE, sACT, sDVE, sPOOL, dG, dS = self.sPE, self.sACT, self.sDVE, self.sPOOL, self.dG, self.dS
        dW1 = self.sems["dW1"]
        ds = self.dma("sync", "dS", lambda e: e.dma_start(
            out=self.cv[:, 0:8], in_=self.d_mur.rearrange("c p -> p c"),
            allow_slow_non_contiguous=True))
        ds = self.dma("sync", "dS", lambda e: e.dma_start(
            out=self.cv[:, 8:16], in_=self.d_muk.rearrange("c p -> p c"),
            allow_slow_non_contiguous=True))
        d_mu = 0
        for h, (src_c, dst_c) in enumerate(((0, 0), (8, 8))):
            d_mu = self.dve(lambda e, src_c=src_c, dst_c=dst_c: e.tensor_scalar(
                self.svec[:, dst_c:dst_c + 8], self.cv[:, src_c:src_c + 8], -1.0, 1.0,
                ALU.mult, ALU.add), waits=[(dS, ds)] if h == 0 else ())
        d_cd = self.dve(lambda e: e.memset(self.carry[:, 8:16], 0.0))
        xms, xr, xk, rr, ka = self.ck_xms, self.ck["aa"], self.ck["bb"], self.ck["gg"], self.ck["xn"]
        od = self.ck["xm"]
        slot_rel = list(self.wslot_rel)

        def sv(c, q0):
            return self.stream[c][:, q0:q0 + 512]

        dBD = {}
        dBD[0] = self.stats512(lambda c: sv(c, 0), [(sDVE, d_mu)], bsel=0)
        for cq in range(NCH):
            q0 = cq * 512
            bs = cq % 2
            wdma_r = self.wload3("gpsimd", "dG", self.wslot[0], 0, self.d_dwrT, D,
                                 waits=[slot_rel[0]])
            wdma_k = self.wload3("gpsimd", "dW1", self.wslot[1], 0, self.d_dwkT, D,
                                 waits=[slot_rel[1]])
            d_sh = 0
            for c in range(NCH):
                d_sh = self.dve(lambda e, c=c: e.tensor_copy(
                    xms[:, c * 513:c * 513 + 1], self.carry[:, 8 + c:9 + c]),
                    waits=[(sDVE, d_cd)] if (cq == 0 and c == 0) else ())
            d_xm = self.apply512(lambda c, q0=q0: sv(c, q0),
                                 lambda c: xms[:, c * 513 + 1:c * 513 + 513], q0, bsel=bs)
            for c in range(NCH):
                d_xm = self.dve(lambda e, c=c: e.tensor_copy(
                    self.carry[:, 8 + c:9 + c], xms[:, c * 513 + 512:c * 513 + 513]))
            if cq + 1 < NCH:
                dBD[cq + 1] = self.stats512(lambda c, q1=q0 + 512: sv(c, q1),
                                            [(sDVE, 1)], bsel=(cq + 1) % 2)
            d_mix = 0
            for c in range(NCH):
                cur = xms[:, c * 513 + 1:c * 513 + 513]
                prv = xms[:, c * 513:c * 513 + 512]
                d1 = self.dve(lambda e, c=c, prv=prv: e.tensor_scalar_mul(
                    self.tscr2[:, 0:512], prv, self.svec[:, c:c + 1]))
                d_mix = self.dve(lambda e, c=c, cur=cur: e.scalar_tensor_tensor(
                    xr[:, c * 512:(c + 1) * 512], cur, self.cv[:, c:c + 1],
                    self.tscr2[:, 0:512], ALU.mult, ALU.add))
                d1 = self.dve(lambda e, c=c, prv=prv: e.tensor_scalar_mul(
                    self.tscr2[:, 0:512], prv, self.svec[:, 8 + c:9 + c]))
                d_mix = self.dve(lambda e, c=c, cur=cur: e.scalar_tensor_tensor(
                    xk[:, c * 512:(c + 1) * 512], cur, self.cv[:, 8 + c:9 + c],
                    self.tscr2[:, 0:512], ALU.mult, ALU.add))

            def proj2(wsl, dgv, src2, fin, waits0=()):
                lastd = 0
                for j in range(NCH):
                    psum = self.ps_w0 if j % 2 == 0 else self.ps_w1
                    rel = self.rel_w0 if j % 2 == 0 else self.rel_w1
                    p = 0
                    for c in range(NCH):
                        p = self.pe(lambda e, j=j, c=c, psum=psum, wsl=wsl: e.matmul(
                            psum[:, 0:512],
                            self.wslot[wsl][:, c * D + j * 128: c * D + (j + 1) * 128],
                            src2[:, c * 512:(c + 1) * 512],
                            start=(c == 0), stop=(c == NCH - 1)),
                            waits=(list(waits0) + [(self.dG if wsl == 0 else dW1, dgv), rel])
                            if c == 0 else ())
                    lastd, relv = fin(j, psum, p)
                    if j % 2 == 0:
                        self.rel_w0 = relv
                    else:
                        self.rel_w1 = relv
                return lastd

            def fin_r(j, psum, p):
                a = self.act(lambda e, j=j, psum=psum: e.activation(
                    rr[:, j * 512:(j + 1) * 512], psum[:, 0:512], AF.Sigmoid),
                    waits=[(sPE, p)])
                return a, (sACT, a)

            k_prev_act = [0]

            def fin_k(j, psum, p):
                d1 = self.dve(lambda e, psum=psum: e.tensor_scalar_max(
                    self.tscr2[:, 512:1024], psum[:, 0:512], 0.0),
                    waits=[(sPE, p), (sACT, k_prev_act[0])])
                a = self.act(lambda e, j=j: e.activation(
                    ka[:, j * 512:(j + 1) * 512], self.tscr2[:, 512:1024], AF.Square),
                    waits=[(sDVE, d1)])
                k_prev_act[0] = a
                return a, (sDVE, d1)

            def fin_v(j, psum, p):
                d1 = self.dve(lambda e, j=j, psum=psum: e.tensor_mul(
                    od[:, j * 512:(j + 1) * 512], psum[:, 0:512],
                    rr[:, j * 512:(j + 1) * 512]), waits=[(sPE, p)])
                return d1, (sDVE, d1)

            a_r = proj2(0, wdma_r, xr, fin_r, waits0=[(sDVE, d_mix)])
            a_k = proj2(1, wdma_k, xk, fin_k)
            wdma_v = self.wload3("gpsimd", "dG", self.wslot[0], 0, self.d_dwvT, D,
                                 waits=[self.rel_w0, self.rel_w1])
            d_v = proj2(0, wdma_v, ka, fin_v, waits0=[(sACT, self.sACT.n)])
            slot_rel = [self.rel_w0, self.rel_w1]
            dBo = self.stats512(lambda c: od[:, c * 512:(c + 1) * 512], [(sDVE, d_v)], bsel=bs)
            self.close512(lambda c: od[:, c * 512:(c + 1) * 512], q0,
                          extra_ready=[(sDVE, dBo)], bsel=bs)
        self.wslot_rel = [(sPE, self.sPE.n), (sPE, self.sPE.n)]

    def phase_out(self):
        sPE, sACT, sDVE, sST = self.sPE, self.sACT, self.sDVE, self.sST
        st_after = {}
        st_count = 0
        pe_prev = 0
        for tt in range(32):
            t0 = tt * 128
            ob = tt % 2
            d_c = 0
            for c in range(NCH):
                half = self.tscr if c < 4 else self.tscr2
                d_c = self.dve(lambda e, c=c, t0=t0, half=half: e.tensor_copy(
                    half[:, (c % 4) * 128:(c % 4 + 1) * 128], self.stream[c][:, t0:t0 + 128]),
                    waits=([self.stream_written, (sPE, pe_prev), self.B_rel[0], self.B_rel[1]]
                           if c == 0 else ()))
            p = 0
            for c in range(NCH):
                half = self.tscr if c < 4 else self.tscr2
                psum = self.ps_w0 if c < 4 else self.ps_w1
                p = self.pe(lambda e, c=c, half=half, psum=psum: e.transpose(
                    psum[:, (c % 4) * 128:(c % 4 + 1) * 128],
                    half[:, (c % 4) * 128:(c % 4 + 1) * 128], self.ident_f[:]),
                    waits=([(sDVE, d_c), self.rel_w0, self.rel_w1] if c == 0 else ()))
            pe_prev = p
            a = self.act(lambda e, ob=ob: e.copy(self.outstage[ob][:, 0:512], self.ps_w0[:, 0:512]),
                         waits=[(sPE, p)] + ([(sST, st_after.get(tt - 1, 0))] if tt >= 2 else []))
            a = self.act(lambda e, ob=ob: e.copy(self.outstage[ob][:, 512:1024], self.ps_w1[:, 0:512]))
            self.rel_w0 = (sACT, a)
            self.rel_w1 = (sACT, a)
            st_count = self.dma("sync", "ST", lambda e, t0=t0, ob=ob: e.dma_start(
                out=self.d_out[t0:t0 + 128, :], in_=self.outstage[ob][:]),
                waits=[(sACT, a)])
            st_after[tt] = st_count
        if self.on("sync"):
            self.e.wait_ge(self.sST.h, st_count)

    def build(self):
        blk_cm = self.nc.Block()
        blk = blk_cm.__enter__()
        with self.nc.allow_low_precision(reason="bf16 operands are intentional"):
            for engn in ENGS:
                getattr(blk, engn)(lambda e, _n=engn: self.emit(_n, e))
        blk_cm.__exit__(None, None, None)
        first = self._pass_counts[0]
        for cts in self._pass_counts[1:]:
            assert cts == first, (cts, first)
        return self.nc


NCORES = 8


def _host_prepare(inputs, core, xT, tb2):
    f = np.float32
    r0 = 128 * core
    d = dict(
        xT=xT,
        xTk=xT[core],
        wq=np.ascontiguousarray(inputs["L0_attn_Wq"][r0:r0 + 128, :].T.reshape(NCH, 128, 128)),
        wk=np.ascontiguousarray(inputs["L0_attn_Wk"][r0:r0 + 128, :].T.reshape(NCH, 128, 128)),
        wv=np.ascontiguousarray(inputs["L0_attn_Wv"][r0:r0 + 128, :].T.reshape(NCH, 128, 128)),
        bqkv=np.concatenate([inputs["L0_attn_bq"][r0:r0 + 128],
                             inputs["L0_attn_bk"][r0:r0 + 128],
                             inputs["L0_attn_bv"][r0:r0 + 128]]).reshape(1, 384).astype(f),
        woT=np.ascontiguousarray(inputs["L0_attn_Wo"].T.reshape(NCH, 128, D)),
        bo=inputs["L0_attn_bo"].reshape(NCH, 128).astype(f),
        tb2=tb2.astype(f),
        etis=np.array([[inputs["L0_attn_et_gate"].reshape(()),
                        inputs["L0_attn_is_gate"].reshape(())]], dtype=f),
        eye=np.eye(128, dtype=f),
        fwin=np.ascontiguousarray(inputs["L0_ffn_Win"].reshape(NCH, 128, 64)),
        fbin=inputs["L0_ffn_bin_"].reshape(1, 64).astype(f),
        fw1=inputs["L0_ffn_W1"].astype(f), fb1=inputs["L0_ffn_b1"].astype(f),
        fw2=inputs["L0_ffn_W2"].astype(f),
        fb2=inputs["L0_ffn_b2"].reshape(1, 64).astype(f),
        fwout=inputs["L0_ffn_Wout"].astype(f),
        fbout=inputs["L0_ffn_bout"].reshape(NCH, 128).astype(f),
        crmsw=inputs["L1_attn_rms_weight"].reshape(NCH, 128).astype(f),
        cwA=np.ascontiguousarray(inputs["L1_attn_w_A"].reshape(NCH, 128, D)),
        cbA=inputs["L1_attn_b_A"].reshape(NCH, 128).astype(f),
        cwx=np.ascontiguousarray(inputs["L1_attn_w_x"].reshape(NCH, 128, D)),
        cwsg=np.ascontiguousarray(inputs["L1_attn_w_sidegate"].reshape(NCH, 128, D)),
        cwog=np.ascontiguousarray(inputs["L1_attn_w_output"].reshape(NCH, 128, D)),
        cwo=np.ascontiguousarray(inputs["L1_attn_w_o"].reshape(NCH, 128, D)),
        dwrT=np.ascontiguousarray(inputs["L1_ffn_W_r"].T.reshape(NCH, 128, D)),
        dwkT=np.ascontiguousarray(inputs["L1_ffn_W_k"].T.reshape(NCH, 128, D)),
        dwvT=np.ascontiguousarray(inputs["L1_ffn_W_v"].T.reshape(NCH, 128, D)),
        mur=inputs["L1_ffn_mu_r"].reshape(NCH, 128).astype(f),
        muk=inputs["L1_ffn_mu_k"].reshape(NCH, 128).astype(f),
    )
    return d


def kernel(**inputs):
    x = np.asarray(inputs["L0_attn_x"], dtype=np.float32)
    xT = np.ascontiguousarray(x.transpose(0, 2, 1).reshape(B, NCH, 128, T))
    tb = np.asarray(inputs["L0_attn_time_bias"], dtype=np.float32).reshape(-1)
    idx = np.clip(C0 + np.arange(128)[:, None] - np.arange(WB)[None, :], 0, 499)
    tb2 = np.ascontiguousarray(tb[idx])
    bld = Builder()
    nc = bld.build()
    in_maps = [_host_prepare(inputs, k, xT, tb2) for k in range(NCORES)]
    res = run_bass_kernel_spmd(nc, in_maps, core_ids=list(range(NCORES)))
    out = np.stack([np.asarray(res.results[k]["out"]) for k in range(NCORES)], axis=0)
    return out


# version 21
# speedup vs baseline: 1.0267x; 1.0014x over previous
import numpy as np
import ml_dtypes
import concourse.bass as bass
import concourse.mybir as mybir
from concourse.bass_utils import run_bass_kernel_spmd

F32 = mybir.dt.float32
BF16 = mybir.dt.bfloat16
AF = mybir.ActivationFunctionType
ALU = mybir.AluOpType
AX = mybir.AxisListType
AP = bass.AP

B = 8
T = 4096
D = 1024
NCH = 8
NT = 8
NKT = 32
WB = 8064
C0 = 4218
EPS = 1e-5
RD = 1.0 / D

ENGS = ("sync", "tensor", "vector", "scalar", "gpsimd")
UPTO = "full"
CHAIN_SAME_ENGINE = {"DVE": True, "ACT": False, "POOL": False}


def band_class(kt, qc):
    kmin = kt * 128 - (qc * 512 + 511)
    kmax = kt * 128 + 127 - qc * 512
    if kmax <= -250:
        return 1
    if kmin >= 249:
        return 2
    return 0


class Sem:
    __slots__ = ("h", "n", "name")

    def __init__(self, h, name):
        self.h = h
        self.n = 0
        self.name = name


class Builder:
    def __init__(self, upto=UPTO):
        self.upto = upto
        nc = self.nc = bass.Bass("TRN2", target_bir_lowering=False)

        dt_ = nc.dram_tensor
        self.d_xT = dt_("xT", [B, NCH, 128, T], F32, kind="ExternalInput").ap()
        self.d_xTk = dt_("xTk", [NCH, 128, T], F32, kind="ExternalInput").ap()
        self.d_wq = dt_("wq", [NCH, 128, 128], F32, kind="ExternalInput").ap()
        self.d_wk = dt_("wk", [NCH, 128, 128], F32, kind="ExternalInput").ap()
        self.d_wv = dt_("wv", [NCH, 128, 128], F32, kind="ExternalInput").ap()
        self.d_bqkv = dt_("bqkv", [1, 384], F32, kind="ExternalInput").ap()
        self.d_woT = dt_("woT", [NCH, 128, D], F32, kind="ExternalInput").ap()
        self.d_bo = dt_("bo", [NCH, 128], F32, kind="ExternalInput").ap()
        self.d_tb2 = dt_("tb2", [128, WB], F32, kind="ExternalInput").ap()
        self.d_etis = dt_("etis", [1, 2], F32, kind="ExternalInput").ap()
        self.d_eye = dt_("eye", [128, 128], F32, kind="ExternalInput").ap()
        self.d_fwin = dt_("fwin", [NCH, 128, 64], F32, kind="ExternalInput").ap()
        self.d_fbin = dt_("fbin", [1, 64], F32, kind="ExternalInput").ap()
        self.d_fw1 = dt_("fw1", [64, 16], F32, kind="ExternalInput").ap()
        self.d_fb1 = dt_("fb1", [64, 16], F32, kind="ExternalInput").ap()
        self.d_fw2 = dt_("fw2", [64, 16], F32, kind="ExternalInput").ap()
        self.d_fb2 = dt_("fb2", [1, 64], F32, kind="ExternalInput").ap()
        self.d_fwout = dt_("fwout", [64, D], F32, kind="ExternalInput").ap()
        self.d_fbout = dt_("fbout", [NCH, 128], F32, kind="ExternalInput").ap()
        self.d_crmsw = dt_("crmsw", [NCH, 128], F32, kind="ExternalInput").ap()
        self.d_cwA = dt_("cwA", [NCH, 128, D], F32, kind="ExternalInput").ap()
        self.d_cbA = dt_("cbA", [NCH, 128], F32, kind="ExternalInput").ap()
        self.d_cwx = dt_("cwx", [NCH, 128, D], F32, kind="ExternalInput").ap()
        self.d_cwsg = dt_("cwsg", [NCH, 128, D], F32, kind="ExternalInput").ap()
        self.d_cwog = dt_("cwog", [NCH, 128, D], F32, kind="ExternalInput").ap()
        self.d_cwo = dt_("cwo", [NCH, 128, D], F32, kind="ExternalInput").ap()
        self.d_dwrT = dt_("dwrT", [NCH, 128, D], F32, kind="ExternalInput").ap()
        self.d_dwkT = dt_("dwkT", [NCH, 128, D], F32, kind="ExternalInput").ap()
        self.d_dwvT = dt_("dwvT", [NCH, 128, D], F32, kind="ExternalInput").ap()
        self.d_mur = dt_("mur", [NCH, 128], F32, kind="ExternalInput").ap()
        self.d_muk = dt_("muk", [NCH, 128], F32, kind="ExternalInput").ap()
        self.d_out = dt_("out", [T, D], F32, kind="ExternalOutput").ap()

        sb = nc.alloc_sbuf_tensor

        def gsb(name, shape, dt):
            guard = nc.sbuf_tensor(name, shape, dt, side="right")
            h = guard.__enter__()
            self._guards[name] = guard
            return h

        self._guards = {}
        self.xsq = sb("xsq", [128, NCH * 512], BF16)
        self.ident_f = sb("ident_f", [128, 128], F32)
        self.ident_bf = sb("ident_bf", [128, 128], BF16)
        self.ones128 = sb("ones128", [128, 128], BF16)
        self.ones_row = sb("ones_row", [1, 512], BF16)
        self.onescol_bf = sb("onescol_bf", [128, 1], BF16)
        self.ones1_f = sb("ones1_f", [1, 128], F32)
        self.onescol_f = sb("onescol_f", [128, 1], F32)
        self.bias_row = sb("bias_row", [1, 384], BF16)
        self.bo_col = sb("bo_col", [128, NCH], F32)
        self.wsum = sb("wsum", [128, 4], F32)
        self.tiny = sb("tiny", [128, 16], F32)
        self.ubc = sb("ubc", [128, 8], F32)
        self.rmB1 = sb("rmB1", [128, 1024], BF16)
        self.rmB2 = sb("rmB2", [128, 1024], BF16)
        self.tscr = sb("tscr", [128, 1024], F32)
        self.tscr2 = sb("tscr2", [128, 1024], F32)
        self.reciprow = sb("reciprow", [1, 1024], BF16)
        self.recb = sb("recb", [128, 512], BF16)
        self.bscr = sb("bscr", [128, 1024], BF16)
        self.colmax = sb("colmax", [128, 8], F32)
        self.svec = sb("svec", [128, 16], F32)
        self.eps_col = sb("eps_col", [128, 1], F32)
        self.scr = gsb("scr", [128, NCH * T], BF16)
        self.qk = gsb("qk", [128, 4 * T], BF16)
        self.x_nt = gsb("x_nt", [128, 2 * NCH * 512], BF16)
        self.vt = gsb("vt", [128, 2 * 32 * 132], BF16)
        self.band = gsb("band", [128, WB], BF16)
        self.pbuf = gsb("pbuf", [128, 2 * 1024], BF16)
        self.ctxAB = gsb("ctxAB", [128, T], BF16)
        self.wsb = gsb("wsb", [128, 3 * NCH * 128], BF16)
        for n in ("wsb", "ctxAB", "pbuf", "band", "vt", "x_nt"):
            self._guards.pop(n).__exit__(None, None, None)
        self.stream = [sb(f"stream{c}", [128, T], BF16) for c in range(NCH)]
        self.wslot = [sb("wslot0", [128, NCH * D], BF16)]
        for n in ("qk", "scr"):
            self._guards.pop(n).__exit__(None, None, None)
        self.wslot.append(sb("wslot1", [128, NCH * D], BF16))
        self.ck = {n: sb("ck_" + n, [128, NCH * 512], BF16)
                   for n in ("xm", "xn", "aa", "bb", "gg")}
        self.ck_xms = sb("ck_xms", [128, NCH * 513], BF16)
        self.ainv_t = sb("ainv_t", [128, 1024], BF16)
        self.outstage = [sb("outs0", [128, 1024], F32), sb("outs1", [128, 1024], F32)]
        self.fmid = sb("fmid", [64, 512], F32)
        self.fmid_bf = sb("fmid_bf", [64, 512 * 4], BF16)
        self.fw12 = sb("fw12", [64, 48], F32)
        self.carry = sb("carry", [128, 24], F32)
        self.cv = sb("cv", [128, 16], F32)

        ps = nc.alloc_psum_tensor
        self.ps_s0 = ps("ps_s0", [128, 1024], F32)
        self.ps_s1 = ps("ps_s1", [128, 1024], F32)
        self.ps_ctxA = ps("ps_ctxA", [65, 512], F32)
        self.ps_ctxB = ps("ps_ctxB", [65, 512], F32)
        _g0 = nc.psum_tensor("ps_w0", [128, 512], F32)
        self.ps_w0 = _g0.__enter__()
        _g1 = nc.psum_tensor("ps_w1", [128, 512], F32)
        self.ps_w1 = _g1.__enter__()
        _g1.__exit__(None, None, None)
        _g0.__exit__(None, None, None)
        self.ps_wpair = ps("ps_wpair", [128, 1024], F32)

        self.sems = {}
        for name in ("dS", "dG", "dW1", "dT", "dX0", "dX1", "PE", "ACT", "DVE", "POOL", "ST"):
            self.sems[name] = Sem(nc.alloc_semaphore(name), name)
        self._pass_counts = []

    def on(self, eng):
        return self.engn == eng

    def pe(self, fn, waits=()):
        s = self.sems["PE"]
        if self.on("tensor"):
            for sem, val in waits:
                if val > 0:
                    self.e.wait_ge(sem.h, val)
            fn(self.e).then_inc(s.h, 1)
        s.n += 1
        return s.n

    def _chained(self, engkey, engname, fn, waits):
        s = self.sems[engkey]
        if self.on(engname):
            for sem, val in waits:
                if val > 0:
                    self.e.wait_ge(sem.h, val)
            if CHAIN_SAME_ENGINE.get(engkey, True) and s.n > 0:
                self.e.wait_ge(s.h, s.n)
            fn(self.e).then_inc(s.h, 1)
        s.n += 1
        return s.n

    def dve(self, fn, waits=()):
        return self._chained("DVE", "vector", fn, waits)

    def act(self, fn, waits=()):
        return self._chained("ACT", "scalar", fn, waits)

    def pool(self, fn, waits=()):
        return self._chained("POOL", "gpsimd", fn, waits)

    def dma(self, eng, semkey, fn, waits=()):
        s = self.sems[semkey]
        if self.on(eng):
            for sem, val in waits:
                if val > 0:
                    self.e.wait_ge(sem.h, val)
            fn(self.e).then_inc(s.h, 16)
        s.n += 16
        return s.n

    def wload3(self, eng, semkey, dst, dst_off, src, wd, nch=NCH, waits=()):
        cols = dst.shape[1]
        return self.dma(eng, semkey, lambda e: e.dma_start(
            out=AP(dst, dst_off, [[cols, 128], [wd, nch], [1, wd]]),
            in_=src.rearrange("c p m -> p c m")), waits=waits)

    def phase_barrier(self):
        for s in self.sems.values():
            if s.n > 0:
                self.e.wait_ge(s.h, s.n)

    def emit(self, engn, eng):
        self.engn = engn
        self.e = eng
        for x in self.sems.values():
            x.n = 0
        S = self.sems
        self.dS, self.dG = S["dS"], S["dG"]
        self.dX = [S["dX0"], S["dX1"]]
        self.sPE, self.sACT, self.sDVE, self.sPOOL = S["PE"], S["ACT"], S["DVE"], S["POOL"]
        self.sST = S["ST"]
        z = (self.sPE, 0)
        self.rel_s0 = z
        self.rel_s1 = z
        self.rel_w0 = z
        self.rel_w1 = z
        self.rel_ctx = z
        self.xbuf_rel = [z, z]
        self.xsq_rel = z
        self.B_rel = [z, z]
        self.ctxAB_rel = z
        self.qkv_ready = z
        self.stream_rel = [z] * NCH
        self.wslot_rel = [z, z]

        self.ubc_act_rel = 0
        self.qkv_done = {}

        self.preamble()
        self.phaseA()
        self.phase_barrier()
        self.phase_wo_close()
        self.phase_barrier()
        if self.upto in ("l0f", "l1a", "full"):
            self.phaseF()
            self.phase_barrier()
        if self.upto in ("l1a", "full"):
            self.phaseC()
            self.phase_barrier()
        if self.upto == "full":
            self.phaseD()
            self.phase_barrier()
        self.phase_out()
        self._pass_counts.append({k: v.n for k, v in S.items()})

    def preamble(self):
        dS, dG, sPE, sACT, sDVE = self.dS, self.dG, self.sPE, self.sACT, self.sDVE

        self.dma("sync", "dS", lambda e: e.dma_start(out=self.ident_f[:], in_=self.d_eye))
        self.dma("sync", "dS", lambda e: e.dma_start(out=self.tiny[0:1, 0:2], in_=self.d_etis))
        d_sync0 = self.dma("sync", "dS", lambda e: e.dma_start(
            out=self.bo_col[:], in_=self.d_bo.rearrange("c p -> p c"),
            allow_slow_non_contiguous=True))
        self.dma("gpsimd", "dG", lambda e: e.dma_start(out=self.band[:], in_=self.d_tb2))
        for j, w in enumerate((self.d_wq, self.d_wk, self.d_wv)):
            self.wload3("gpsimd", "dG", self.wsb, j * 1024, w, 128)
        d_g0 = self.dma("gpsimd", "dG", lambda e: e.dma_start(out=self.bias_row[:], in_=self.d_bqkv))

        self.dve(lambda e: e.memset(self.ones128[:], 1.0))
        self.dve(lambda e: e.memset(self.ones_row[:], 1.0))
        self.dve(lambda e: e.memset(self.eps_col[:], EPS))
        self.dve(lambda e: e.memset(self.onescol_bf[:], 1.0))
        self.dve(lambda e: e.memset(self.ones1_f[:], 1.0))
        self.dve(lambda e: e.memset(self.onescol_f[:], 1.0))
        self.dve(lambda e: e.memset(AP(self.vt, 64, [[2 * 32 * 132, 128], [132, 64]]), 1.0))
        self.dve(lambda e: e.memset(AP(self.vt, 130, [[2 * 32 * 132, 128], [132, 64]]), 1.0))
        d_idbf = self.dve(lambda e: e.tensor_copy(self.ident_bf[:], self.ident_f[:]),
                          waits=[(dS, d_sync0)])
        a_sig = self.act(lambda e: e.activation(self.tiny[0:1, 3:4], self.tiny[0:1, 0:1], AF.Sigmoid),
                         waits=[(dS, d_sync0)])
        p = 0
        for o in range(3):
            for c in range(NCH):
                p = self.pe(lambda e, o=o, c=c: e.matmul(
                    self.ps_w0[:, o:o + 1],
                    self.wsb[:, o * 1024 + c * 128: o * 1024 + (c + 1) * 128],
                    self.onescol_bf[:, 0:1], start=(c == 0), stop=(c == NCH - 1)),
                    waits=([(dG, d_g0), (sDVE, d_idbf)] if (o == 0 and c == 0) else ()))
        a_ws = self.act(lambda e: e.copy(self.wsum[:, 0:3], self.ps_w0[:, 0:3]), waits=[(sPE, p)])
        self.dve(lambda e: e.tensor_scalar_mul(self.tiny[0:1, 4:5], self.tiny[0:1, 3:4], 8.0),
                 waits=[(sACT, a_sig)])
        d2 = self.dve(lambda e: e.tensor_scalar_mul(self.tiny[0:1, 5:6], self.tiny[0:1, 1:2], 8.0))
        p = self.pe(lambda e: e.matmul(self.ps_w1[:, 0:1], self.ones1_f[0:1, :], self.tiny[0:1, 4:5],
                                       start=True, stop=True), waits=[(sDVE, d2)])
        p = self.pe(lambda e: e.matmul(self.ps_w1[:, 1:2], self.ones1_f[0:1, :], self.tiny[0:1, 5:6],
                                       start=True, stop=True))
        a_sc = self.act(lambda e: e.copy(self.tiny[:, 6:8], self.ps_w1[:, 0:2]), waits=[(sPE, p)])
        d_band = self.dve(lambda e: e.tensor_scalar(self.band[:], self.band[:], self.tiny[:, 6:7],
                                                    self.tiny[:, 7:8], ALU.mult, ALU.add),
                          waits=[(sACT, a_sc), (dG, d_g0)])
        self.dve(lambda e: e.reduce_max(self.colmax[:, 2:3], self.band[:], axis=AX.X))
        d3 = self.dve(lambda e: e.tensor_copy(self.tscr[:, 0:1], self.colmax[:, 2:3]))
        p2 = self.pe(lambda e: e.transpose(self.ps_w1[0:1, 0:128], self.tscr[:, 0:1], self.ident_f[:]),
                     waits=[(sDVE, d3), (sACT, a_sc)])
        self.dve(lambda e: e.reduce_max(self.tiny[0:1, 8:9], self.ps_w1[0:1, 0:128], axis=AX.X),
                 waits=[(sPE, p2)])
        self.dve(lambda e: e.tensor_scalar_mul(self.tiny[0:1, 9:10], self.tiny[0:1, 8:9], 0.125))
        self.dve(lambda e: e.tensor_scalar_mul(self.tiny[0:1, 10:11], self.band[0:1, WB - 1:WB], 0.125))
        d_pre = self.dve(lambda e: e.tensor_scalar_mul(self.tiny[0:1, 11:12], self.band[0:1, 0:1], 0.125))
        self.band_ready = (sDVE, d_band)
        self.pre_done = (sDVE, d_pre)
        self.rel_w0 = (sACT, a_ws)
        self.rel_w1 = (sDVE, d_pre)

    def stats512(self, src, ready, rms=False, bsel=0, wp=False, wide_src=None):
        sPE, sACT, sDVE, sPOOL = self.sPE, self.sACT, self.sDVE, self.sPOOL
        h0 = bsel * 512
        B1 = self.rmB1[:, h0:h0 + 512]
        B2 = self.rmB2[:, h0:h0 + 512]
        if wp:
            pA, pB = self.ps_w0[:, 0:512], self.ps_w1[:, 0:512]
            relw = [self.rel_w0, self.rel_w1]
        else:
            pA, pB = self.ps_s0[:, 0:512], self.ps_s0[:, 512:1024]
            relw = [self.rel_s0]
        pl = 0
        if wide_src is not None:
            pl = self.pool(lambda e: e.tensor_mul(self.xsq[:, 0:4096], wide_src(), wide_src()),
                           waits=list(ready) + [self.xsq_rel])
        else:
            for c in range(NCH):
                pl = self.pool(lambda e, c=c: e.tensor_mul(
                    self.xsq[:, c * 512:(c + 1) * 512], src(c), src(c)),
                    waits=(list(ready) + [self.xsq_rel]) if c == 0 else ())
        pe_c = 0
        for c in range(NCH):
            pe_c = self.pe(lambda e, c=c, pA=pA: e.matmul(pA, self.ones128[:], src(c),
                                                          start=(c == 0), stop=(c == NCH - 1)),
                           waits=(list(ready) + relw) if c == 0 else ())
        for c in range(NCH):
            pe_c = self.pe(lambda e, c=c, pB=pB: e.matmul(pB, self.ones128[:],
                                                          self.xsq[:, c * 512:(c + 1) * 512],
                                                          start=(c == 0), stop=(c == NCH - 1)),
                           waits=[(sPOOL, pl)] if c == 0 else ())
        self.xsq_rel = (sPE, pe_c)
        pool_snap = self.sPOOL.n
        mS = self.tscr[:, h0:h0 + 512]
        d = self.dve(lambda e, pA=pA: e.tensor_scalar_mul(mS, pA, RD),
                     waits=[(sPE, pe_c), self.B_rel[bsel], (self.sPOOL, pool_snap)])
        d = self.dve(lambda e: e.tensor_mul(B2, mS, mS))
        vdst = self.tscr2[:, 512:1024] if rms else B2
        d_v = self.dve(lambda e, vdst=vdst, pB=pB: e.scalar_tensor_tensor(
            vdst, pB, RD, B2, ALU.mult, ALU.subtract))
        if wp:
            self.rel_w0 = (sDVE, d_v)
            self.rel_w1 = (sDVE, d_v)
        else:
            self.rel_s0 = (sDVE, d_v)
        a = self.act(lambda e, vdst=vdst: e.activation(B1, vdst, AF.Ln,
                                                       bias=self.eps_col[:, 0:1]),
                     waits=[(sDVE, d_v)])
        a = self.act(lambda e: e.activation(B2, B1, AF.Exp, scale=-0.5))
        dB = self.dve(lambda e: e.scalar_tensor_tensor(B1, mS, -1.0, B2, ALU.mult, ALU.mult),
                      waits=[(sACT, a)])
        if rms:
            d = self.dve(lambda e: e.tensor_mul(self.tscr[:, 512:1024], B2, B2))
            d = self.dve(lambda e: e.tensor_mul(self.tscr[:, 512:1024], self.tscr[:, 512:1024],
                                                vdst))
            a = self.act(lambda e: e.activation(self.tscr2[:, 512:1024], self.tscr[:, 512:1024],
                                                AF.Ln, bias=self.eps_col[:, 0:1]), waits=[(sDVE, d)])
            a = self.act(lambda e: e.activation(self.ainv_t[:, h0:h0 + 512],
                                                self.tscr2[:, 512:1024], AF.Exp, scale=-0.5))
            dB = self.dve(lambda e: e.engine_nop(), waits=[(sACT, a)])
        return dB

    def qkp(self, b):
        return (b % 2) * 2 * T

    def vtp(self, b):
        return (b % 2) * 32 * 132

    def _qkv_xv(self, c, nt):
        xoff = (nt % 2) * 4096
        return self.x_nt[:, xoff + c * 512: xoff + (c + 1) * 512]

    def _qkv_dma(self, b, nt):
        buf = nt % 2
        xoff = buf * 4096
        self.dgA[nt] = self.dma("gpsimd", "dX0" if buf == 0 else "dX1",
                                lambda e, b=b, nt=nt, xoff=xoff: e.dma_start(
            out=AP(self.x_nt, xoff, [[2 * NCH * 512, 128], [512, NCH], [1, 512]]),
            in_=self.d_xT[b, :, :, nt * 512:nt * 512 + 512].rearrange("c p t -> p c t")),
            waits=[self.xbuf_rel[buf]])

    def _qkv_stats_a(self, b, nt):
        sPE, sPOOL = self.sPE, self.sPOOL
        buf = nt % 2
        xoff = buf * 4096
        ready = [(self.dX[buf], self.dgA[nt])]
        pl = self.pool(lambda e, xoff=xoff: e.tensor_mul(
            self.xsq[:, 0:4096], self.x_nt[:, xoff:xoff + 4096],
            self.x_nt[:, xoff:xoff + 4096]), waits=ready + [self.xsq_rel])
        pe_c = 0
        for c in range(NCH):
            pe_c = self.pe(lambda e, c=c: e.matmul(
                self.ps_w0[:, 0:512], self.ones128[:], self._qkv_xv(c, nt),
                start=(c == 0), stop=(c == NCH - 1)),
                waits=(ready + [self.rel_w0, self.rel_w1]) if c == 0 else ())
        self._st_pl = pl
        self._st_pe = pe_c

    def _qkv_stats_b(self, b, nt):
        sPE, sACT, sDVE, sPOOL = self.sPE, self.sACT, self.sDVE, self.sPOOL
        buf = nt % 2
        h0 = buf * 512
        B1 = self.rmB1[:, h0:h0 + 512]
        B2 = self.rmB2[:, h0:h0 + 512]
        pe_c = 0
        for c in range(NCH):
            pe_c = self.pe(lambda e, c=c: e.matmul(
                self.ps_w1[:, 0:512], self.ones128[:], self.xsq[:, c * 512:(c + 1) * 512],
                start=(c == 0), stop=(c == NCH - 1)),
                waits=[(sPOOL, self._st_pl)] if c == 0 else ())
        self.xsq_rel = (sPE, pe_c)
        mS = self.tscr[:, h0:h0 + 512]
        d = self.dve(lambda e: e.tensor_scalar_mul(mS, self.ps_w0[:, 0:512], RD),
                     waits=[(sPE, pe_c), self.B_rel[buf]])
        d = self.dve(lambda e: e.tensor_mul(B2, mS, mS))
        d_v = self.dve(lambda e: e.scalar_tensor_tensor(
            B2, self.ps_w1[:, 0:512], RD, B2, ALU.mult, ALU.subtract))
        self.rel_w0 = (sDVE, d_v)
        self.rel_w1 = (sDVE, d_v)
        a = self.act(lambda e: e.activation(B1, B2, AF.Ln, bias=self.eps_col[:, 0:1]),
                     waits=[(sDVE, d_v)])
        a = self.act(lambda e: e.activation(B2, B1, AF.Exp, scale=-0.5))
        self.dBA[nt] = self.dve(lambda e: e.scalar_tensor_tensor(
            B1, mS, -1.0, B2, ALU.mult, ALU.mult), waits=[(sACT, a)])

    def _qkv_proj_a(self, b, nt, o):
        sPE, sDVE = self.sPE, self.sDVE
        psum, rel = ((self.ps_w0, self.rel_w0) if o != 1 else (self.ps_w1, self.rel_w1))
        pq = 0
        for c in range(4):
            pq = self.pe(lambda e, o=o, c=c, nt=nt, psum=psum: e.matmul(
                psum[:, 0:512],
                self.wsb[:, o * 1024 + c * 128: o * 1024 + (c + 1) * 128],
                self._qkv_xv(c, nt), start=(c == 0), stop=False),
                waits=[rel] if c == 0 else ())

    def _qkv_proj_b(self, b, nt, o):
        sPE, sDVE = self.sPE, self.sDVE
        buf = nt % 2
        h0 = buf * 512
        q0 = nt * 512
        qo = self.qkp(b)
        dests = (self.qk[:, qo + q0:qo + q0 + 512],
                 self.qk[:, qo + T + q0:qo + T + q0 + 512], None)
        psum = self.ps_w0 if o != 1 else self.ps_w1
        pq = 0
        for c in range(4, NCH):
            pq = self.pe(lambda e, o=o, c=c, nt=nt, psum=psum: e.matmul(
                psum[:, 0:512],
                self.wsb[:, o * 1024 + c * 128: o * 1024 + (c + 1) * 128],
                self._qkv_xv(c, nt), start=False, stop=False))
        pq = self.pe(lambda e, o=o, psum=psum: e.matmul(
            psum[:, 0:512], self.bias_row[0:1, o * 128:(o + 1) * 128],
            self.ones_row[0:1, 0:512], start=False, stop=True))
        d_u = self.dve(lambda e, psum=psum, h0=h0: e.tensor_mul(
            self.tscr2[:, 512:1024], psum[:, 0:512], self.rmB2[:, h0:h0 + 512]),
            waits=[(sPE, pq), (sDVE, self.dBA[nt])])
        dst = dests[o] if o != 2 else self.tscr2[:, 0:512]
        d_o = self.dve(lambda e, o=o, dst=dst, h0=h0: e.scalar_tensor_tensor(
            dst, self.rmB1[:, h0:h0 + 512], self.wsum[:, o:o + 1],
            self.tscr2[:, 512:1024], ALU.mult, ALU.add))
        self.fixA.setdefault(nt, {})[o] = d_o
        if o != 1:
            self.rel_w0 = (sDVE, d_u)
        else:
            self.rel_w1 = (sDVE, d_u)
        if o == 2:
            self.B_rel[buf] = (sDVE, d_o)
            self.xbuf_rel[buf] = (sPE, pq)

    def _qkv_vt(self, b, nt):
        sPE, sDVE = self.sPE, self.sDVE
        vo = self.vtp(b)
        pt = 0
        for i in range(4):
            pt = self.pe(lambda e, i=i: e.transpose(
                self.ps_w1[:, i * 128:(i + 1) * 128], self.tscr2[:, i * 128:(i + 1) * 128],
                self.ident_f[:]),
                waits=[(sDVE, self.fixA[nt][2]), self.rel_w1] if i == 0 else ())
        d_vt = self.dve(lambda e, nt=nt, vo=vo: e.tensor_copy(
            AP(self.vt, vo + nt * 4 * 132, [[2 * 32 * 132, 128], [132, 4], [66, 2], [1, 64]]),
            AP(self.ps_w1, 0, [[512, 128], [128, 4], [64, 2], [1, 64]])),
            waits=[(sPE, pt)])
        self.rel_w1 = (sDVE, d_vt)
        self.vt_rel = (sDVE, d_vt)

    def make_qkv_thunks(self, b, stage):
        th = []
        if stage == 0:
            p = b % 2
            th.append(lambda: self._qkv_dma(b, 0))
            th.append(lambda: self.dve(
                lambda e, p=p: e.memset(self.colmax[:, 4 * p:4 * p + 2], 0.0)))
            th.append(lambda: self._qkv_stats_a(b, 0))
            th.append(lambda: self._qkv_stats_b(b, 0))
            th.append(lambda: self._qkv_dma(b, 1))
            return th
        nt = stage - 1
        if stage < NT:
            th.append(lambda: self._qkv_stats_a(b, stage))
            th.append(lambda: self._qkv_stats_b(b, stage))
        for o in range(3):
            th.append(lambda o=o: self._qkv_proj_a(b, nt, o))
            th.append(lambda o=o: self._qkv_proj_b(b, nt, o))
        th.append(lambda: self._qkv_vt(b, nt))
        if stage + 1 < NT:
            th.append(lambda: self._qkv_dma(b, stage + 1))
        return th

    def _ub_norms(self, b, j):
        sPE, sACT, sDVE, sPOOL = self.sPE, self.sACT, self.sDVE, self.sPOOL
        p = b % 2
        qo = self.qkp(b)
        cm = 4 * p
        off = j * T
        pl = self.pool(lambda e, qo=qo, off=off: e.tensor_mul(
            self.xsq[:, 0:4096], self.qk[:, qo + off:qo + off + 4096],
            self.qk[:, qo + off:qo + off + 4096]),
            waits=[(sDVE, self.fixA[7][j]), self.xsq_rel])
        pn = 0
        for r in range(4):
            pn = self.pe(lambda e, r=r: e.matmul(
                self.ps_w0[0:1, 0:512], self.onescol_bf[:, 0:1],
                self.xsq[:, (2 * r) * 512:(2 * r + 1) * 512], start=True, stop=True),
                waits=[(sPOOL, pl), self.rel_w0] if r == 0 else [self.rel_w0])
            pn = self.pe(lambda e, r=r: e.matmul(
                self.ps_w1[0:1, 0:512], self.onescol_bf[:, 0:1],
                self.xsq[:, (2 * r + 1) * 512:(2 * r + 2) * 512], start=True, stop=True),
                waits=[self.rel_w1])
            dq = self.dve(lambda e, cm=cm: e.reduce_max(
                self.colmax[0:1, cm + 2:cm + 3],
                AP(self.ps_wpair, 0, [[1024, 1], [1, 1024]]), axis=AX.X),
                waits=[(sPE, pn)])
            self.dve(lambda e, j=j, cm=cm: e.tensor_max(
                self.colmax[0:1, cm + j:cm + j + 1], self.colmax[0:1, cm + j:cm + j + 1],
                self.colmax[0:1, cm + 2:cm + 3]))
            self.rel_w0 = (sDVE, dq)
            self.rel_w1 = (sDVE, dq)
        self.xsq_rel = (sPE, pn)

    def emit_ub(self, b):
        sPE, sACT, sDVE, sPOOL = self.sPE, self.sACT, self.sDVE, self.sPOOL
        p = b % 2
        cm = 4 * p
        d_k_last = self.sDVE.n
        d = self.dve(lambda e, cm=cm: e.tensor_mul(self.tiny[0:1, 14:15],
                                                   self.colmax[0:1, cm:cm + 1],
                                                   self.colmax[0:1, cm + 1:cm + 2]),
                     waits=[(sDVE, d_k_last), (sACT, self.ubc_act_rel)])
        a = self.act(lambda e: e.activation(self.tiny[0:1, 15:16], self.tiny[0:1, 14:15], AF.Ln),
                     waits=[(sDVE, d)])
        a = self.act(lambda e: e.activation(self.tiny[0:1, 14:15], self.tiny[0:1, 15:16], AF.Exp,
                                            scale=0.5))
        d = self.dve(lambda e: e.scalar_tensor_tensor(self.tiny[0:1, 15:16], self.tiny[0:1, 14:15],
                                                      0.125, self.tiny[0:1, 9:10], ALU.mult, ALU.add),
                     waits=[(sACT, a)])
        d = self.dve(lambda e: e.tensor_scalar_mul(self.tiny[0:1, 12:13], self.tiny[0:1, 15:16], -1.0))
        d = self.dve(lambda e: e.tensor_sub(self.tiny[0:1, 13:14], self.tiny[0:1, 10:11],
                                            self.tiny[0:1, 15:16]))
        d = self.dve(lambda e: e.tensor_sub(self.tiny[0:1, 14:15], self.tiny[0:1, 11:12],
                                            self.tiny[0:1, 15:16]))
        pb = self.pe(lambda e: e.matmul(self.ps_w0[:, 2:3], self.ones1_f[0:1, :],
                                        self.tiny[0:1, 12:13], start=True, stop=True),
                     waits=[(sDVE, d), self.rel_w0])
        pb = self.pe(lambda e: e.matmul(self.ps_w0[:, 3:4], self.ones1_f[0:1, :],
                                        self.tiny[0:1, 13:14], start=True, stop=True))
        pb = self.pe(lambda e: e.matmul(self.ps_w0[:, 4:5], self.ones1_f[0:1, :],
                                        self.tiny[0:1, 14:15], start=True, stop=True))
        a_ub = self.act(lambda e, p=p: e.copy(self.ubc[:, 4 * p:4 * p + 3], self.ps_w0[:, 2:5]),
                        waits=[(sPE, pb)])
        self.rel_w0 = (sACT, a_ub)
        self.ubc_act_rel = a_ub
        self.qkv_done[b] = [(sACT, a_ub), self.band_ready, self.vt_rel]

    def ub_thunks(self, b):
        return [lambda: self._ub_norms(b, 0), lambda: self._ub_norms(b, 1),
                lambda: self.emit_ub(b)]

    def emit_attn_qc(self, b, qc, thunks=()):
        thunks = list(thunks)
        if thunks:
            step = max(1, (NKT - 3) // len(thunks))
            points = set(2 + i * step for i in range(len(thunks)))
        else:
            points = set()
        sPE, sACT, sDVE, sPOOL = self.sPE, self.sACT, self.sDVE, self.sPOOL
        p = b % 2
        qo = self.qkp(b)
        vo = self.vtp(b)
        cm = 4 * p
        slots = (self.ps_s0, self.ps_s1)
        q0 = qc * 512
        pv_done = 0
        exp_of = {}

        def emit_pv(kt):
            nonlocal pv_done
            poff = (kt % 2) * 1024
            w = [(sACT, exp_of[kt])]
            if kt == 0:
                w.append(self.rel_ctx)
            self.pe(lambda e, kt=kt, poff=poff, vo=vo: e.matmul(
                self.ps_ctxA[:, :], self.vt[:, vo + kt * 132:vo + kt * 132 + 65],
                self.pbuf[:, poff:poff + 512], start=(kt == 0), stop=(kt == NKT - 1)), waits=w)
            pv_done = self.pe(lambda e, kt=kt, poff=poff, vo=vo: e.matmul(
                self.ps_ctxB[:, :], self.vt[:, vo + kt * 132 + 66:vo + kt * 132 + 131],
                self.pbuf[:, poff + 512:poff + 1024], start=(kt == 0), stop=(kt == NKT - 1)))

        for kt in range(NKT):
            sl = kt % 2
            pss = slots[sl]
            poff = sl * 1024
            cls = band_class(kt, qc)
            w = list(self.slot_rel[sl])
            if qc == 0 and kt < 2:
                w += self.qkv_done[b]
            self.pe(lambda e, pss=pss, kt=kt, q0=q0, cls=cls, qo=qo: e.matmul(
                pss[:, 0:512], self.qk[0:64, qo + T + kt * 128:qo + T + (kt + 1) * 128],
                self.qk[0:64, qo + q0:qo + q0 + 512], start=True, stop=(cls != 0),
                tile_position=(0, 0)), waits=w)
            ps_last = self.pe(lambda e, pss=pss, kt=kt, q0=q0, cls=cls, qo=qo: e.matmul(
                pss[:, 512:1024], self.qk[64:128, qo + T + kt * 128:qo + T + (kt + 1) * 128],
                self.qk[64:128, qo + q0:qo + q0 + 512], start=True, stop=(cls != 0),
                tile_position=(64, 0)))
            if cls == 0:
                s0_ = 3968 - kt * 128 + q0
                self.pe(lambda e, pss=pss, s0_=s0_: e.matmul(
                    pss[:, 0:512], self.ident_bf[:], self.band[:, s0_:s0_ + 512],
                    start=False, stop=True))
                ps_last = self.pe(lambda e, pss=pss, s0_=s0_: e.matmul(
                    pss[:, 512:1024], self.ident_bf[:], self.band[:, s0_:s0_ + 512],
                    start=False, stop=True))
            a_exp = self.act(lambda e, pss=pss, poff=poff, cls=cls, cm=cm: e.activation(
                self.pbuf[:, poff:poff + 1024], pss[:, 0:1024], AF.Exp, scale=0.125,
                bias=self.ubc[:, cm + cls:cm + cls + 1]), waits=[(sPE, ps_last)])
            exp_of[kt] = a_exp
            self.slot_rel[sl] = [(sACT, a_exp)]
            if kt >= 1:
                emit_pv(kt - 1)
            if thunks and kt in points:
                thunks.pop(0)()
        while thunks:
            thunks.pop(0)()
        emit_pv(NKT - 1)
        a_r = self.act(lambda e: e.activation(self.ps_w0[0:1, 0:512], self.ps_ctxA[64:65, 0:512],
                                              AF.Ln), waits=[(sPE, pv_done), self.rel_w0,
                                                             self.rel_w1])
        a_r = self.act(lambda e: e.activation(self.ps_w1[0:1, 0:512], self.ps_ctxB[64:65, 0:512],
                                              AF.Ln), waits=[self.ctxAB_rel])
        a_r = self.act(lambda e: e.activation(self.reciprow[0:1, 0:1024],
                                              self.ps_wpair[0:1, 0:1024], AF.Exp, scale=-1.0))
        pb = self.pe(lambda e: e.matmul(self.ps_w0[0:64, 0:512], self.ones_row[0:1, 0:64],
                                        self.reciprow[0:1, 0:512], start=True, stop=True),
                     waits=[(sACT, a_r)])
        pb = self.pe(lambda e: e.matmul(self.ps_w1[0:64, 0:512], self.ones_row[0:1, 0:64],
                                        self.reciprow[0:1, 512:1024], start=True, stop=True))
        a_e = self.act(lambda e: e.copy(self.recb[0:64, 0:512], self.ps_w0[0:64, 0:512]),
                       waits=[(sPE, pb)])
        a_e = self.act(lambda e: e.copy(self.recb[64:128, 0:512], self.ps_w1[0:64, 0:512]))
        d_norm = self.dve(lambda e, q0=q0: e.tensor_mul(
            self.ctxAB[0:64, q0:q0 + 512], self.ps_ctxA[0:64, 0:512],
            self.recb[0:64, 0:512]), waits=[(sACT, a_e)])
        d_norm = self.dve(lambda e, q0=q0: e.tensor_mul(
            self.ctxAB[64:128, q0:q0 + 512], self.ps_ctxB[0:64, 0:512],
            self.recb[64:128, 0:512]))
        self.rel_w0 = (sACT, a_e)
        self.rel_w1 = (sACT, a_e)
        self.rel_ctx = (sDVE, d_norm)
        self.d_norm_last = d_norm

    def _scr_part(self, b, part, dn):
        sPOOL, sDVE = self.sPOOL, self.sDVE
        pl = 0
        for idx in range(part * 4, part * 4 + 4):
            h, s16 = divmod(idx, 16)
            g = 8 * h + b
            pch = s16 // 2
            pb = 64 * (s16 % 2)
            pl = self.pool(lambda e, g=g, h=h, s16=s16, pch=pch, pb=pb: e.tensor_copy(
                self.scr[pb:pb + 64, pch * T + g * 256: pch * T + (g + 1) * 256],
                self.ctxAB[h * 64:(h + 1) * 64, s16:T:16]),
                waits=[(sDVE, dn)] if idx == part * 4 else ())
        if part == 7:
            self.ctxAB_rel = (sPOOL, pl)

    def scr_thunks(self, b):
        dn = self.d_norm_last
        return [lambda part=part: self._scr_part(b, part, dn) for part in range(8)]

    def emit_scr(self, b):
        dn = self.d_norm_last
        for part in range(8):
            self._scr_part(b, part, dn)

    def phaseA(self):
        self.dgA = {}
        self.dBA = {}
        self.fixA = {}
        self.vt_rel = (self.sPE, 0)
        self.slot_rel = [[self.rel_s0], [self.rel_s1]]
        for stage in range(NT + 1):
            for t in self.make_qkv_thunks(0, stage):
                t()
        for t in self.ub_thunks(0):
            t()
        for b in range(B):
            for qc in range(NT):
                th = []
                if b + 1 < B:
                    th += self.make_qkv_thunks(b + 1, qc)
                    if qc == NT - 1:
                        th += self.make_qkv_thunks(b + 1, NT) + self.ub_thunks(b + 1)
                if b >= 1 and qc == 0:
                    th = th[:1] + self.scr_thunks(b - 1) + th[1:]
                self.emit_attn_qc(b, qc, th)
            if b == B - 1:
                self.emit_scr(b)
        self.qkv_ready = (self.sPE, self.sPE.n)
        self.rel_s0 = self.slot_rel[0][0]
        self.rel_s1 = self.slot_rel[1][0]

    def close512(self, src, q0, extra_ready=(), dst=None, bsel=0):
        h0 = bsel * 512
        """stream[c][:, q0:q0+512] (or dst chunks) += src(c)*rmB2 + rmB1."""
        sPOOL, sDVE, sACT = self.sPOOL, self.sDVE, self.sACT
        d_cl = 0
        d_read = self.sDVE.n
        for c in range(NCH):
            pl = self.pool(lambda e, c=c, h0=h0: e.tensor_mul(self.bscr[:, 512:1024], src(c),
                                                              self.rmB2[:, h0:h0 + 512]),
                           waits=(list(extra_ready) if c == 0 else []) + [(sDVE, d_read)])
            d_read = self.dve(lambda e, h0=h0: e.tensor_add(
                self.bscr[:, 0:512], self.bscr[:, 512:1024], self.rmB1[:, h0:h0 + 512]),
                              waits=[(sPOOL, pl)])
            tgt = self.stream[c][:, q0:q0 + 512] if dst is None else dst(c)
            d_cl = self.dve(lambda e, tgt=tgt: e.tensor_add(tgt, tgt, self.bscr[:, 0:512]))
        del d_read
        self.B_rel[bsel] = (sDVE, d_cl)
        return d_cl

    def phase_wo_close(self):
        sPE, sACT, sDVE, sPOOL, dG = self.sPE, self.sACT, self.sDVE, self.sPOOL, self.dG
        dg_w = self.wload3("gpsimd", "dG", self.wslot[0], 0, self.d_woT, D,
                           waits=[self.wslot_rel[0]])
        dg_x = {}
        for c in range(NCH):
            dg_x[c] = self.dma("gpsimd", "dT", lambda e, c=c: e.dma_start(
                out=self.stream[c][:], in_=self.d_xTk[c]),
                waits=[self.stream_rel[c]])
        d_close = 0
        for quar in range(4):
            qq0 = quar * 1024
            pool_snap = self.sPOOL.n
            a_y = 0
            for j in range(NCH):
                for ntl in range(2):
                    q0 = qq0 + ntl * 512
                    even = (j * 2 + ntl) % 2 == 0
                    psum = self.ps_w0 if even else self.ps_w1
                    rel = self.rel_w0 if even else self.rel_w1
                    p = 0
                    for c in range(NCH):
                        p = self.pe(lambda e, j=j, c=c, q0=q0, psum=psum: e.matmul(
                            psum[:, 0:512],
                            self.wslot[0][:, c * D + j * 128: c * D + (j + 1) * 128],
                            self.scr[:, c * T + q0: c * T + q0 + 512],
                            start=(c == 0), stop=(c == NCH - 1)),
                            waits=([rel, (dG, dg_w)] +
                                   ([self.qkv_ready, self.ctxAB_rel]
                                    if (quar == 0 and j == 0 and ntl == 0) else []))
                            if c == 0 else ())
                    a_y = self.act(lambda e, j=j, ntl=ntl, psum=psum: e.activation(
                        self.qk[:, j * 1024 + ntl * 512: j * 1024 + ntl * 512 + 512],
                        psum[:, 0:512], AF.Identity, bias=self.bo_col[:, j:j + 1]),
                        waits=[(sPE, p), (sPOOL, pool_snap)])
                    if even:
                        self.rel_w0 = (sACT, a_y)
                    else:
                        self.rel_w1 = (sACT, a_y)
            for ntl in range(2):
                q0 = qq0 + ntl * 512

                def yv(c, ntl=ntl):
                    return self.qk[:, c * 1024 + ntl * 512: c * 1024 + ntl * 512 + 512]

                dB = self.stats512(yv, [(sACT, a_y)])
                xw = [(self.sems["dT"], dg_x[NCH - 1]), (sDVE, dB)]
                d_close = self.close512(yv, q0, extra_ready=xw)
        self.wslot_rel[0] = (sPE, self.sems["PE"].n)
        self.stream_written = (sDVE, d_close)
        self.qk_rel = (sDVE, d_close)

    def apply512(self, src, dst, q0, extra=None, bsel=0):
        h0 = bsel * 512
        """dst(c) (bf16) = src(c)*rmB2 + rmB1 (extra: per-c second multiply col/tile)."""
        sPOOL, sDVE = self.sPOOL, self.sDVE
        d_read = self.sDVE.n
        d = 0
        for c in range(NCH):
            pl = self.pool(lambda e, c=c, h0=h0: e.tensor_mul(self.bscr[:, 512:1024], src(c),
                                                              self.rmB2[:, h0:h0 + 512]),
                           waits=[(sDVE, d_read)])
            d_read = self.dve(lambda e, c=c, h0=h0: e.tensor_add(dst(c), self.bscr[:, 512:1024],
                                                                 self.rmB1[:, h0:h0 + 512]),
                              waits=[(sPOOL, pl)])
            d = d_read
        self.B_rel[bsel] = (sDVE, d)
        return d

    def phaseF(self):
        sPE, sACT, sDVE, sPOOL, dG, dS = self.sPE, self.sACT, self.sDVE, self.sPOOL, self.dG, self.dS
        self.wload3("gpsimd", "dG", self.wslot[0], 0, self.d_fwin, 64)
        dg_w = self.dma("gpsimd", "dG", lambda e: e.dma_start(
            out=self.wslot[1][0:64, 0:D], in_=self.d_fwout))
        ds = self.dma("sync", "dS", lambda e: e.dma_start(out=self.fw12[:, 0:16], in_=self.d_fw1))
        ds = self.dma("sync", "dS", lambda e: e.dma_start(out=self.fw12[:, 16:32], in_=self.d_fb1))
        ds = self.dma("sync", "dS", lambda e: e.dma_start(out=self.fw12[:, 32:48], in_=self.d_fw2))
        ds = self.dma("sync", "dS", lambda e: e.dma_start(
            out=self.svec[0:64, 0:1], in_=self.d_fbin.rearrange("a m -> m a"),
            allow_slow_non_contiguous=True))
        ds = self.dma("sync", "dS", lambda e: e.dma_start(
            out=self.svec[0:64, 1:2], in_=self.d_fb2.rearrange("a m -> m a"),
            allow_slow_non_contiguous=True))
        ds = self.dma("sync", "dS", lambda e: e.dma_start(
            out=self.svec[:, 8:16], in_=self.d_fbout.rearrange("c p -> p c"),
            allow_slow_non_contiguous=True))
        xn, outf = self.ck["xn"], self.ck["xm"]

        def sv(c, q0):
            return self.stream[c][:, q0:q0 + 512]

        dBF = {}
        dBF[0] = self.stats512(lambda c: sv(c, 0), [(sDVE, 1)], bsel=0)
        for cq in range(NCH):
            q0 = cq * 512
            bs = cq % 2
            if cq + 1 < NCH:
                dBF[cq + 1] = self.stats512(lambda c, q1=q0 + 512: sv(c, q1),
                                            [(sDVE, 1)], bsel=(cq + 1) % 2)
            d_xn = self.apply512(lambda c, q0=q0: sv(c, q0),
                                 lambda c: xn[:, c * 512:(c + 1) * 512], q0, bsel=bs)
            p = 0
            for c in range(NCH):
                p = self.pe(lambda e, c=c: e.matmul(
                    self.ps_w0[0:64, 0:512], self.wslot[0][:, c * 64:(c + 1) * 64],
                    xn[:, c * 512:(c + 1) * 512], start=(c == 0), stop=(c == NCH - 1)),
                    waits=[(sDVE, d_xn), (dG, dg_w), self.rel_w0] if c == 0 else ())
            a_s = self.act(lambda e: e.activation(self.fmid_bf[0:64, 0:512], self.ps_w0[0:64, 0:512],
                                                  AF.Identity, bias=self.svec[0:64, 0:1]),
                           waits=[(sPE, p), (dS, ds)])
            self.rel_w0 = (sACT, a_s)
            a_g = self.act(lambda e: e.activation(self.fmid[0:64, 0:512], self.fmid_bf[0:64, 0:512],
                                                  AF.Identity, scale=0.0, bias=self.svec[0:64, 1:2]))
            d_g = 0
            a_r = {}
            for i in range(16):
                ro = 512 * (1 + i % 3)
                w = []
                if i >= 3:
                    w.append((sDVE, d_g - (2 if False else 2)))
                a_r[i] = self.act(lambda e, i=i, ro=ro: e.activation(
                    self.fmid_bf[0:64, ro:ro + 512], self.fmid_bf[0:64, 0:512], AF.Relu,
                    scale=self.fw12[0:64, i:i + 1], bias=self.fw12[0:64, 16 + i:17 + i]),
                    waits=[(sDVE, max(0, d_g - 2))] if i >= 3 else ())
                d_g = self.dve(lambda e, i=i, ro=ro: e.scalar_tensor_tensor(
                    self.fmid[0:64, 0:512], self.fmid_bf[0:64, ro:ro + 512],
                    self.fw12[0:64, 32 + i:33 + i], self.fmid[0:64, 0:512],
                    ALU.mult, ALU.add), waits=[(sACT, a_r[i]), (sACT, a_g)] if i == 0 else [(sACT, a_r[i])])
            d_g = self.dve(lambda e: e.tensor_copy(self.fmid_bf[0:64, 0:512],
                                                   self.fmid[0:64, 0:512]))
            a_o = 0
            for j in range(NCH):
                psum = self.ps_w0 if j % 2 == 0 else self.ps_w1
                rel = self.rel_w0 if j % 2 == 0 else self.rel_w1
                p = self.pe(lambda e, j=j, psum=psum: e.matmul(
                    psum[:, 0:512], self.wslot[1][0:64, j * 128:(j + 1) * 128],
                    self.fmid_bf[0:64, 0:512], start=True, stop=True),
                    waits=[(sDVE, d_g), rel])
                a_o = self.act(lambda e, j=j, psum=psum: e.activation(
                    outf[:, j * 512:(j + 1) * 512], psum[:, 0:512], AF.Identity,
                    bias=self.svec[:, 8 + j:9 + j]), waits=[(sPE, p)])
                if j % 2 == 0:
                    self.rel_w0 = (sACT, a_o)
                else:
                    self.rel_w1 = (sACT, a_o)
            dBo = self.stats512(lambda c: outf[:, c * 512:(c + 1) * 512], [(sACT, a_o)], bsel=bs)
            self.close512(lambda c: outf[:, c * 512:(c + 1) * 512], q0,
                          extra_ready=[(sDVE, dBo)], bsel=bs)
        self.wslot_rel = [(sPE, self.sPE.n), (sPE, self.sPE.n)]

    def phaseC(self):
        sPE, sACT, sDVE, sPOOL, dG, dS = self.sPE, self.sACT, self.sDVE, self.sPOOL, self.dG, self.dS
        dW1 = self.sems["dW1"]
        ds = self.dma("sync", "dS", lambda e: e.dma_start(
            out=self.cv[:, 0:8], in_=self.d_crmsw.rearrange("c p -> p c"),
            allow_slow_non_contiguous=True))
        ds = self.dma("sync", "dS", lambda e: e.dma_start(
            out=self.cv[:, 8:16], in_=self.d_cbA.rearrange("c p -> p c"),
            allow_slow_non_contiguous=True))
        d_c0 = self.dve(lambda e: e.memset(self.carry[:, 0:8], 0.0))
        xm, xn, aa, bb, gg = (self.ck[n] for n in ("xm", "xn", "aa", "bb", "gg"))
        slot_rel = list(self.wslot_rel)
        slot_sem = (self.dG, dW1)

        def sv(c, q0):
            return self.stream[c][:, q0:q0 + 512]

        dBC = {}
        dBC[0] = self.stats512(lambda c: sv(c, 0), [(sDVE, 1)], rms=True, bsel=0)
        d_h_prev = 0
        for cq in range(NCH):
            q0 = cq * 512
            bs = cq % 2
            worder = ((self.d_cwog, 0), (self.d_cwx, 1), (self.d_cwA, 0),
                      (self.d_cwsg, 1), (self.d_cwo, 0))
            wdma = []
            for wsrc, sl in worder[:2]:
                wdma.append(self.wload3("gpsimd", "dG" if sl == 0 else "dW1",
                                        self.wslot[sl], 0, wsrc, D, waits=[slot_rel[sl]]))
            d_xm = self.apply512(lambda c, q0=q0: sv(c, q0),
                                 lambda c: xm[:, c * 512:(c + 1) * 512], q0, bsel=bs)
            d_xn = 0
            for c in range(NCH):
                d_xn = self.dve(lambda e, c=c, bs=bs: e.scalar_tensor_tensor(
                    xn[:, c * 512:(c + 1) * 512], xm[:, c * 512:(c + 1) * 512],
                    self.cv[:, c:c + 1], self.ainv_t[:, bs * 512:bs * 512 + 512],
                    ALU.mult, ALU.mult),
                    waits=[(dS, ds)] if c == 0 else ())

            def proj(widx, wsl, dgv, act, dst, bias=None):
                act_snap = self.sACT.n
                lastd = 0
                for j in range(NCH):
                    psum = self.ps_w0 if j % 2 == 0 else self.ps_w1
                    rel = self.rel_w0 if j % 2 == 0 else self.rel_w1
                    p = 0
                    for c in range(NCH):
                        p = self.pe(lambda e, j=j, c=c, psum=psum, wsl=wsl: e.matmul(
                            psum[:, 0:512],
                            self.wslot[wsl][:, c * D + j * 128: c * D + (j + 1) * 128],
                            xn[:, c * 512:(c + 1) * 512],
                            start=(c == 0), stop=(c == NCH - 1)),
                            waits=[(sDVE, d_xn), (slot_sem[wsl], dgv), rel] if c == 0 else ())
                    if act == "sig":
                        lastd = self.act(lambda e, j=j, psum=psum: e.activation(
                            dst(j), psum[:, 0:512], AF.Sigmoid,
                            bias=(self.cv[:, 8 + j:9 + j] if bias else 0.0)),
                            waits=[(sPE, p), (sDVE, d_h_prev)])
                        relv = (sACT, lastd)
                    else:
                        lastd = self.dve(lambda e, j=j, psum=psum: e.tensor_mul(
                            dst(j), psum[:, 0:512], bb[:, j * 512:(j + 1) * 512]),
                            waits=[(sPE, p), (sACT, act_snap)])
                        relv = (sDVE, lastd)
                    if j % 2 == 0:
                        self.rel_w0 = relv
                    else:
                        self.rel_w1 = relv
                return lastd

            a_sg = proj(0, 0, wdma[0], "sig", lambda j: bb[:, j * 512:(j + 1) * 512])
            if cq + 1 < NCH:
                dBC[cq + 1] = self.stats512(lambda c, q1=q0 + 512: sv(c, q1),
                                            [(sDVE, 1)], rms=True, bsel=(cq + 1) % 2)
            wdma.append(self.wload3("gpsimd", "dG", self.wslot[0], 0, worder[2][0], D,
                                    waits=[self.rel_w0, self.rel_w1]))
            d_b = proj(1, 1, wdma[1], "mulsg", lambda j: bb[:, j * 512:(j + 1) * 512])
            wdma.append(self.wload3("gpsimd", "dW1", self.wslot[1], 0, worder[3][0], D,
                                    waits=[self.rel_w0, self.rel_w1]))
            a_A = proj(2, 0, wdma[2], "sig", lambda j: aa[:, j * 512:(j + 1) * 512], bias=True)
            wdma.append(self.wload3("gpsimd", "dG", self.wslot[0], 0, worder[4][0], D,
                                    waits=[self.rel_w0, self.rel_w1]))
            a_gate = proj(3, 1, wdma[3], "sig", lambda j: gg[:, j * 512:(j + 1) * 512])
            pe_xn_done = self.sPE.n
            d_h = 0
            for c in range(NCH):
                d_h = self.dve(lambda e, c=c: e.tensor_tensor_scan(
                    xn[:, c * 512:(c + 1) * 512], aa[:, c * 512:(c + 1) * 512],
                    bb[:, c * 512:(c + 1) * 512], self.carry[:, c:c + 1],
                    ALU.mult, ALU.add),
                    waits=([(sACT, a_A), (sDVE, d_b), (sPE, pe_xn_done)] if c == 0 else ()))
                d_h = self.dve(lambda e, c=c: e.tensor_copy(
                    self.carry[:, c:c + 1], xn[:, c * 512 + 511:c * 512 + 512]))
            d_h_prev = d_h
            d_y = 0
            for c in range(NCH):
                d_y = self.dve(lambda e, c=c: e.tensor_mul(
                    gg[:, c * 512:(c + 1) * 512], gg[:, c * 512:(c + 1) * 512],
                    xn[:, c * 512:(c + 1) * 512]), waits=[(sACT, a_gate)] if c == 0 else ())
            d_oc = 0
            for j in range(NCH):
                psum = self.ps_w0 if j % 2 == 0 else self.ps_w1
                rel = self.rel_w0 if j % 2 == 0 else self.rel_w1
                p = 0
                for c in range(NCH):
                    p = self.pe(lambda e, j=j, c=c, psum=psum: e.matmul(
                        psum[:, 0:512],
                        self.wslot[0][:, c * D + j * 128: c * D + (j + 1) * 128],
                        gg[:, c * 512:(c + 1) * 512],
                        start=(c == 0), stop=(c == NCH - 1)),
                        waits=[(sDVE, d_y), (dG, wdma[4]), rel] if c == 0 else ())
                d_oc = self.dve(lambda e, j=j, psum=psum: e.tensor_add(
                    xm[:, j * 512:(j + 1) * 512], psum[:, 0:512],
                    xm[:, j * 512:(j + 1) * 512]), waits=[(sPE, p)])
                if j % 2 == 0:
                    self.rel_w0 = (sDVE, d_oc)
                else:
                    self.rel_w1 = (sDVE, d_oc)
            slot_rel = [self.rel_w0, self.rel_w1]
            dBo = self.stats512(lambda c: xm[:, c * 512:(c + 1) * 512], [(sDVE, d_oc)], bsel=bs)
            self.close512(lambda c: xm[:, c * 512:(c + 1) * 512], q0,
                          extra_ready=[(sDVE, dBo)], bsel=bs)
        self.wslot_rel = [(sPE, self.sPE.n), (sPE, self.sPE.n)]

    def phaseD(self):
        sPE, sACT, sDVE, sPOOL, dG, dS = self.sPE, self.sACT, self.sDVE, self.sPOOL, self.dG, self.dS
        dW1 = self.sems["dW1"]
        ds = self.dma("sync", "dS", lambda e: e.dma_start(
            out=self.cv[:, 0:8], in_=self.d_mur.rearrange("c p -> p c"),
            allow_slow_non_contiguous=True))
        ds = self.dma("sync", "dS", lambda e: e.dma_start(
            out=self.cv[:, 8:16], in_=self.d_muk.rearrange("c p -> p c"),
            allow_slow_non_contiguous=True))
        d_mu = 0
        for h, (src_c, dst_c) in enumerate(((0, 0), (8, 8))):
            d_mu = self.dve(lambda e, src_c=src_c, dst_c=dst_c: e.tensor_scalar(
                self.svec[:, dst_c:dst_c + 8], self.cv[:, src_c:src_c + 8], -1.0, 1.0,
                ALU.mult, ALU.add), waits=[(dS, ds)] if h == 0 else ())
        d_cd = self.dve(lambda e: e.memset(self.carry[:, 8:16], 0.0))
        xms, xr, xk, rr, ka = self.ck_xms, self.ck["aa"], self.ck["bb"], self.ck["gg"], self.ck["xn"]
        od = self.ck["xm"]
        slot_rel = list(self.wslot_rel)

        def sv(c, q0):
            return self.stream[c][:, q0:q0 + 512]

        dBD = {}
        dBD[0] = self.stats512(lambda c: sv(c, 0), [(sDVE, d_mu)], bsel=0)
        for cq in range(NCH):
            q0 = cq * 512
            bs = cq % 2
            wdma_r = self.wload3("gpsimd", "dG", self.wslot[0], 0, self.d_dwrT, D,
                                 waits=[slot_rel[0]])
            wdma_k = self.wload3("gpsimd", "dW1", self.wslot[1], 0, self.d_dwkT, D,
                                 waits=[slot_rel[1]])
            d_sh = 0
            for c in range(NCH):
                d_sh = self.dve(lambda e, c=c: e.tensor_copy(
                    xms[:, c * 513:c * 513 + 1], self.carry[:, 8 + c:9 + c]),
                    waits=[(sDVE, d_cd)] if (cq == 0 and c == 0) else ())
            d_xm = self.apply512(lambda c, q0=q0: sv(c, q0),
                                 lambda c: xms[:, c * 513 + 1:c * 513 + 513], q0, bsel=bs)
            for c in range(NCH):
                d_xm = self.dve(lambda e, c=c: e.tensor_copy(
                    self.carry[:, 8 + c:9 + c], xms[:, c * 513 + 512:c * 513 + 513]))
            if cq + 1 < NCH:
                dBD[cq + 1] = self.stats512(lambda c, q1=q0 + 512: sv(c, q1),
                                            [(sDVE, 1)], bsel=(cq + 1) % 2)
            d_mix = 0
            for c in range(NCH):
                cur = xms[:, c * 513 + 1:c * 513 + 513]
                prv = xms[:, c * 513:c * 513 + 512]
                d1 = self.dve(lambda e, c=c, prv=prv: e.tensor_scalar_mul(
                    self.tscr2[:, 0:512], prv, self.svec[:, c:c + 1]))
                d_mix = self.dve(lambda e, c=c, cur=cur: e.scalar_tensor_tensor(
                    xr[:, c * 512:(c + 1) * 512], cur, self.cv[:, c:c + 1],
                    self.tscr2[:, 0:512], ALU.mult, ALU.add))
                d1 = self.dve(lambda e, c=c, prv=prv: e.tensor_scalar_mul(
                    self.tscr2[:, 0:512], prv, self.svec[:, 8 + c:9 + c]))
                d_mix = self.dve(lambda e, c=c, cur=cur: e.scalar_tensor_tensor(
                    xk[:, c * 512:(c + 1) * 512], cur, self.cv[:, 8 + c:9 + c],
                    self.tscr2[:, 0:512], ALU.mult, ALU.add))

            def proj2(wsl, dgv, src2, fin, waits0=()):
                lastd = 0
                for j in range(NCH):
                    psum = self.ps_w0 if j % 2 == 0 else self.ps_w1
                    rel = self.rel_w0 if j % 2 == 0 else self.rel_w1
                    p = 0
                    for c in range(NCH):
                        p = self.pe(lambda e, j=j, c=c, psum=psum, wsl=wsl: e.matmul(
                            psum[:, 0:512],
                            self.wslot[wsl][:, c * D + j * 128: c * D + (j + 1) * 128],
                            src2[:, c * 512:(c + 1) * 512],
                            start=(c == 0), stop=(c == NCH - 1)),
                            waits=(list(waits0) + [(self.dG if wsl == 0 else dW1, dgv), rel])
                            if c == 0 else ())
                    lastd, relv = fin(j, psum, p)
                    if j % 2 == 0:
                        self.rel_w0 = relv
                    else:
                        self.rel_w1 = relv
                return lastd

            def fin_r(j, psum, p):
                a = self.act(lambda e, j=j, psum=psum: e.activation(
                    rr[:, j * 512:(j + 1) * 512], psum[:, 0:512], AF.Sigmoid),
                    waits=[(sPE, p)])
                return a, (sACT, a)

            k_prev_act = [0]

            def fin_k(j, psum, p):
                d1 = self.dve(lambda e, psum=psum: e.tensor_scalar_max(
                    self.tscr2[:, 512:1024], psum[:, 0:512], 0.0),
                    waits=[(sPE, p), (sACT, k_prev_act[0])])
                a = self.act(lambda e, j=j: e.activation(
                    ka[:, j * 512:(j + 1) * 512], self.tscr2[:, 512:1024], AF.Square),
                    waits=[(sDVE, d1)])
                k_prev_act[0] = a
                return a, (sDVE, d1)

            def fin_v(j, psum, p):
                d1 = self.dve(lambda e, j=j, psum=psum: e.tensor_mul(
                    od[:, j * 512:(j + 1) * 512], psum[:, 0:512],
                    rr[:, j * 512:(j + 1) * 512]), waits=[(sPE, p)])
                return d1, (sDVE, d1)

            a_r = proj2(0, wdma_r, xr, fin_r, waits0=[(sDVE, d_mix)])
            a_k = proj2(1, wdma_k, xk, fin_k)
            wdma_v = self.wload3("gpsimd", "dG", self.wslot[0], 0, self.d_dwvT, D,
                                 waits=[self.rel_w0, self.rel_w1])
            d_v = proj2(0, wdma_v, ka, fin_v, waits0=[(sACT, self.sACT.n)])
            slot_rel = [self.rel_w0, self.rel_w1]
            dBo = self.stats512(lambda c: od[:, c * 512:(c + 1) * 512], [(sDVE, d_v)], bsel=bs)
            self.close512(lambda c: od[:, c * 512:(c + 1) * 512], q0,
                          extra_ready=[(sDVE, dBo)], bsel=bs)
        self.wslot_rel = [(sPE, self.sPE.n), (sPE, self.sPE.n)]

    def phase_out(self):
        sPE, sACT, sDVE, sST = self.sPE, self.sACT, self.sDVE, self.sST
        st_after = {}
        st_count = 0
        pe_prev = 0
        for tt in range(32):
            t0 = tt * 128
            ob = tt % 2
            d_c = 0
            for c in range(NCH):
                half = self.tscr if c < 4 else self.tscr2
                d_c = self.dve(lambda e, c=c, t0=t0, half=half: e.tensor_copy(
                    half[:, (c % 4) * 128:(c % 4 + 1) * 128], self.stream[c][:, t0:t0 + 128]),
                    waits=([self.stream_written, (sPE, pe_prev), self.B_rel[0], self.B_rel[1]]
                           if c == 0 else ()))
            p = 0
            for c in range(NCH):
                half = self.tscr if c < 4 else self.tscr2
                psum = self.ps_w0 if c < 4 else self.ps_w1
                p = self.pe(lambda e, c=c, half=half, psum=psum: e.transpose(
                    psum[:, (c % 4) * 128:(c % 4 + 1) * 128],
                    half[:, (c % 4) * 128:(c % 4 + 1) * 128], self.ident_f[:]),
                    waits=([(sDVE, d_c), self.rel_w0, self.rel_w1] if c == 0 else ()))
            pe_prev = p
            a = self.act(lambda e, ob=ob: e.copy(self.outstage[ob][:, 0:512], self.ps_w0[:, 0:512]),
                         waits=[(sPE, p)] + ([(sST, st_after.get(tt - 1, 0))] if tt >= 2 else []))
            a = self.act(lambda e, ob=ob: e.copy(self.outstage[ob][:, 512:1024], self.ps_w1[:, 0:512]))
            self.rel_w0 = (sACT, a)
            self.rel_w1 = (sACT, a)
            st_count = self.dma("sync", "ST", lambda e, t0=t0, ob=ob: e.dma_start(
                out=self.d_out[t0:t0 + 128, :], in_=self.outstage[ob][:]),
                waits=[(sACT, a)])
            st_after[tt] = st_count
        if self.on("sync"):
            self.e.wait_ge(self.sST.h, st_count)

    def build(self):
        blk_cm = self.nc.Block()
        blk = blk_cm.__enter__()
        with self.nc.allow_low_precision(reason="bf16 operands are intentional"):
            for engn in ENGS:
                getattr(blk, engn)(lambda e, _n=engn: self.emit(_n, e))
        blk_cm.__exit__(None, None, None)
        first = self._pass_counts[0]
        for cts in self._pass_counts[1:]:
            assert cts == first, (cts, first)
        return self.nc


NCORES = 8


def _host_prepare(inputs, core, xT, tb2):
    f = np.float32
    r0 = 128 * core
    d = dict(
        xT=xT,
        xTk=xT[core],
        wq=np.ascontiguousarray(inputs["L0_attn_Wq"][r0:r0 + 128, :].T.reshape(NCH, 128, 128)),
        wk=np.ascontiguousarray(inputs["L0_attn_Wk"][r0:r0 + 128, :].T.reshape(NCH, 128, 128)),
        wv=np.ascontiguousarray(inputs["L0_attn_Wv"][r0:r0 + 128, :].T.reshape(NCH, 128, 128)),
        bqkv=np.concatenate([inputs["L0_attn_bq"][r0:r0 + 128],
                             inputs["L0_attn_bk"][r0:r0 + 128],
                             inputs["L0_attn_bv"][r0:r0 + 128]]).reshape(1, 384).astype(f),
        woT=np.ascontiguousarray(inputs["L0_attn_Wo"].T.reshape(NCH, 128, D)),
        bo=inputs["L0_attn_bo"].reshape(NCH, 128).astype(f),
        tb2=tb2.astype(f),
        etis=np.array([[inputs["L0_attn_et_gate"].reshape(()),
                        inputs["L0_attn_is_gate"].reshape(())]], dtype=f),
        eye=np.eye(128, dtype=f),
        fwin=np.ascontiguousarray(inputs["L0_ffn_Win"].reshape(NCH, 128, 64)),
        fbin=inputs["L0_ffn_bin_"].reshape(1, 64).astype(f),
        fw1=inputs["L0_ffn_W1"].astype(f), fb1=inputs["L0_ffn_b1"].astype(f),
        fw2=inputs["L0_ffn_W2"].astype(f),
        fb2=inputs["L0_ffn_b2"].reshape(1, 64).astype(f),
        fwout=inputs["L0_ffn_Wout"].astype(f),
        fbout=inputs["L0_ffn_bout"].reshape(NCH, 128).astype(f),
        crmsw=inputs["L1_attn_rms_weight"].reshape(NCH, 128).astype(f),
        cwA=np.ascontiguousarray(inputs["L1_attn_w_A"].reshape(NCH, 128, D)),
        cbA=inputs["L1_attn_b_A"].reshape(NCH, 128).astype(f),
        cwx=np.ascontiguousarray(inputs["L1_attn_w_x"].reshape(NCH, 128, D)),
        cwsg=np.ascontiguousarray(inputs["L1_attn_w_sidegate"].reshape(NCH, 128, D)),
        cwog=np.ascontiguousarray(inputs["L1_attn_w_output"].reshape(NCH, 128, D)),
        cwo=np.ascontiguousarray(inputs["L1_attn_w_o"].reshape(NCH, 128, D)),
        dwrT=np.ascontiguousarray(inputs["L1_ffn_W_r"].T.reshape(NCH, 128, D)),
        dwkT=np.ascontiguousarray(inputs["L1_ffn_W_k"].T.reshape(NCH, 128, D)),
        dwvT=np.ascontiguousarray(inputs["L1_ffn_W_v"].T.reshape(NCH, 128, D)),
        mur=inputs["L1_ffn_mu_r"].reshape(NCH, 128).astype(f),
        muk=inputs["L1_ffn_mu_k"].reshape(NCH, 128).astype(f),
    )
    return d


def kernel(**inputs):
    x = np.asarray(inputs["L0_attn_x"], dtype=np.float32)
    xT = np.ascontiguousarray(x.transpose(0, 2, 1).reshape(B, NCH, 128, T))
    tb = np.asarray(inputs["L0_attn_time_bias"], dtype=np.float32).reshape(-1)
    idx = np.clip(C0 + np.arange(128)[:, None] - np.arange(WB)[None, :], 0, 499)
    tb2 = np.ascontiguousarray(tb[idx])
    bld = Builder()
    nc = bld.build()
    in_maps = [_host_prepare(inputs, k, xT, tb2) for k in range(NCORES)]
    res = run_bass_kernel_spmd(nc, in_maps, core_ids=list(range(NCORES)))
    out = np.stack([np.asarray(res.results[k]["out"]) for k in range(NCORES)], axis=0)
    return out


# version 22
# speedup vs baseline: 1.0425x; 1.0154x over previous
import numpy as np
import ml_dtypes
import concourse.bass as bass
import concourse.mybir as mybir
from concourse.bass_utils import run_bass_kernel_spmd

F32 = mybir.dt.float32
BF16 = mybir.dt.bfloat16
AF = mybir.ActivationFunctionType
ALU = mybir.AluOpType
AX = mybir.AxisListType
AP = bass.AP

B = 8
T = 4096
D = 1024
NCH = 8
NT = 8
NKT = 32
WB = 8064
C0 = 4218
EPS = 1e-5
RD = 1.0 / D

ENGS = ("sync", "tensor", "vector", "scalar", "gpsimd")
UPTO = "full"
CHAIN_SAME_ENGINE = {"DVE": True, "ACT": False, "POOL": False}


def band_class(kt, qc):
    kmin = kt * 128 - (qc * 512 + 511)
    kmax = kt * 128 + 127 - qc * 512
    if kmax <= -250:
        return 1
    if kmin >= 249:
        return 2
    return 0


class Sem:
    __slots__ = ("h", "n", "name")

    def __init__(self, h, name):
        self.h = h
        self.n = 0
        self.name = name


class Builder:
    def __init__(self, upto=UPTO):
        self.upto = upto
        nc = self.nc = bass.Bass("TRN2", target_bir_lowering=False)

        dt_ = nc.dram_tensor
        self.d_xT = dt_("xT", [B, NCH, 128, T], F32, kind="ExternalInput").ap()
        self.d_xTk = dt_("xTk", [NCH, 128, T], F32, kind="ExternalInput").ap()
        self.d_wq = dt_("wq", [NCH, 128, 128], F32, kind="ExternalInput").ap()
        self.d_wk = dt_("wk", [NCH, 128, 128], F32, kind="ExternalInput").ap()
        self.d_wv = dt_("wv", [NCH, 128, 128], F32, kind="ExternalInput").ap()
        self.d_bqkv = dt_("bqkv", [1, 384], F32, kind="ExternalInput").ap()
        self.d_woT = dt_("woT", [NCH, 128, D], F32, kind="ExternalInput").ap()
        self.d_bo = dt_("bo", [NCH, 128], F32, kind="ExternalInput").ap()
        self.d_tb2 = dt_("tb2", [128, WB], F32, kind="ExternalInput").ap()
        self.d_etis = dt_("etis", [1, 2], F32, kind="ExternalInput").ap()
        self.d_eye = dt_("eye", [128, 128], F32, kind="ExternalInput").ap()
        self.d_fwin = dt_("fwin", [NCH, 128, 64], F32, kind="ExternalInput").ap()
        self.d_fbin = dt_("fbin", [1, 64], F32, kind="ExternalInput").ap()
        self.d_fw1 = dt_("fw1", [64, 16], F32, kind="ExternalInput").ap()
        self.d_fb1 = dt_("fb1", [64, 16], F32, kind="ExternalInput").ap()
        self.d_fw2 = dt_("fw2", [64, 16], F32, kind="ExternalInput").ap()
        self.d_fb2 = dt_("fb2", [1, 64], F32, kind="ExternalInput").ap()
        self.d_fwout = dt_("fwout", [64, D], F32, kind="ExternalInput").ap()
        self.d_fbout = dt_("fbout", [NCH, 128], F32, kind="ExternalInput").ap()
        self.d_crmsw = dt_("crmsw", [NCH, 128], F32, kind="ExternalInput").ap()
        self.d_cwA = dt_("cwA", [NCH, 128, D], F32, kind="ExternalInput").ap()
        self.d_cbA = dt_("cbA", [NCH, 128], F32, kind="ExternalInput").ap()
        self.d_cwx = dt_("cwx", [NCH, 128, D], F32, kind="ExternalInput").ap()
        self.d_cwsg = dt_("cwsg", [NCH, 128, D], F32, kind="ExternalInput").ap()
        self.d_cwog = dt_("cwog", [NCH, 128, D], F32, kind="ExternalInput").ap()
        self.d_cwo = dt_("cwo", [NCH, 128, D], F32, kind="ExternalInput").ap()
        self.d_dwrT = dt_("dwrT", [NCH, 128, D], F32, kind="ExternalInput").ap()
        self.d_dwkT = dt_("dwkT", [NCH, 128, D], F32, kind="ExternalInput").ap()
        self.d_dwvT = dt_("dwvT", [NCH, 128, D], F32, kind="ExternalInput").ap()
        self.d_mur = dt_("mur", [NCH, 128], F32, kind="ExternalInput").ap()
        self.d_muk = dt_("muk", [NCH, 128], F32, kind="ExternalInput").ap()
        self.d_wbf = dt_("wbf", [8, NCH, 128, D], BF16, kind="Internal").ap()
        self.d_out = dt_("out", [T, D], F32, kind="ExternalOutput").ap()

        sb = nc.alloc_sbuf_tensor

        def gsb(name, shape, dt):
            guard = nc.sbuf_tensor(name, shape, dt, side="right")
            h = guard.__enter__()
            self._guards[name] = guard
            return h

        self._guards = {}
        self.xsq = sb("xsq", [128, NCH * 512], BF16)
        self.ident_f = sb("ident_f", [128, 128], F32)
        self.ident_bf = sb("ident_bf", [128, 128], BF16)
        self.ones128 = sb("ones128", [128, 128], BF16)
        self.ones_row = sb("ones_row", [1, 512], BF16)
        self.onescol_bf = sb("onescol_bf", [128, 1], BF16)
        self.ones1_f = sb("ones1_f", [1, 128], F32)
        self.onescol_f = sb("onescol_f", [128, 1], F32)
        self.bias_row = sb("bias_row", [1, 384], BF16)
        self.bo_col = sb("bo_col", [128, NCH], F32)
        self.wsum = sb("wsum", [128, 4], F32)
        self.tiny = sb("tiny", [128, 16], F32)
        self.ubc = sb("ubc", [128, 8], F32)
        self.rmB1 = sb("rmB1", [128, 1024], BF16)
        self.rmB2 = sb("rmB2", [128, 1024], BF16)
        self.tscr = sb("tscr", [128, 1024], F32)
        self.tscr2 = sb("tscr2", [128, 1024], F32)
        self.reciprow = sb("reciprow", [1, 1024], BF16)
        self.recb = sb("recb", [128, 512], BF16)
        self.bscr = sb("bscr", [128, 1024], BF16)
        self.colmax = sb("colmax", [128, 8], F32)
        self.svec = sb("svec", [128, 16], F32)
        self.eps_col = sb("eps_col", [128, 1], F32)
        self.scr = gsb("scr", [128, NCH * T], BF16)
        self.qk = gsb("qk", [128, 4 * T], BF16)
        self.x_nt = gsb("x_nt", [128, 2 * NCH * 512], BF16)
        self.vt = gsb("vt", [128, 2 * 32 * 132], BF16)
        self.band = gsb("band", [128, WB], BF16)
        self.pbuf = gsb("pbuf", [128, 2 * 1024], BF16)
        self.ctxAB = gsb("ctxAB", [128, T], BF16)
        self.wsb = gsb("wsb", [128, 3 * NCH * 128], BF16)
        for n in ("wsb", "ctxAB", "pbuf", "band", "vt", "x_nt"):
            self._guards.pop(n).__exit__(None, None, None)
        self.stream = [sb(f"stream{c}", [128, T], BF16) for c in range(NCH)]
        self.wslot = [sb("wslot0", [128, NCH * D], BF16)]
        for n in ("qk", "scr"):
            self._guards.pop(n).__exit__(None, None, None)
        self.wslot.append(sb("wslot1", [128, NCH * D], BF16))
        self.ck = {n: sb("ck_" + n, [128, NCH * 512], BF16)
                   for n in ("xm", "xn", "aa", "bb", "gg")}
        self.ck_xms = sb("ck_xms", [128, NCH * 513], BF16)
        self.ainv_t = sb("ainv_t", [128, 1024], BF16)
        self.outstage = [sb("outs0", [128, 1024], F32), sb("outs1", [128, 1024], F32)]
        self.fmid = sb("fmid", [64, 512], F32)
        self.fmid_bf = sb("fmid_bf", [64, 512 * 4], BF16)
        self.fw12 = sb("fw12", [64, 48], F32)
        self.carry = sb("carry", [128, 24], F32)
        self.cv = sb("cv", [128, 16], F32)

        ps = nc.alloc_psum_tensor
        self.ps_s0 = ps("ps_s0", [128, 1024], F32)
        self.ps_s1 = ps("ps_s1", [128, 1024], F32)
        self.ps_ctxA = ps("ps_ctxA", [65, 512], F32)
        self.ps_ctxB = ps("ps_ctxB", [65, 512], F32)
        _g0 = nc.psum_tensor("ps_w0", [128, 512], F32)
        self.ps_w0 = _g0.__enter__()
        _g1 = nc.psum_tensor("ps_w1", [128, 512], F32)
        self.ps_w1 = _g1.__enter__()
        _g1.__exit__(None, None, None)
        _g0.__exit__(None, None, None)
        self.ps_wpair = ps("ps_wpair", [128, 1024], F32)

        self.sems = {}
        for name in ("dS", "dG", "dW1", "dT", "dX0", "dX1", "PE", "ACT", "DVE", "POOL", "ST"):
            self.sems[name] = Sem(nc.alloc_semaphore(name), name)
        self._pass_counts = []

    def on(self, eng):
        return self.engn == eng

    def pe(self, fn, waits=()):
        s = self.sems["PE"]
        if self.on("tensor"):
            for sem, val in waits:
                if val > 0:
                    self.e.wait_ge(sem.h, val)
            fn(self.e).then_inc(s.h, 1)
        s.n += 1
        return s.n

    def _chained(self, engkey, engname, fn, waits):
        s = self.sems[engkey]
        if self.on(engname):
            for sem, val in waits:
                if val > 0:
                    self.e.wait_ge(sem.h, val)
            if CHAIN_SAME_ENGINE.get(engkey, True) and s.n > 0:
                self.e.wait_ge(s.h, s.n)
            fn(self.e).then_inc(s.h, 1)
        s.n += 1
        return s.n

    def dve(self, fn, waits=()):
        return self._chained("DVE", "vector", fn, waits)

    def act(self, fn, waits=()):
        return self._chained("ACT", "scalar", fn, waits)

    def pool(self, fn, waits=()):
        return self._chained("POOL", "gpsimd", fn, waits)

    def dma(self, eng, semkey, fn, waits=()):
        s = self.sems[semkey]
        if self.on(eng):
            for sem, val in waits:
                if val > 0:
                    self.e.wait_ge(sem.h, val)
            fn(self.e).then_inc(s.h, 16)
        s.n += 16
        return s.n

    def wload3(self, eng, semkey, dst, dst_off, src, wd, nch=NCH, waits=()):
        cols = dst.shape[1]
        return self.dma(eng, semkey, lambda e: e.dma_start(
            out=AP(dst, dst_off, [[cols, 128], [wd, nch], [1, wd]]),
            in_=src.rearrange("c p m -> p c m")), waits=waits)

    def phase_barrier(self):
        for s in self.sems.values():
            if s.n > 0:
                self.e.wait_ge(s.h, s.n)

    def emit(self, engn, eng):
        self.engn = engn
        self.e = eng
        for x in self.sems.values():
            x.n = 0
        S = self.sems
        self.dS, self.dG = S["dS"], S["dG"]
        self.dX = [S["dX0"], S["dX1"]]
        self.sPE, self.sACT, self.sDVE, self.sPOOL = S["PE"], S["ACT"], S["DVE"], S["POOL"]
        self.sST = S["ST"]
        z = (self.sPE, 0)
        self.rel_s0 = z
        self.rel_s1 = z
        self.rel_w0 = z
        self.rel_w1 = z
        self.rel_ctx = z
        self.xbuf_rel = [z, z]
        self.xsq_rel = z
        self.B_rel = [z, z]
        self.ctxAB_rel = z
        self.qkv_ready = z
        self.stream_rel = [z] * NCH
        self.wslot_rel = [z, z]

        self.ubc_act_rel = 0
        self.qkv_done = {}

        self.preamble()
        self.phaseA()
        self.phase_barrier()
        self.phase_wo_close()
        self.phase_barrier()
        if self.upto in ("l0f", "l1a", "full"):
            self.phaseF()
            self.phase_barrier()
        if self.upto in ("l1a", "full"):
            self.phaseC()
            self.phase_barrier()
        if self.upto == "full":
            self.phaseD()
            self.phase_barrier()
        self.phase_out()
        self._pass_counts.append({k: v.n for k, v in S.items()})

    def preamble(self):
        dS, dG, sPE, sACT, sDVE = self.dS, self.dG, self.sPE, self.sACT, self.sDVE

        self.dma("sync", "dS", lambda e: e.dma_start(out=self.ident_f[:], in_=self.d_eye))
        self.dma("sync", "dS", lambda e: e.dma_start(out=self.tiny[0:1, 0:2], in_=self.d_etis))
        d_sync0 = self.dma("sync", "dS", lambda e: e.dma_start(
            out=self.bo_col[:], in_=self.d_bo.rearrange("c p -> p c"),
            allow_slow_non_contiguous=True))
        self.dma("gpsimd", "dG", lambda e: e.dma_start(out=self.band[:], in_=self.d_tb2))
        for j, w in enumerate((self.d_wq, self.d_wk, self.d_wv)):
            self.wload3("gpsimd", "dG", self.wsb, j * 1024, w, 128)
        d_g0 = self.dma("gpsimd", "dG", lambda e: e.dma_start(out=self.bias_row[:], in_=self.d_bqkv))

        self.dve(lambda e: e.memset(self.ones128[:], 1.0))
        self.dve(lambda e: e.memset(self.ones_row[:], 1.0))
        self.dve(lambda e: e.memset(self.eps_col[:], EPS))
        self.dve(lambda e: e.memset(self.onescol_bf[:], 1.0))
        self.dve(lambda e: e.memset(self.ones1_f[:], 1.0))
        self.dve(lambda e: e.memset(self.onescol_f[:], 1.0))
        self.dve(lambda e: e.memset(AP(self.vt, 64, [[2 * 32 * 132, 128], [132, 64]]), 1.0))
        self.dve(lambda e: e.memset(AP(self.vt, 130, [[2 * 32 * 132, 128], [132, 64]]), 1.0))
        d_idbf = self.dve(lambda e: e.tensor_copy(self.ident_bf[:], self.ident_f[:]),
                          waits=[(dS, d_sync0)])
        a_sig = self.act(lambda e: e.activation(self.tiny[0:1, 3:4], self.tiny[0:1, 0:1], AF.Sigmoid),
                         waits=[(dS, d_sync0)])
        p = 0
        for o in range(3):
            for c in range(NCH):
                p = self.pe(lambda e, o=o, c=c: e.matmul(
                    self.ps_w0[:, o:o + 1],
                    self.wsb[:, o * 1024 + c * 128: o * 1024 + (c + 1) * 128],
                    self.onescol_bf[:, 0:1], start=(c == 0), stop=(c == NCH - 1)),
                    waits=([(dG, d_g0), (sDVE, d_idbf)] if (o == 0 and c == 0) else ()))
        a_ws = self.act(lambda e: e.copy(self.wsum[:, 0:3], self.ps_w0[:, 0:3]), waits=[(sPE, p)])
        self.dve(lambda e: e.tensor_scalar_mul(self.tiny[0:1, 4:5], self.tiny[0:1, 3:4], 8.0),
                 waits=[(sACT, a_sig)])
        d2 = self.dve(lambda e: e.tensor_scalar_mul(self.tiny[0:1, 5:6], self.tiny[0:1, 1:2], 8.0))
        p = self.pe(lambda e: e.matmul(self.ps_w1[:, 0:1], self.ones1_f[0:1, :], self.tiny[0:1, 4:5],
                                       start=True, stop=True), waits=[(sDVE, d2)])
        p = self.pe(lambda e: e.matmul(self.ps_w1[:, 1:2], self.ones1_f[0:1, :], self.tiny[0:1, 5:6],
                                       start=True, stop=True))
        a_sc = self.act(lambda e: e.copy(self.tiny[:, 6:8], self.ps_w1[:, 0:2]), waits=[(sPE, p)])
        d_band = self.dve(lambda e: e.tensor_scalar(self.band[:], self.band[:], self.tiny[:, 6:7],
                                                    self.tiny[:, 7:8], ALU.mult, ALU.add),
                          waits=[(sACT, a_sc), (dG, d_g0)])
        self.dve(lambda e: e.reduce_max(self.colmax[:, 2:3], self.band[:], axis=AX.X))
        d3 = self.dve(lambda e: e.tensor_copy(self.tscr[:, 0:1], self.colmax[:, 2:3]))
        p2 = self.pe(lambda e: e.transpose(self.ps_w1[0:1, 0:128], self.tscr[:, 0:1], self.ident_f[:]),
                     waits=[(sDVE, d3), (sACT, a_sc)])
        self.dve(lambda e: e.reduce_max(self.tiny[0:1, 8:9], self.ps_w1[0:1, 0:128], axis=AX.X),
                 waits=[(sPE, p2)])
        self.dve(lambda e: e.tensor_scalar_mul(self.tiny[0:1, 9:10], self.tiny[0:1, 8:9], 0.125))
        self.dve(lambda e: e.tensor_scalar_mul(self.tiny[0:1, 10:11], self.band[0:1, WB - 1:WB], 0.125))
        d_pre = self.dve(lambda e: e.tensor_scalar_mul(self.tiny[0:1, 11:12], self.band[0:1, 0:1], 0.125))
        self.band_ready = (sDVE, d_band)
        self.pre_done = (sDVE, d_pre)
        self.rel_w0 = (sACT, a_ws)
        self.rel_w1 = (sDVE, d_pre)

    def stats512(self, src, ready, rms=False, bsel=0, wp=False, wide_src=None):
        sPE, sACT, sDVE, sPOOL = self.sPE, self.sACT, self.sDVE, self.sPOOL
        h0 = bsel * 512
        B1 = self.rmB1[:, h0:h0 + 512]
        B2 = self.rmB2[:, h0:h0 + 512]
        if wp:
            pA, pB = self.ps_w0[:, 0:512], self.ps_w1[:, 0:512]
            relw = [self.rel_w0, self.rel_w1]
        else:
            pA, pB = self.ps_s0[:, 0:512], self.ps_s0[:, 512:1024]
            relw = [self.rel_s0]
        pl = 0
        if wide_src is not None:
            pl = self.pool(lambda e: e.tensor_mul(self.xsq[:, 0:4096], wide_src(), wide_src()),
                           waits=list(ready) + [self.xsq_rel])
        else:
            for c in range(NCH):
                pl = self.pool(lambda e, c=c: e.tensor_mul(
                    self.xsq[:, c * 512:(c + 1) * 512], src(c), src(c)),
                    waits=(list(ready) + [self.xsq_rel]) if c == 0 else ())
        pe_c = 0
        for c in range(NCH):
            pe_c = self.pe(lambda e, c=c, pA=pA: e.matmul(pA, self.ones128[:], src(c),
                                                          start=(c == 0), stop=(c == NCH - 1)),
                           waits=(list(ready) + relw) if c == 0 else ())
        for c in range(NCH):
            pe_c = self.pe(lambda e, c=c, pB=pB: e.matmul(pB, self.ones128[:],
                                                          self.xsq[:, c * 512:(c + 1) * 512],
                                                          start=(c == 0), stop=(c == NCH - 1)),
                           waits=[(sPOOL, pl)] if c == 0 else ())
        self.xsq_rel = (sPE, pe_c)
        pool_snap = self.sPOOL.n
        mS = self.tscr[:, h0:h0 + 512]
        d = self.dve(lambda e, pA=pA: e.tensor_scalar_mul(mS, pA, RD),
                     waits=[(sPE, pe_c), self.B_rel[bsel], (self.sPOOL, pool_snap)])
        d = self.dve(lambda e: e.tensor_mul(B2, mS, mS))
        vdst = self.tscr2[:, 512:1024] if rms else B2
        d_v = self.dve(lambda e, vdst=vdst, pB=pB: e.scalar_tensor_tensor(
            vdst, pB, RD, B2, ALU.mult, ALU.subtract))
        if wp:
            self.rel_w0 = (sDVE, d_v)
            self.rel_w1 = (sDVE, d_v)
        else:
            self.rel_s0 = (sDVE, d_v)
        a = self.act(lambda e, vdst=vdst: e.activation(B1, vdst, AF.Ln,
                                                       bias=self.eps_col[:, 0:1]),
                     waits=[(sDVE, d_v)])
        a = self.act(lambda e: e.activation(B2, B1, AF.Exp, scale=-0.5))
        dB = self.dve(lambda e: e.scalar_tensor_tensor(B1, mS, -1.0, B2, ALU.mult, ALU.mult),
                      waits=[(sACT, a)])
        if rms:
            d = self.dve(lambda e: e.tensor_mul(self.tscr[:, 512:1024], B2, B2))
            d = self.dve(lambda e: e.tensor_mul(self.tscr[:, 512:1024], self.tscr[:, 512:1024],
                                                vdst))
            a = self.act(lambda e: e.activation(self.tscr2[:, 512:1024], self.tscr[:, 512:1024],
                                                AF.Ln, bias=self.eps_col[:, 0:1]), waits=[(sDVE, d)])
            a = self.act(lambda e: e.activation(self.ainv_t[:, h0:h0 + 512],
                                                self.tscr2[:, 512:1024], AF.Exp, scale=-0.5))
            dB = self.dve(lambda e: e.engine_nop(), waits=[(sACT, a)])
        return dB

    def qkp(self, b):
        return (b % 2) * 2 * T

    def vtp(self, b):
        return (b % 2) * 32 * 132

    def _qkv_xv(self, c, nt):
        xoff = (nt % 2) * 4096
        return self.x_nt[:, xoff + c * 512: xoff + (c + 1) * 512]

    def _qkv_dma(self, b, nt):
        buf = nt % 2
        xoff = buf * 4096
        self.dgA[nt] = self.dma("gpsimd", "dX0" if buf == 0 else "dX1",
                                lambda e, b=b, nt=nt, xoff=xoff: e.dma_start(
            out=AP(self.x_nt, xoff, [[2 * NCH * 512, 128], [512, NCH], [1, 512]]),
            in_=self.d_xT[b, :, :, nt * 512:nt * 512 + 512].rearrange("c p t -> p c t")),
            waits=[self.xbuf_rel[buf]])

    def _qkv_stats_a(self, b, nt):
        sPE, sPOOL = self.sPE, self.sPOOL
        buf = nt % 2
        xoff = buf * 4096
        ready = [(self.dX[buf], self.dgA[nt])]
        pl = self.pool(lambda e, xoff=xoff: e.tensor_mul(
            self.xsq[:, 0:4096], self.x_nt[:, xoff:xoff + 4096],
            self.x_nt[:, xoff:xoff + 4096]), waits=ready + [self.xsq_rel])
        pe_c = 0
        for c in range(NCH):
            pe_c = self.pe(lambda e, c=c: e.matmul(
                self.ps_w0[:, 0:512], self.ones128[:], self._qkv_xv(c, nt),
                start=(c == 0), stop=(c == NCH - 1)),
                waits=(ready + [self.rel_w0, self.rel_w1]) if c == 0 else ())
        self._st_pl = pl
        self._st_pe = pe_c

    def _qkv_stats_b(self, b, nt):
        sPE, sACT, sDVE, sPOOL = self.sPE, self.sACT, self.sDVE, self.sPOOL
        buf = nt % 2
        h0 = buf * 512
        B1 = self.rmB1[:, h0:h0 + 512]
        B2 = self.rmB2[:, h0:h0 + 512]
        pe_c = 0
        for c in range(NCH):
            pe_c = self.pe(lambda e, c=c: e.matmul(
                self.ps_w1[:, 0:512], self.ones128[:], self.xsq[:, c * 512:(c + 1) * 512],
                start=(c == 0), stop=(c == NCH - 1)),
                waits=[(sPOOL, self._st_pl)] if c == 0 else ())
        self.xsq_rel = (sPE, pe_c)
        mS = self.tscr[:, h0:h0 + 512]
        d = self.dve(lambda e: e.tensor_scalar_mul(mS, self.ps_w0[:, 0:512], RD),
                     waits=[(sPE, pe_c), self.B_rel[buf]])
        d = self.dve(lambda e: e.tensor_mul(B2, mS, mS))
        d_v = self.dve(lambda e: e.scalar_tensor_tensor(
            B2, self.ps_w1[:, 0:512], RD, B2, ALU.mult, ALU.subtract))
        self.rel_w0 = (sDVE, d_v)
        self.rel_w1 = (sDVE, d_v)
        a = self.act(lambda e: e.activation(B1, B2, AF.Ln, bias=self.eps_col[:, 0:1]),
                     waits=[(sDVE, d_v)])
        a = self.act(lambda e: e.activation(B2, B1, AF.Exp, scale=-0.5))
        self.dBA[nt] = self.dve(lambda e: e.scalar_tensor_tensor(
            B1, mS, -1.0, B2, ALU.mult, ALU.mult), waits=[(sACT, a)])

    def _qkv_proj_a(self, b, nt, o):
        sPE, sDVE = self.sPE, self.sDVE
        psum, rel = ((self.ps_w0, self.rel_w0) if o != 1 else (self.ps_w1, self.rel_w1))
        pq = 0
        for c in range(4):
            pq = self.pe(lambda e, o=o, c=c, nt=nt, psum=psum: e.matmul(
                psum[:, 0:512],
                self.wsb[:, o * 1024 + c * 128: o * 1024 + (c + 1) * 128],
                self._qkv_xv(c, nt), start=(c == 0), stop=False),
                waits=[rel] if c == 0 else ())

    def _qkv_proj_b(self, b, nt, o):
        sPE, sDVE = self.sPE, self.sDVE
        buf = nt % 2
        h0 = buf * 512
        q0 = nt * 512
        qo = self.qkp(b)
        dests = (self.qk[:, qo + q0:qo + q0 + 512],
                 self.qk[:, qo + T + q0:qo + T + q0 + 512], None)
        psum = self.ps_w0 if o != 1 else self.ps_w1
        pq = 0
        for c in range(4, NCH):
            pq = self.pe(lambda e, o=o, c=c, nt=nt, psum=psum: e.matmul(
                psum[:, 0:512],
                self.wsb[:, o * 1024 + c * 128: o * 1024 + (c + 1) * 128],
                self._qkv_xv(c, nt), start=False, stop=False))
        pq = self.pe(lambda e, o=o, psum=psum: e.matmul(
            psum[:, 0:512], self.bias_row[0:1, o * 128:(o + 1) * 128],
            self.ones_row[0:1, 0:512], start=False, stop=True))
        d_u = self.dve(lambda e, psum=psum, h0=h0: e.tensor_mul(
            self.tscr2[:, 512:1024], psum[:, 0:512], self.rmB2[:, h0:h0 + 512]),
            waits=[(sPE, pq), (sDVE, self.dBA[nt])])
        dst = dests[o] if o != 2 else self.tscr2[:, 0:512]
        d_o = self.dve(lambda e, o=o, dst=dst, h0=h0: e.scalar_tensor_tensor(
            dst, self.rmB1[:, h0:h0 + 512], self.wsum[:, o:o + 1],
            self.tscr2[:, 512:1024], ALU.mult, ALU.add))
        self.fixA.setdefault(nt, {})[o] = d_o
        if o != 1:
            self.rel_w0 = (sDVE, d_u)
        else:
            self.rel_w1 = (sDVE, d_u)
        if o == 2:
            self.B_rel[buf] = (sDVE, d_o)
            self.xbuf_rel[buf] = (sPE, pq)

    def _qkv_vt(self, b, nt):
        sPE, sDVE = self.sPE, self.sDVE
        vo = self.vtp(b)
        pt = 0
        for i in range(4):
            pt = self.pe(lambda e, i=i: e.transpose(
                self.ps_w1[:, i * 128:(i + 1) * 128], self.tscr2[:, i * 128:(i + 1) * 128],
                self.ident_f[:]),
                waits=[(sDVE, self.fixA[nt][2]), self.rel_w1] if i == 0 else ())
        d_vt = self.dve(lambda e, nt=nt, vo=vo: e.tensor_copy(
            AP(self.vt, vo + nt * 4 * 132, [[2 * 32 * 132, 128], [132, 4], [66, 2], [1, 64]]),
            AP(self.ps_w1, 0, [[512, 128], [128, 4], [64, 2], [1, 64]])),
            waits=[(sPE, pt)])
        self.rel_w1 = (sDVE, d_vt)
        self.vt_rel = (sDVE, d_vt)

    def make_qkv_thunks(self, b, stage):
        th = []
        if stage == 0:
            p = b % 2
            th.append(lambda: self._qkv_dma(b, 0))
            th.append(lambda: self.dve(
                lambda e, p=p: e.memset(self.colmax[:, 4 * p:4 * p + 2], 0.0)))
            th.append(lambda: self._qkv_stats_a(b, 0))
            th.append(lambda: self._qkv_stats_b(b, 0))
            th.append(lambda: self._qkv_dma(b, 1))
            return th
        nt = stage - 1
        if stage < NT:
            th.append(lambda: self._qkv_stats_a(b, stage))
            th.append(lambda: self._qkv_stats_b(b, stage))
        for o in range(3):
            th.append(lambda o=o: self._qkv_proj_a(b, nt, o))
            th.append(lambda o=o: self._qkv_proj_b(b, nt, o))
        th.append(lambda: self._qkv_vt(b, nt))
        if stage + 1 < NT:
            th.append(lambda: self._qkv_dma(b, stage + 1))
        return th

    def _ub_norms(self, b, j):
        sPE, sACT, sDVE, sPOOL = self.sPE, self.sACT, self.sDVE, self.sPOOL
        p = b % 2
        qo = self.qkp(b)
        cm = 4 * p
        off = j * T
        pl = self.pool(lambda e, qo=qo, off=off: e.tensor_mul(
            self.xsq[:, 0:4096], self.qk[:, qo + off:qo + off + 4096],
            self.qk[:, qo + off:qo + off + 4096]),
            waits=[(sDVE, self.fixA[7][j]), self.xsq_rel])
        pn = 0
        for r in range(4):
            pn = self.pe(lambda e, r=r: e.matmul(
                self.ps_w0[0:1, 0:512], self.onescol_bf[:, 0:1],
                self.xsq[:, (2 * r) * 512:(2 * r + 1) * 512], start=True, stop=True),
                waits=[(sPOOL, pl), self.rel_w0] if r == 0 else [self.rel_w0])
            pn = self.pe(lambda e, r=r: e.matmul(
                self.ps_w1[0:1, 0:512], self.onescol_bf[:, 0:1],
                self.xsq[:, (2 * r + 1) * 512:(2 * r + 2) * 512], start=True, stop=True),
                waits=[self.rel_w1])
            dq = self.dve(lambda e, cm=cm: e.reduce_max(
                self.colmax[0:1, cm + 2:cm + 3],
                AP(self.ps_wpair, 0, [[1024, 1], [1, 1024]]), axis=AX.X),
                waits=[(sPE, pn)])
            self.dve(lambda e, j=j, cm=cm: e.tensor_max(
                self.colmax[0:1, cm + j:cm + j + 1], self.colmax[0:1, cm + j:cm + j + 1],
                self.colmax[0:1, cm + 2:cm + 3]))
            self.rel_w0 = (sDVE, dq)
            self.rel_w1 = (sDVE, dq)
        self.xsq_rel = (sPE, pn)

    def emit_ub(self, b):
        sPE, sACT, sDVE, sPOOL = self.sPE, self.sACT, self.sDVE, self.sPOOL
        p = b % 2
        cm = 4 * p
        d_k_last = self.sDVE.n
        d = self.dve(lambda e, cm=cm: e.tensor_mul(self.tiny[0:1, 14:15],
                                                   self.colmax[0:1, cm:cm + 1],
                                                   self.colmax[0:1, cm + 1:cm + 2]),
                     waits=[(sDVE, d_k_last), (sACT, self.ubc_act_rel)])
        a = self.act(lambda e: e.activation(self.tiny[0:1, 15:16], self.tiny[0:1, 14:15], AF.Ln),
                     waits=[(sDVE, d)])
        a = self.act(lambda e: e.activation(self.tiny[0:1, 14:15], self.tiny[0:1, 15:16], AF.Exp,
                                            scale=0.5))
        d = self.dve(lambda e: e.scalar_tensor_tensor(self.tiny[0:1, 15:16], self.tiny[0:1, 14:15],
                                                      0.125, self.tiny[0:1, 9:10], ALU.mult, ALU.add),
                     waits=[(sACT, a)])
        d = self.dve(lambda e: e.tensor_scalar_mul(self.tiny[0:1, 12:13], self.tiny[0:1, 15:16], -1.0))
        d = self.dve(lambda e: e.tensor_sub(self.tiny[0:1, 13:14], self.tiny[0:1, 10:11],
                                            self.tiny[0:1, 15:16]))
        d = self.dve(lambda e: e.tensor_sub(self.tiny[0:1, 14:15], self.tiny[0:1, 11:12],
                                            self.tiny[0:1, 15:16]))
        pb = self.pe(lambda e: e.matmul(self.ps_w0[:, 2:3], self.ones1_f[0:1, :],
                                        self.tiny[0:1, 12:13], start=True, stop=True),
                     waits=[(sDVE, d), self.rel_w0])
        pb = self.pe(lambda e: e.matmul(self.ps_w0[:, 3:4], self.ones1_f[0:1, :],
                                        self.tiny[0:1, 13:14], start=True, stop=True))
        pb = self.pe(lambda e: e.matmul(self.ps_w0[:, 4:5], self.ones1_f[0:1, :],
                                        self.tiny[0:1, 14:15], start=True, stop=True))
        a_ub = self.act(lambda e, p=p: e.copy(self.ubc[:, 4 * p:4 * p + 3], self.ps_w0[:, 2:5]),
                        waits=[(sPE, pb)])
        self.rel_w0 = (sACT, a_ub)
        self.ubc_act_rel = a_ub
        self.qkv_done[b] = [(sACT, a_ub), self.band_ready, self.vt_rel]

    def ub_thunks(self, b):
        return [lambda: self._ub_norms(b, 0), lambda: self._ub_norms(b, 1),
                lambda: self.emit_ub(b)]

    def emit_attn_qc(self, b, qc, thunks=()):
        thunks = list(thunks)
        if thunks:
            step = max(1, (NKT - 3) // len(thunks))
            points = set(2 + i * step for i in range(len(thunks)))
        else:
            points = set()
        sPE, sACT, sDVE, sPOOL = self.sPE, self.sACT, self.sDVE, self.sPOOL
        p = b % 2
        qo = self.qkp(b)
        vo = self.vtp(b)
        cm = 4 * p
        slots = (self.ps_s0, self.ps_s1)
        q0 = qc * 512
        pv_done = 0
        exp_of = {}

        def emit_pv(kt):
            nonlocal pv_done
            poff = (kt % 2) * 1024
            w = [(sACT, exp_of[kt])]
            if kt == 0:
                w.append(self.rel_ctx)
            self.pe(lambda e, kt=kt, poff=poff, vo=vo: e.matmul(
                self.ps_ctxA[:, :], self.vt[:, vo + kt * 132:vo + kt * 132 + 65],
                self.pbuf[:, poff:poff + 512], start=(kt == 0), stop=(kt == NKT - 1)), waits=w)
            pv_done = self.pe(lambda e, kt=kt, poff=poff, vo=vo: e.matmul(
                self.ps_ctxB[:, :], self.vt[:, vo + kt * 132 + 66:vo + kt * 132 + 131],
                self.pbuf[:, poff + 512:poff + 1024], start=(kt == 0), stop=(kt == NKT - 1)))

        for kt in range(NKT):
            sl = kt % 2
            pss = slots[sl]
            poff = sl * 1024
            cls = band_class(kt, qc)
            w = list(self.slot_rel[sl])
            if qc == 0 and kt < 2:
                w += self.qkv_done[b]
            self.pe(lambda e, pss=pss, kt=kt, q0=q0, cls=cls, qo=qo: e.matmul(
                pss[:, 0:512], self.qk[0:64, qo + T + kt * 128:qo + T + (kt + 1) * 128],
                self.qk[0:64, qo + q0:qo + q0 + 512], start=True, stop=(cls != 0),
                tile_position=(0, 0)), waits=w)
            ps_last = self.pe(lambda e, pss=pss, kt=kt, q0=q0, cls=cls, qo=qo: e.matmul(
                pss[:, 512:1024], self.qk[64:128, qo + T + kt * 128:qo + T + (kt + 1) * 128],
                self.qk[64:128, qo + q0:qo + q0 + 512], start=True, stop=(cls != 0),
                tile_position=(64, 0)))
            if cls == 0:
                s0_ = 3968 - kt * 128 + q0
                self.pe(lambda e, pss=pss, s0_=s0_: e.matmul(
                    pss[:, 0:512], self.ident_bf[:], self.band[:, s0_:s0_ + 512],
                    start=False, stop=True))
                ps_last = self.pe(lambda e, pss=pss, s0_=s0_: e.matmul(
                    pss[:, 512:1024], self.ident_bf[:], self.band[:, s0_:s0_ + 512],
                    start=False, stop=True))
            a_exp = self.act(lambda e, pss=pss, poff=poff, cls=cls, cm=cm: e.activation(
                self.pbuf[:, poff:poff + 1024], pss[:, 0:1024], AF.Exp, scale=0.125,
                bias=self.ubc[:, cm + cls:cm + cls + 1]), waits=[(sPE, ps_last)])
            exp_of[kt] = a_exp
            self.slot_rel[sl] = [(sACT, a_exp)]
            if kt >= 1:
                emit_pv(kt - 1)
            if thunks and kt in points:
                thunks.pop(0)()
        while thunks:
            thunks.pop(0)()
        emit_pv(NKT - 1)
        a_r = self.act(lambda e: e.activation(self.ps_w0[0:1, 0:512], self.ps_ctxA[64:65, 0:512],
                                              AF.Ln), waits=[(sPE, pv_done), self.rel_w0,
                                                             self.rel_w1])
        a_r = self.act(lambda e: e.activation(self.ps_w1[0:1, 0:512], self.ps_ctxB[64:65, 0:512],
                                              AF.Ln), waits=[self.ctxAB_rel])
        a_r = self.act(lambda e: e.activation(self.reciprow[0:1, 0:1024],
                                              self.ps_wpair[0:1, 0:1024], AF.Exp, scale=-1.0))
        pb = self.pe(lambda e: e.matmul(self.ps_w0[0:64, 0:512], self.ones_row[0:1, 0:64],
                                        self.reciprow[0:1, 0:512], start=True, stop=True),
                     waits=[(sACT, a_r)])
        pb = self.pe(lambda e: e.matmul(self.ps_w1[0:64, 0:512], self.ones_row[0:1, 0:64],
                                        self.reciprow[0:1, 512:1024], start=True, stop=True))
        a_e = self.act(lambda e: e.copy(self.recb[0:64, 0:512], self.ps_w0[0:64, 0:512]),
                       waits=[(sPE, pb)])
        a_e = self.act(lambda e: e.copy(self.recb[64:128, 0:512], self.ps_w1[0:64, 0:512]))
        d_norm = self.dve(lambda e, q0=q0: e.tensor_mul(
            self.ctxAB[0:64, q0:q0 + 512], self.ps_ctxA[0:64, 0:512],
            self.recb[0:64, 0:512]), waits=[(sACT, a_e)])
        d_norm = self.dve(lambda e, q0=q0: e.tensor_mul(
            self.ctxAB[64:128, q0:q0 + 512], self.ps_ctxB[0:64, 0:512],
            self.recb[64:128, 0:512]))
        self.rel_w0 = (sACT, a_e)
        self.rel_w1 = (sACT, a_e)
        self.rel_ctx = (sDVE, d_norm)
        self.d_norm_last = d_norm

    def _scr_part(self, b, part, dn):
        sPOOL, sDVE = self.sPOOL, self.sDVE
        pl = 0
        for idx in range(part * 4, part * 4 + 4):
            h, s16 = divmod(idx, 16)
            g = 8 * h + b
            pch = s16 // 2
            pb = 64 * (s16 % 2)
            pl = self.pool(lambda e, g=g, h=h, s16=s16, pch=pch, pb=pb: e.tensor_copy(
                self.scr[pb:pb + 64, pch * T + g * 256: pch * T + (g + 1) * 256],
                self.ctxAB[h * 64:(h + 1) * 64, s16:T:16]),
                waits=[(sDVE, dn)] if idx == part * 4 else ())
        if part == 7:
            self.ctxAB_rel = (sPOOL, pl)

    def scr_thunks(self, b):
        dn = self.d_norm_last
        return [lambda part=part: self._scr_part(b, part, dn) for part in range(8)]

    def emit_scr(self, b):
        dn = self.d_norm_last
        for part in range(8):
            self._scr_part(b, part, dn)

    def phaseA(self):
        self.dgA = {}
        self.dBA = {}
        self.fixA = {}
        self.vt_rel = (self.sPE, 0)
        self.slot_rel = [[self.rel_s0], [self.rel_s1]]
        for stage in range(NT + 1):
            for t in self.make_qkv_thunks(0, stage):
                t()
        for t in self.ub_thunks(0):
            t()
        for b in range(B):
            for qc in range(NT):
                th = []
                if b + 1 < B:
                    th += self.make_qkv_thunks(b + 1, qc)
                    if qc == NT - 1:
                        th += self.make_qkv_thunks(b + 1, NT) + self.ub_thunks(b + 1)
                if b >= 1 and qc == 0:
                    th = th[:1] + self.scr_thunks(b - 1) + th[1:]
                self.emit_attn_qc(b, qc, th)
            if b == B - 1:
                self.emit_scr(b)
        self.qkv_ready = (self.sPE, self.sPE.n)
        self.rel_s0 = self.slot_rel[0][0]
        self.rel_s1 = self.slot_rel[1][0]

    def close512(self, src, q0, extra_ready=(), dst=None, bsel=0):
        h0 = bsel * 512
        """stream[c][:, q0:q0+512] (or dst chunks) += src(c)*rmB2 + rmB1."""
        sPOOL, sDVE, sACT = self.sPOOL, self.sDVE, self.sACT
        d_cl = 0
        d_par = [self.sDVE.n, self.sDVE.n]
        for c in range(NCH):
            s0 = (c % 2) * 512
            pl = self.pool(lambda e, c=c, h0=h0, s0=s0: e.tensor_mul(
                self.bscr[:, s0:s0 + 512], src(c), self.rmB2[:, h0:h0 + 512]),
                waits=(list(extra_ready) if c < 2 else []) + [(sDVE, d_par[c % 2])])
            self.dve(lambda e, h0=h0, s0=s0: e.tensor_add(
                self.bscr[:, s0:s0 + 512], self.bscr[:, s0:s0 + 512],
                self.rmB1[:, h0:h0 + 512]), waits=[(sPOOL, pl)])
            tgt = self.stream[c][:, q0:q0 + 512] if dst is None else dst(c)
            d_cl = self.dve(lambda e, tgt=tgt, s0=s0: e.tensor_add(
                tgt, tgt, self.bscr[:, s0:s0 + 512]))
            d_par[c % 2] = d_cl
        self.B_rel[bsel] = (sDVE, d_cl)
        return d_cl

    def phase_wo_close(self):
        sPE, sACT, sDVE, sPOOL, dG = self.sPE, self.sACT, self.sDVE, self.sPOOL, self.dG
        dg_w = self.wload3("gpsimd", "dG", self.wslot[0], 0, self.d_woT, D,
                           waits=[self.wslot_rel[0]])
        dg_x = {}
        for c in range(NCH):
            dg_x[c] = self.dma("gpsimd", "dT", lambda e, c=c: e.dma_start(
                out=self.stream[c][:], in_=self.d_xTk[c]),
                waits=[self.stream_rel[c]])
        d_close = 0
        for quar in range(4):
            qq0 = quar * 1024
            pool_snap = self.sPOOL.n
            a_y = 0
            for j in range(NCH):
                for ntl in range(2):
                    q0 = qq0 + ntl * 512
                    even = (j * 2 + ntl) % 2 == 0
                    psum = self.ps_w0 if even else self.ps_w1
                    rel = self.rel_w0 if even else self.rel_w1
                    p = 0
                    for c in range(NCH):
                        p = self.pe(lambda e, j=j, c=c, q0=q0, psum=psum: e.matmul(
                            psum[:, 0:512],
                            self.wslot[0][:, c * D + j * 128: c * D + (j + 1) * 128],
                            self.scr[:, c * T + q0: c * T + q0 + 512],
                            start=(c == 0), stop=(c == NCH - 1)),
                            waits=([rel, (dG, dg_w)] +
                                   ([self.qkv_ready, self.ctxAB_rel]
                                    if (quar == 0 and j == 0 and ntl == 0) else []))
                            if c == 0 else ())
                    a_y = self.act(lambda e, j=j, ntl=ntl, psum=psum: e.activation(
                        self.qk[:, j * 1024 + ntl * 512: j * 1024 + ntl * 512 + 512],
                        psum[:, 0:512], AF.Identity, bias=self.bo_col[:, j:j + 1]),
                        waits=[(sPE, p), (sPOOL, pool_snap)])
                    if even:
                        self.rel_w0 = (sACT, a_y)
                    else:
                        self.rel_w1 = (sACT, a_y)
            for ntl in range(2):
                q0 = qq0 + ntl * 512

                def yv(c, ntl=ntl):
                    return self.qk[:, c * 1024 + ntl * 512: c * 1024 + ntl * 512 + 512]

                dB = self.stats512(yv, [(sACT, a_y)])
                xw = [(self.sems["dT"], dg_x[NCH - 1]), (sDVE, dB)]
                d_close = self.close512(yv, q0, extra_ready=xw)
        self.wslot_rel[0] = (sPE, self.sems["PE"].n)
        self.stream_written = (sDVE, d_close)
        self.qk_rel = (sDVE, d_close)

    def apply512(self, src, dst, q0, extra=None, bsel=0):
        h0 = bsel * 512
        """dst(c) (bf16) = src(c)*rmB2 + rmB1 (extra: per-c second multiply col/tile)."""
        sPOOL, sDVE = self.sPOOL, self.sDVE
        d_par = [self.sDVE.n, self.sDVE.n]
        d = 0
        for c in range(NCH):
            s0 = (c % 2) * 512
            pl = self.pool(lambda e, c=c, h0=h0, s0=s0: e.tensor_mul(
                self.bscr[:, s0:s0 + 512], src(c), self.rmB2[:, h0:h0 + 512]),
                waits=[(sDVE, d_par[c % 2])])
            d = self.dve(lambda e, c=c, h0=h0, s0=s0: e.tensor_add(
                dst(c), self.bscr[:, s0:s0 + 512], self.rmB1[:, h0:h0 + 512]),
                waits=[(sPOOL, pl)])
            d_par[c % 2] = d
        self.B_rel[bsel] = (sDVE, d)
        return d

    def phaseF(self):
        sPE, sACT, sDVE, sPOOL, dG, dS = self.sPE, self.sACT, self.sDVE, self.sPOOL, self.dG, self.dS
        self.wload3("gpsimd", "dG", self.wslot[0], 0, self.d_fwin, 64)
        dg_w = self.dma("gpsimd", "dG", lambda e: e.dma_start(
            out=self.wslot[1][0:64, 0:D], in_=self.d_fwout))
        ds = self.dma("sync", "dS", lambda e: e.dma_start(out=self.fw12[:, 0:16], in_=self.d_fw1))
        ds = self.dma("sync", "dS", lambda e: e.dma_start(out=self.fw12[:, 16:32], in_=self.d_fb1))
        ds = self.dma("sync", "dS", lambda e: e.dma_start(out=self.fw12[:, 32:48], in_=self.d_fw2))
        ds = self.dma("sync", "dS", lambda e: e.dma_start(
            out=self.svec[0:64, 0:1], in_=self.d_fbin.rearrange("a m -> m a"),
            allow_slow_non_contiguous=True))
        ds = self.dma("sync", "dS", lambda e: e.dma_start(
            out=self.svec[0:64, 1:2], in_=self.d_fb2.rearrange("a m -> m a"),
            allow_slow_non_contiguous=True))
        ds = self.dma("sync", "dS", lambda e: e.dma_start(
            out=self.svec[:, 8:16], in_=self.d_fbout.rearrange("c p -> p c"),
            allow_slow_non_contiguous=True))
        stg = [self.ck["aa"], self.ck["bb"]]
        up = {}
        i = 0
        for widx, wsrc in enumerate((self.d_cwog, self.d_cwx, self.d_cwA, self.d_cwsg,
                                     self.d_cwo, self.d_dwrT, self.d_dwkT, self.d_dwvT)):
            for hh in range(2):
                w = ([self.qk_rel, self.wslot_rel[0]] if i < 2
                     else [(self.dS, up[i - 2])])
                dnc = self.wload3("gpsimd", "dX0" if i % 2 == 0 else "dX1",
                                  stg[i % 2], 0, wsrc[4 * hh:4 * hh + 4], D, nch=4,
                                  waits=w)
                up[i] = self.dma("sync", "dS",
                                 lambda e, widx=widx, hh=hh, s=stg[i % 2]: e.dma_start(
                                     out=self.d_wbf[widx, 4 * hh:4 * hh + 4].rearrange(
                                         "c p m -> p c m"),
                                     in_=AP(s, 0, [[NCH * 512, 128], [1024, 4], [1, 1024]])),
                                 waits=[(self.dX[i % 2], dnc)])
                i += 1
        self.precast_done = (self.dS, up[15])
        xn, outf = self.ck["xn"], self.ck["xm"]

        def sv(c, q0):
            return self.stream[c][:, q0:q0 + 512]

        dBF = {}
        dBF[0] = self.stats512(lambda c: sv(c, 0), [(sDVE, 1)], bsel=0)
        for cq in range(NCH):
            q0 = cq * 512
            bs = cq % 2
            if cq + 1 < NCH:
                dBF[cq + 1] = self.stats512(lambda c, q1=q0 + 512: sv(c, q1),
                                            [(sDVE, 1)], bsel=(cq + 1) % 2)
            d_xn = self.apply512(lambda c, q0=q0: sv(c, q0),
                                 lambda c: xn[:, c * 512:(c + 1) * 512], q0, bsel=bs)
            p = 0
            for c in range(NCH):
                p = self.pe(lambda e, c=c: e.matmul(
                    self.ps_w0[0:64, 0:512], self.wslot[0][:, c * 64:(c + 1) * 64],
                    xn[:, c * 512:(c + 1) * 512], start=(c == 0), stop=(c == NCH - 1)),
                    waits=[(sDVE, d_xn), (dG, dg_w), self.rel_w0] if c == 0 else ())
            a_s = self.act(lambda e: e.activation(self.fmid_bf[0:64, 0:512], self.ps_w0[0:64, 0:512],
                                                  AF.Identity, bias=self.svec[0:64, 0:1]),
                           waits=[(sPE, p), (dS, ds)])
            self.rel_w0 = (sACT, a_s)
            a_g = self.act(lambda e: e.activation(self.fmid[0:64, 0:512], self.fmid_bf[0:64, 0:512],
                                                  AF.Identity, scale=0.0, bias=self.svec[0:64, 1:2]))
            d_g = 0
            a_r = {}
            for i in range(16):
                ro = 512 * (1 + i % 3)
                w = []
                if i >= 3:
                    w.append((sDVE, d_g - (2 if False else 2)))
                a_r[i] = self.act(lambda e, i=i, ro=ro: e.activation(
                    self.fmid_bf[0:64, ro:ro + 512], self.fmid_bf[0:64, 0:512], AF.Relu,
                    scale=self.fw12[0:64, i:i + 1], bias=self.fw12[0:64, 16 + i:17 + i]),
                    waits=[(sDVE, max(0, d_g - 2))] if i >= 3 else ())
                d_g = self.dve(lambda e, i=i, ro=ro: e.scalar_tensor_tensor(
                    self.fmid[0:64, 0:512], self.fmid_bf[0:64, ro:ro + 512],
                    self.fw12[0:64, 32 + i:33 + i], self.fmid[0:64, 0:512],
                    ALU.mult, ALU.add), waits=[(sACT, a_r[i]), (sACT, a_g)] if i == 0 else [(sACT, a_r[i])])
            d_g = self.dve(lambda e: e.tensor_copy(self.fmid_bf[0:64, 0:512],
                                                   self.fmid[0:64, 0:512]))
            a_o = 0
            for j in range(NCH):
                psum = self.ps_w0 if j % 2 == 0 else self.ps_w1
                rel = self.rel_w0 if j % 2 == 0 else self.rel_w1
                p = self.pe(lambda e, j=j, psum=psum: e.matmul(
                    psum[:, 0:512], self.wslot[1][0:64, j * 128:(j + 1) * 128],
                    self.fmid_bf[0:64, 0:512], start=True, stop=True),
                    waits=[(sDVE, d_g), rel])
                a_o = self.act(lambda e, j=j, psum=psum: e.activation(
                    outf[:, j * 512:(j + 1) * 512], psum[:, 0:512], AF.Identity,
                    bias=self.svec[:, 8 + j:9 + j]), waits=[(sPE, p)])
                if j % 2 == 0:
                    self.rel_w0 = (sACT, a_o)
                else:
                    self.rel_w1 = (sACT, a_o)
            dBo = self.stats512(lambda c: outf[:, c * 512:(c + 1) * 512], [(sACT, a_o)], bsel=bs)
            self.close512(lambda c: outf[:, c * 512:(c + 1) * 512], q0,
                          extra_ready=[(sDVE, dBo)], bsel=bs)
        self.wslot_rel = [(sPE, self.sPE.n), (sPE, self.sPE.n)]

    def phaseC(self):
        sPE, sACT, sDVE, sPOOL, dG, dS = self.sPE, self.sACT, self.sDVE, self.sPOOL, self.dG, self.dS
        dW1 = self.sems["dW1"]
        ds = self.dma("sync", "dS", lambda e: e.dma_start(
            out=self.cv[:, 0:8], in_=self.d_crmsw.rearrange("c p -> p c"),
            allow_slow_non_contiguous=True))
        ds = self.dma("sync", "dS", lambda e: e.dma_start(
            out=self.cv[:, 8:16], in_=self.d_cbA.rearrange("c p -> p c"),
            allow_slow_non_contiguous=True))
        d_c0 = self.dve(lambda e: e.memset(self.carry[:, 0:8], 0.0))
        xm, xn, aa, bb, gg = (self.ck[n] for n in ("xm", "xn", "aa", "bb", "gg"))
        slot_rel = list(self.wslot_rel)

        def sv(c, q0):
            return self.stream[c][:, q0:q0 + 512]

        dBC = {}
        dBC[0] = self.stats512(lambda c: sv(c, 0), [(sDVE, 1)], rms=True, bsel=0)
        d_h_prev = 0
        for cq in range(NCH):
            q0 = cq * 512
            bs = cq % 2
            wq5 = ("dG", "dW1", "dX0", "dX1", "dT")
            pc = [self.precast_done] if cq == 0 else []
            wdma = []
            for widx in range(2):
                cnt = self.wload3("gpsimd", wq5[widx], self.wslot[widx % 2], 0,
                                  self.d_wbf[widx], D, waits=[slot_rel[widx % 2]] + pc)
                wdma.append((self.sems[wq5[widx]], cnt))
            d_xm = self.apply512(lambda c, q0=q0: sv(c, q0),
                                 lambda c: xm[:, c * 512:(c + 1) * 512], q0, bsel=bs)
            d_xn = 0
            for c in range(NCH):
                d_xn = self.dve(lambda e, c=c, bs=bs: e.scalar_tensor_tensor(
                    xn[:, c * 512:(c + 1) * 512], xm[:, c * 512:(c + 1) * 512],
                    self.cv[:, c:c + 1], self.ainv_t[:, bs * 512:bs * 512 + 512],
                    ALU.mult, ALU.mult),
                    waits=[(dS, ds)] if c == 0 else ())

            def proj(widx, wsl, dgv, act, dst, bias=None):
                act_snap = self.sACT.n
                lastd = 0
                for j in range(NCH):
                    psum = self.ps_w0 if j % 2 == 0 else self.ps_w1
                    rel = self.rel_w0 if j % 2 == 0 else self.rel_w1
                    p = 0
                    for c in range(NCH):
                        p = self.pe(lambda e, j=j, c=c, psum=psum, wsl=wsl: e.matmul(
                            psum[:, 0:512],
                            self.wslot[wsl][:, c * D + j * 128: c * D + (j + 1) * 128],
                            xn[:, c * 512:(c + 1) * 512],
                            start=(c == 0), stop=(c == NCH - 1)),
                            waits=[(sDVE, d_xn), dgv, rel] if c == 0 else ())
                    if act == "sig":
                        lastd = self.act(lambda e, j=j, psum=psum: e.activation(
                            dst(j), psum[:, 0:512], AF.Sigmoid,
                            bias=(self.cv[:, 8 + j:9 + j] if bias else 0.0)),
                            waits=[(sPE, p), (sDVE, d_h_prev)])
                        relv = (sACT, lastd)
                    else:
                        lastd = self.dve(lambda e, j=j, psum=psum: e.tensor_mul(
                            dst(j), psum[:, 0:512], bb[:, j * 512:(j + 1) * 512]),
                            waits=[(sPE, p), (sACT, act_snap)])
                        relv = (sDVE, lastd)
                    if j % 2 == 0:
                        self.rel_w0 = relv
                    else:
                        self.rel_w1 = relv
                return lastd

            a_sg = proj(0, 0, wdma[0], "sig", lambda j: bb[:, j * 512:(j + 1) * 512])
            if cq + 1 < NCH:
                dBC[cq + 1] = self.stats512(lambda c, q1=q0 + 512: sv(c, q1),
                                            [(sDVE, 1)], rms=True, bsel=(cq + 1) % 2)
            wdma.append((self.sems["dX0"],
                         self.wload3("gpsimd", "dX0", self.wslot[0], 0, self.d_wbf[2], D,
                                     waits=[self.rel_w0, self.rel_w1] + pc)))
            d_b = proj(1, 1, wdma[1], "mulsg", lambda j: bb[:, j * 512:(j + 1) * 512])
            wdma.append((self.sems["dX1"],
                         self.wload3("gpsimd", "dX1", self.wslot[1], 0, self.d_wbf[3], D,
                                     waits=[self.rel_w0, self.rel_w1] + pc)))
            a_A = proj(2, 0, wdma[2], "sig", lambda j: aa[:, j * 512:(j + 1) * 512], bias=True)
            wdma.append((self.sems["dT"],
                         self.wload3("gpsimd", "dT", self.wslot[0], 0, self.d_wbf[4], D,
                                     waits=[self.rel_w0, self.rel_w1] + pc)))
            a_gate = proj(3, 1, wdma[3], "sig", lambda j: gg[:, j * 512:(j + 1) * 512])
            pe_xn_done = self.sPE.n
            d_h = 0
            for c in range(NCH):
                d_h = self.dve(lambda e, c=c: e.tensor_tensor_scan(
                    xn[:, c * 512:(c + 1) * 512], aa[:, c * 512:(c + 1) * 512],
                    bb[:, c * 512:(c + 1) * 512], self.carry[:, c:c + 1],
                    ALU.mult, ALU.add),
                    waits=([(sACT, a_A), (sDVE, d_b), (sPE, pe_xn_done)] if c == 0 else ()))
                d_h = self.dve(lambda e, c=c: e.tensor_copy(
                    self.carry[:, c:c + 1], xn[:, c * 512 + 511:c * 512 + 512]))
            d_h_prev = d_h
            d_y = 0
            for c in range(NCH):
                d_y = self.dve(lambda e, c=c: e.tensor_mul(
                    gg[:, c * 512:(c + 1) * 512], gg[:, c * 512:(c + 1) * 512],
                    xn[:, c * 512:(c + 1) * 512]), waits=[(sACT, a_gate)] if c == 0 else ())
            d_oc = 0
            for j in range(NCH):
                psum = self.ps_w0 if j % 2 == 0 else self.ps_w1
                rel = self.rel_w0 if j % 2 == 0 else self.rel_w1
                p = 0
                for c in range(NCH):
                    p = self.pe(lambda e, j=j, c=c, psum=psum: e.matmul(
                        psum[:, 0:512],
                        self.wslot[0][:, c * D + j * 128: c * D + (j + 1) * 128],
                        gg[:, c * 512:(c + 1) * 512],
                        start=(c == 0), stop=(c == NCH - 1)),
                        waits=[(sDVE, d_y), wdma[4], rel] if c == 0 else ())
                d_oc = self.dve(lambda e, j=j, psum=psum: e.tensor_add(
                    xm[:, j * 512:(j + 1) * 512], psum[:, 0:512],
                    xm[:, j * 512:(j + 1) * 512]), waits=[(sPE, p)])
                if j % 2 == 0:
                    self.rel_w0 = (sDVE, d_oc)
                else:
                    self.rel_w1 = (sDVE, d_oc)
            slot_rel = [self.rel_w0, self.rel_w1]
            dBo = self.stats512(lambda c: xm[:, c * 512:(c + 1) * 512], [(sDVE, d_oc)], bsel=bs)
            self.close512(lambda c: xm[:, c * 512:(c + 1) * 512], q0,
                          extra_ready=[(sDVE, dBo)], bsel=bs)
        self.wslot_rel = [(sPE, self.sPE.n), (sPE, self.sPE.n)]

    def phaseD(self):
        sPE, sACT, sDVE, sPOOL, dG, dS = self.sPE, self.sACT, self.sDVE, self.sPOOL, self.dG, self.dS
        dW1 = self.sems["dW1"]
        ds = self.dma("sync", "dS", lambda e: e.dma_start(
            out=self.cv[:, 0:8], in_=self.d_mur.rearrange("c p -> p c"),
            allow_slow_non_contiguous=True))
        ds = self.dma("sync", "dS", lambda e: e.dma_start(
            out=self.cv[:, 8:16], in_=self.d_muk.rearrange("c p -> p c"),
            allow_slow_non_contiguous=True))
        d_mu = 0
        for h, (src_c, dst_c) in enumerate(((0, 0), (8, 8))):
            d_mu = self.dve(lambda e, src_c=src_c, dst_c=dst_c: e.tensor_scalar(
                self.svec[:, dst_c:dst_c + 8], self.cv[:, src_c:src_c + 8], -1.0, 1.0,
                ALU.mult, ALU.add), waits=[(dS, ds)] if h == 0 else ())
        d_cd = self.dve(lambda e: e.memset(self.carry[:, 8:16], 0.0))
        xms, xr, xk, rr, ka = self.ck_xms, self.ck["aa"], self.ck["bb"], self.ck["gg"], self.ck["xn"]
        od = self.ck["xm"]
        slot_rel = list(self.wslot_rel)

        def sv(c, q0):
            return self.stream[c][:, q0:q0 + 512]

        dBD = {}
        dBD[0] = self.stats512(lambda c: sv(c, 0), [(sDVE, d_mu)], bsel=0)
        for cq in range(NCH):
            q0 = cq * 512
            bs = cq % 2
            wdma_r = (self.dG, self.wload3("gpsimd", "dG", self.wslot[0], 0,
                                           self.d_wbf[5], D, waits=[slot_rel[0]]))
            wdma_k = (self.sems["dW1"], self.wload3("gpsimd", "dW1", self.wslot[1], 0,
                                                    self.d_wbf[6], D, waits=[slot_rel[1]]))
            d_sh = 0
            for c in range(NCH):
                d_sh = self.dve(lambda e, c=c: e.tensor_copy(
                    xms[:, c * 513:c * 513 + 1], self.carry[:, 8 + c:9 + c]),
                    waits=[(sDVE, d_cd)] if (cq == 0 and c == 0) else ())
            d_xm = self.apply512(lambda c, q0=q0: sv(c, q0),
                                 lambda c: xms[:, c * 513 + 1:c * 513 + 513], q0, bsel=bs)
            for c in range(NCH):
                d_xm = self.dve(lambda e, c=c: e.tensor_copy(
                    self.carry[:, 8 + c:9 + c], xms[:, c * 513 + 512:c * 513 + 513]))
            if cq + 1 < NCH:
                dBD[cq + 1] = self.stats512(lambda c, q1=q0 + 512: sv(c, q1),
                                            [(sDVE, 1)], bsel=(cq + 1) % 2)
            d_mix = 0
            for c in range(NCH):
                cur = xms[:, c * 513 + 1:c * 513 + 513]
                prv = xms[:, c * 513:c * 513 + 512]
                d1 = self.dve(lambda e, c=c, prv=prv: e.tensor_scalar_mul(
                    self.tscr2[:, 0:512], prv, self.svec[:, c:c + 1]))
                d_mix = self.dve(lambda e, c=c, cur=cur: e.scalar_tensor_tensor(
                    xr[:, c * 512:(c + 1) * 512], cur, self.cv[:, c:c + 1],
                    self.tscr2[:, 0:512], ALU.mult, ALU.add))
                d1 = self.dve(lambda e, c=c, prv=prv: e.tensor_scalar_mul(
                    self.tscr2[:, 0:512], prv, self.svec[:, 8 + c:9 + c]))
                d_mix = self.dve(lambda e, c=c, cur=cur: e.scalar_tensor_tensor(
                    xk[:, c * 512:(c + 1) * 512], cur, self.cv[:, 8 + c:9 + c],
                    self.tscr2[:, 0:512], ALU.mult, ALU.add))

            def proj2(wsl, dgv, src2, fin, waits0=()):
                lastd = 0
                for j in range(NCH):
                    psum = self.ps_w0 if j % 2 == 0 else self.ps_w1
                    rel = self.rel_w0 if j % 2 == 0 else self.rel_w1
                    p = 0
                    for c in range(NCH):
                        p = self.pe(lambda e, j=j, c=c, psum=psum, wsl=wsl: e.matmul(
                            psum[:, 0:512],
                            self.wslot[wsl][:, c * D + j * 128: c * D + (j + 1) * 128],
                            src2[:, c * 512:(c + 1) * 512],
                            start=(c == 0), stop=(c == NCH - 1)),
                            waits=(list(waits0) + [dgv, rel]) if c == 0 else ())
                    lastd, relv = fin(j, psum, p)
                    if j % 2 == 0:
                        self.rel_w0 = relv
                    else:
                        self.rel_w1 = relv
                return lastd

            def fin_r(j, psum, p):
                a = self.act(lambda e, j=j, psum=psum: e.activation(
                    rr[:, j * 512:(j + 1) * 512], psum[:, 0:512], AF.Sigmoid),
                    waits=[(sPE, p)])
                return a, (sACT, a)

            k_prev_act = [0]

            def fin_k(j, psum, p):
                d1 = self.dve(lambda e, psum=psum: e.tensor_scalar_max(
                    self.tscr2[:, 512:1024], psum[:, 0:512], 0.0),
                    waits=[(sPE, p), (sACT, k_prev_act[0])])
                a = self.act(lambda e, j=j: e.activation(
                    ka[:, j * 512:(j + 1) * 512], self.tscr2[:, 512:1024], AF.Square),
                    waits=[(sDVE, d1)])
                k_prev_act[0] = a
                return a, (sDVE, d1)

            def fin_v(j, psum, p):
                d1 = self.dve(lambda e, j=j, psum=psum: e.tensor_mul(
                    od[:, j * 512:(j + 1) * 512], psum[:, 0:512],
                    rr[:, j * 512:(j + 1) * 512]), waits=[(sPE, p)])
                return d1, (sDVE, d1)

            a_r = proj2(0, wdma_r, xr, fin_r, waits0=[(sDVE, d_mix)])
            a_k = proj2(1, wdma_k, xk, fin_k)
            wdma_v = (self.dX[0], self.wload3("gpsimd", "dX0", self.wslot[0], 0,
                                              self.d_wbf[7], D,
                                              waits=[self.rel_w0, self.rel_w1]))
            d_v = proj2(0, wdma_v, ka, fin_v, waits0=[(sACT, self.sACT.n)])
            slot_rel = [self.rel_w0, self.rel_w1]
            dBo = self.stats512(lambda c: od[:, c * 512:(c + 1) * 512], [(sDVE, d_v)], bsel=bs)
            self.close512(lambda c: od[:, c * 512:(c + 1) * 512], q0,
                          extra_ready=[(sDVE, dBo)], bsel=bs)
        self.wslot_rel = [(sPE, self.sPE.n), (sPE, self.sPE.n)]

    def phase_out(self):
        sPE, sACT, sDVE, sST = self.sPE, self.sACT, self.sDVE, self.sST
        st_after = {}
        st_count = 0
        pe_prev = 0
        for tt in range(32):
            t0 = tt * 128
            ob = tt % 2
            d_c = 0
            for c in range(NCH):
                half = self.tscr if c < 4 else self.tscr2
                d_c = self.dve(lambda e, c=c, t0=t0, half=half: e.tensor_copy(
                    half[:, (c % 4) * 128:(c % 4 + 1) * 128], self.stream[c][:, t0:t0 + 128]),
                    waits=([self.stream_written, (sPE, pe_prev), self.B_rel[0], self.B_rel[1]]
                           if c == 0 else ()))
            p = 0
            for c in range(NCH):
                half = self.tscr if c < 4 else self.tscr2
                psum = self.ps_w0 if c < 4 else self.ps_w1
                p = self.pe(lambda e, c=c, half=half, psum=psum: e.transpose(
                    psum[:, (c % 4) * 128:(c % 4 + 1) * 128],
                    half[:, (c % 4) * 128:(c % 4 + 1) * 128], self.ident_f[:]),
                    waits=([(sDVE, d_c), self.rel_w0, self.rel_w1] if c == 0 else ()))
            pe_prev = p
            a = self.act(lambda e, ob=ob: e.copy(self.outstage[ob][:, 0:512], self.ps_w0[:, 0:512]),
                         waits=[(sPE, p)] + ([(sST, st_after.get(tt - 1, 0))] if tt >= 2 else []))
            a = self.act(lambda e, ob=ob: e.copy(self.outstage[ob][:, 512:1024], self.ps_w1[:, 0:512]))
            self.rel_w0 = (sACT, a)
            self.rel_w1 = (sACT, a)
            st_count = self.dma("sync", "ST", lambda e, t0=t0, ob=ob: e.dma_start(
                out=self.d_out[t0:t0 + 128, :], in_=self.outstage[ob][:]),
                waits=[(sACT, a)])
            st_after[tt] = st_count
        if self.on("sync"):
            self.e.wait_ge(self.sST.h, st_count)

    def build(self):
        blk_cm = self.nc.Block()
        blk = blk_cm.__enter__()
        with self.nc.allow_low_precision(reason="bf16 operands are intentional"):
            for engn in ENGS:
                getattr(blk, engn)(lambda e, _n=engn: self.emit(_n, e))
        blk_cm.__exit__(None, None, None)
        first = self._pass_counts[0]
        for cts in self._pass_counts[1:]:
            assert cts == first, (cts, first)
        return self.nc


NCORES = 8


def _host_prepare(inputs, core, xT, tb2):
    f = np.float32
    r0 = 128 * core
    d = dict(
        xT=xT,
        xTk=xT[core],
        wq=np.ascontiguousarray(inputs["L0_attn_Wq"][r0:r0 + 128, :].T.reshape(NCH, 128, 128)),
        wk=np.ascontiguousarray(inputs["L0_attn_Wk"][r0:r0 + 128, :].T.reshape(NCH, 128, 128)),
        wv=np.ascontiguousarray(inputs["L0_attn_Wv"][r0:r0 + 128, :].T.reshape(NCH, 128, 128)),
        bqkv=np.concatenate([inputs["L0_attn_bq"][r0:r0 + 128],
                             inputs["L0_attn_bk"][r0:r0 + 128],
                             inputs["L0_attn_bv"][r0:r0 + 128]]).reshape(1, 384).astype(f),
        woT=np.ascontiguousarray(inputs["L0_attn_Wo"].T.reshape(NCH, 128, D)),
        bo=inputs["L0_attn_bo"].reshape(NCH, 128).astype(f),
        tb2=tb2.astype(f),
        etis=np.array([[inputs["L0_attn_et_gate"].reshape(()),
                        inputs["L0_attn_is_gate"].reshape(())]], dtype=f),
        eye=np.eye(128, dtype=f),
        fwin=np.ascontiguousarray(inputs["L0_ffn_Win"].reshape(NCH, 128, 64)),
        fbin=inputs["L0_ffn_bin_"].reshape(1, 64).astype(f),
        fw1=inputs["L0_ffn_W1"].astype(f), fb1=inputs["L0_ffn_b1"].astype(f),
        fw2=inputs["L0_ffn_W2"].astype(f),
        fb2=inputs["L0_ffn_b2"].reshape(1, 64).astype(f),
        fwout=inputs["L0_ffn_Wout"].astype(f),
        fbout=inputs["L0_ffn_bout"].reshape(NCH, 128).astype(f),
        crmsw=inputs["L1_attn_rms_weight"].reshape(NCH, 128).astype(f),
        cwA=np.ascontiguousarray(inputs["L1_attn_w_A"].reshape(NCH, 128, D)),
        cbA=inputs["L1_attn_b_A"].reshape(NCH, 128).astype(f),
        cwx=np.ascontiguousarray(inputs["L1_attn_w_x"].reshape(NCH, 128, D)),
        cwsg=np.ascontiguousarray(inputs["L1_attn_w_sidegate"].reshape(NCH, 128, D)),
        cwog=np.ascontiguousarray(inputs["L1_attn_w_output"].reshape(NCH, 128, D)),
        cwo=np.ascontiguousarray(inputs["L1_attn_w_o"].reshape(NCH, 128, D)),
        dwrT=np.ascontiguousarray(inputs["L1_ffn_W_r"].T.reshape(NCH, 128, D)),
        dwkT=np.ascontiguousarray(inputs["L1_ffn_W_k"].T.reshape(NCH, 128, D)),
        dwvT=np.ascontiguousarray(inputs["L1_ffn_W_v"].T.reshape(NCH, 128, D)),
        mur=inputs["L1_ffn_mu_r"].reshape(NCH, 128).astype(f),
        muk=inputs["L1_ffn_mu_k"].reshape(NCH, 128).astype(f),
    )
    return d


def kernel(**inputs):
    x = np.asarray(inputs["L0_attn_x"], dtype=np.float32)
    xT = np.ascontiguousarray(x.transpose(0, 2, 1).reshape(B, NCH, 128, T))
    tb = np.asarray(inputs["L0_attn_time_bias"], dtype=np.float32).reshape(-1)
    idx = np.clip(C0 + np.arange(128)[:, None] - np.arange(WB)[None, :], 0, 499)
    tb2 = np.ascontiguousarray(tb[idx])
    bld = Builder()
    nc = bld.build()
    in_maps = [_host_prepare(inputs, k, xT, tb2) for k in range(NCORES)]
    res = run_bass_kernel_spmd(nc, in_maps, core_ids=list(range(NCORES)))
    out = np.stack([np.asarray(res.results[k]["out"]) for k in range(NCORES)], axis=0)
    return out
